# version 01
# baseline (speedup 1.0000x reference)
import math
import numpy as np
from contextlib import ExitStack

import concourse.bass as bass
import concourse.mybir as mybir
from concourse.bass_utils import run_bass_kernel_spmd
from concourse.aot_env import allow_ici_all_to_all

F32 = mybir.dt.float32
BF16 = mybir.dt.bfloat16
AF = mybir.ActivationFunctionType
ALU = mybir.AluOpType
AX = mybir.AxisListType

NCORE = 8
T = 8192
D = 2048
TPC = T // NCORE
FC = D // 128
NQC = T // 512
NH = 16
HD = 128
SOFTCAP = 50.0
ISQHD = 1.0 / math.sqrt(HD)
LN_EPS = 1e-5
RMS_EPS = float(np.finfo(np.float32).eps)
LAMBDA_INIT = 0.2
NEG_BIG = -1e30

DEBUG_UPTO = None
DEBUG_DUMPS = ()
DEBUG_FFN_GU_ONLY = False
DEBUG_FFN_JJ = 32
DEBUG_FFN_NOSILU = False
DEBUG_FFN_STAGE = 9
DEBUG_OP_STAGE = 9


class SerialEngine:

    _PASS = ("wait_ge", "nop", "maybe_nop_then_inc")

    def __init__(self, ctx, eng, sem):
        self._ctx, self._eng, self._sem = ctx, eng, sem

    def __getattr__(self, name):
        fn = getattr(self._eng, name)
        if name in SerialEngine._PASS or name.startswith("_"):
            return fn

        def wrapped(*a, **k):
            self._ctx.w(self._eng, self._sem)
            instr = fn(*a, **k)
            self._ctx.inc(instr, self._sem, 1)
            return instr
        return wrapped


class Ctx:
    def __init__(self, nc, stack):
        self.nc = nc
        self.stack = stack
        self.sems = {}
        self.cnt = {}
        self.store_chans = set()
        for name in ("ls", "lw", "st", "cc", "pe", "se", "ve", "gp",
                     "b0", "b1", "b2", "b3", "b4"):
            self.sems[name] = stack.enter_context(nc.semaphore(name))
            self.cnt[name] = 0
        self.round = 0

    def chan(self, name, store=False):
        if name not in self.sems:
            self.sems[name] = self.stack.enter_context(self.nc.semaphore(name))
            self.cnt[name] = 0
        if store:
            self.store_chans.add(name)
        return name

    def wc(self, eng, name):
        if self.cnt.get(name, 0) > 0:
            eng.wait_ge(self.sems[name], self.cnt[name])

    def wall_stores(self, eng):
        self.wc(eng, "st")
        for c in sorted(self.store_chans):
            self.wc(eng, c)

    def inc(self, instr, name, k=1):
        instr.then_inc(self.sems[name], k)
        self.cnt[name] += k
        return self.cnt[name]

    def last(self, instr, name, k=1):
        return self.cnt[name]

    def w(self, eng, name, v=None):
        val = self.cnt[name] if v is None else v
        if val > 0:
            eng.wait_ge(self.sems[name], val)

    def barrier(self):
        nc = self.nc
        self.round += 1
        r = self.round
        for name in list(self.sems):
            if name.startswith("b"):
                continue
            if name in ("pe", "se", "ve", "gp"):
                continue
            self.wc(nc.sync, name)
        engs = [(nc.sync, "b0", None), (nc.gpsimd, "b1", "gp"),
                (nc.tensor, "b2", "pe"), (nc.scalar, "b3", "se"),
                (nc.vector, "b4", "ve")]
        for eng, s, own in engs:
            if own is not None:
                self.wc(eng, own)
            self.inc(eng.nop(), s, 1)
        for eng, s, _ in engs:
            for _, s2, _ in engs:
                if s2 != s:
                    eng.wait_ge(self.sems[s2], r)


def _split4(n, k=4):
    k = min(k, n)
    base, rem = n // k, n % k
    out, a = [], 0
    for i in range(k):
        b = a + base + (1 if i < rem else 0)
        out.append((a, b))
        a = b
    return out


def build(debug_upto=None, debug_dumps=()):
    nc = bass.Bass("TRN2", target_bir_lowering=False)

    xT_in = nc.dram_tensor("xT", [D, TPC], F32, kind="ExternalInput").ap()
    wa_in = nc.dram_tensor("wa", [D, 768], F32, kind="ExternalInput").ap()
    wg_in = nc.dram_tensor("wg", [D, 16], F32, kind="ExternalInput").ap()
    wo_in = nc.dram_tensor("wo", [16, 128, 16, 128], F32, kind="ExternalInput").ap()
    wb_in = nc.dram_tensor("wb", [D, 768], F32, kind="ExternalInput").ap()
    w1_in = nc.dram_tensor("w1", [128, 128, 16, 128], F32, kind="ExternalInput").ap()
    w2_in = nc.dram_tensor("w2", [16, 2, 128, 32, 128], F32, kind="ExternalInput").ap()
    cos_in = nc.dram_tensor("cos2", [128, T], F32, kind="ExternalInput").ap()
    sin_in = nc.dram_tensor("sin2", [128, T], F32, kind="ExternalInput").ap()
    logp_in = nc.dram_tensor("logpos", [1, T], F32, kind="ExternalInput").ap()
    scal_in = nc.dram_tensor("scal", [1, 2], F32, kind="ExternalInput").ap()
    lam_in = nc.dram_tensor("lam", [1, 256], F32, kind="ExternalInput").ap()
    idf_in = nc.dram_tensor("identf", [128, 128], F32, kind="ExternalInput").ap()
    pi_in = nc.dram_tensor("pimat", [128, 128], F32, kind="ExternalInput").ap()
    wmqk_in = nc.dram_tensor("wmqk", [128, 896], F32, kind="ExternalInput").ap()
    wmkq_in = nc.dram_tensor("wmkq", [128, 896], F32, kind="ExternalInput").ap()
    out_ext = nc.dram_tensor("out", [TPC, D], F32, kind="ExternalOutput").ap()

    dbg = {}
    for name in debug_dumps:
        shp = {"x": [D, TPC], "xln": [D, TPC], "qa": [256, T], "ka": [256, T],
               "va": [2 * T, 128], "qb": [256, T], "kb": [256, T],
               "vb": [T, 256], "g": [16, TPC], "oa": [D, TPC],
               "ob": [D, TPC], "aain": [D, TPC], "ai_k": [128, 512], "ai_q": [128, 512],
               "ai_pt": [128, 512], "ai_rs": [1, 512], "ai_o": [128, 512],
               "ai_f": [128, 512]}[name]
        dbg[name] = nc.dram_tensor("dbg_" + name, shp, F32,
                                   kind="ExternalOutput").ap()

    ag_in = nc.dram_tensor("ag_in", [D, TPC], BF16)
    ag_out = nc.dram_tensor("ag_out", [NCORE * D, TPC], BF16)
    g_in = nc.dram_tensor("g_in", [16, TPC], F32)
    g_out = nc.dram_tensor("g_out", [16, TPC], F32)
    a2aa_in = nc.dram_tensor("a2aa_in", [D, TPC], BF16)
    a2aa_out = nc.dram_tensor("a2aa_out", [D, TPC], BF16)
    a2ab_in = nc.dram_tensor("a2ab_in", [D, TPC], F32)
    a2ab_out = nc.dram_tensor("a2ab_out", [D, TPC], F32)
    qa_s = nc.dram_tensor("qa_s", [256, T], BF16)
    ka_s = nc.dram_tensor("ka_s", [256, T], BF16)
    va_s = nc.dram_tensor("va_s", [2, T, 128], BF16)
    qb_s = nc.dram_tensor("qb_s", [256, T], BF16)
    kb_s = nc.dram_tensor("kb_s", [256, T], BF16)
    vb_s = nc.dram_tensor("vb_s", [T, 256], BF16)
    mb_d = nc.dram_tensor("mb_d", [2, NQC, 512], F32)

    RG = [list(range(NCORE))]

    with ExitStack() as st_:
        E = st_.enter_context
        ctx = Ctx(nc, st_)
        SY, GP, TE = nc.sync, nc.gpsimd, nc.tensor
        SC = SerialEngine(ctx, nc.scalar, "se")
        VE = SerialEngine(ctx, nc.vector, "ve")

        x_sb = E(nc.sbuf_tensor("x_sb", [128, FC * TPC], F32))
        xln_sb = E(nc.sbuf_tensor("xln_sb", [128, FC * TPC], BF16))
        idf_sb = E(nc.sbuf_tensor("idf_sb", [128, 128], F32))
        idb_sb = E(nc.sbuf_tensor("idb_sb", [128, 128], BF16))
        pi_sb = E(nc.sbuf_tensor("pi_sb", [128, 128], BF16))
        wmqk_sb = E(nc.sbuf_tensor("wmqk_sb", [128, 896], F32))
        wmkq_sb = E(nc.sbuf_tensor("wmkq_sb", [128, 896], F32))
        onesb = E(nc.sbuf_tensor("onesb", [128, 1], BF16))
        onesf = E(nc.sbuf_tensor("onesf", [1, 128], F32))
        onesb2 = E(nc.sbuf_tensor("onesb2", [1, 128], BF16))
        neg20 = E(nc.sbuf_tensor("neg20", [128, 1], F32))
        epsln = E(nc.sbuf_tensor("epsln", [1, 1], F32))
        epsrms = E(nc.sbuf_tensor("epsrms", [1, 1], F32))
        scal_sb = E(nc.sbuf_tensor("scal_sb", [1, 2], F32))
        lam_sb = E(nc.sbuf_tensor("lam_sb", [1, 256], F32))
        lamn_sb = E(nc.sbuf_tensor("lamn_sb", [128, 1], F32))
        wg_sb = E(nc.sbuf_tensor("wg_sb", [128, FC * 16], BF16))
        block = E(nc.Block())


        def phase0():
            for a, b in _split4(FC):
                ap = bass.AP(xT_in.tensor, a * 128 * TPC,
                             [[TPC, 128], [128 * TPC, b - a], [1, TPC]])
                ctx.inc(SY.dma_start(
                    out=x_sb[:, a * TPC:b * TPC].rearrange("p (c t) -> p c t",
                                                           c=b - a),
                    in_=ap), "ls", 16)
            ctx.inc(SY.dma_start(out=idf_sb[:], in_=idf_in), "ls", 16)
            ctx.inc(SY.dma_start(out=wmqk_sb[:], in_=wmqk_in), "ls", 16)
            ctx.inc(SY.dma_start(out=wmkq_sb[:], in_=wmkq_in), "ls", 16)
            ctx.inc(SY.dma_start(out=scal_sb[:], in_=scal_in), "ls", 16)
            ctx.inc(SY.dma_start(out=lam_sb[:], in_=lam_in), "ls", 16)
            ctx.inc(GP.dma_start(out=idb_sb[:], in_=idf_in), "lw", 16)
            ctx.inc(GP.dma_start(out=pi_sb[:], in_=pi_in), "lw", 16)
            ap = bass.AP(wg_in.tensor, 0, [[16, 128], [128 * 16, FC], [1, 16]])
            ctx.inc(GP.dma_start(
                out=wg_sb[:].rearrange("p (c m) -> p c m", c=FC), in_=ap),
                "lw", 16)
            GP.memset(onesb[:], 1.0)
            GP.memset(onesf[:], 1.0)
            GP.memset(onesb2[:], 1.0)
            GP.memset(neg20[:], -20.0)
            GP.memset(epsln[:], LN_EPS)
            ctx.inc(GP.memset(epsrms[:], RMS_EPS), "gp", 1)
            ctx.w(VE, "ls")
            ctx.w(VE, "gp")
            with nc.sbuf_tensor("lrow", [1, 64], F32) as lrow, \
                 nc.sbuf_tensor("lsc", [1, 4], F32) as lsc, \
                 nc.psum_tensor("ps_l", [128, 1], F32) as ps_l:
                VE.tensor_mul(lrow[:], lam_sb[:, 0:64], lam_sb[:, 64:128])
                VE.reduce_sum(lsc[:, 0:1], lrow[:], AX.X)
                VE.tensor_mul(lrow[:], lam_sb[:, 128:192], lam_sb[:, 192:256])
                ctx.last(VE.reduce_sum(lsc[:, 1:2], lrow[:], AX.X), "ve", 1)
                ctx.w(SC, "ve")
                SC.activation(lsc[:, 2:3], lsc[:, 0:1], AF.Exp)
                ctx.last(SC.activation(lsc[:, 3:4], lsc[:, 1:2], AF.Exp), "se", 1)
                ctx.w(VE, "se")
                VE.tensor_sub(lsc[:, 0:1], lsc[:, 3:4], lsc[:, 2:3])
                ctx.last(VE.tensor_scalar_add(lsc[:, 1:2], lsc[:, 0:1],
                                             -LAMBDA_INIT), "ve", 1)
                ctx.w(TE, "ve")
                ctx.inc(TE.matmul(ps_l[:], onesf[:], lsc[:, 1:2],
                                  start=True, stop=True), "pe", 1)
                ctx.w(SC, "pe")
                ctx.last(SC.copy(lamn_sb[:], ps_l[:]), "se", 1)

        def phase_ln(tag, to_ag=False, gates=False):
            with ExitStack() as ps_:
                e = ps_.enter_context
                xb = e(nc.sbuf_tensor(f"xb{tag}", [128, 2 * TPC], BF16))
                sq = e(nc.sbuf_tensor(f"sq{tag}", [128, 2 * TPC], BF16))
                rows = e(nc.sbuf_tensor(f"ro{tag}", [1, 5 * TPC], F32))
                tsc = e(nc.sbuf_tensor(f"tc{tag}", [128, 2 * TPC], F32))
                m_row = rows[:, 0:TPC]
                msq = rows[:, TPC:2 * TPC]
                varr = rows[:, 2 * TPC:3 * TPC]
                lnt = rows[:, 3 * TPC:4 * TPC]
                rstd = rows[:, 4 * TPC:5 * TPC]
                with nc.psum_tensor(f"p1{tag}", [1, TPC], F32) as ps1, \
                     nc.psum_tensor(f"p2{tag}", [1, TPC], F32) as ps2:
                    pe_done = [0] * FC
                    for fc in range(FC):
                        par = fc % 2
                        if fc >= 2:
                            ctx.w(SC, "pe", pe_done[fc - 2])
                        xs = x_sb[:, fc * TPC:(fc + 1) * TPC]
                        cpd = ctx.last(SC.copy(
                            xb[:, par * TPC:(par + 1) * TPC], xs), "se", 1)
                        ctx.w(SC, "se", cpd)
                        ctx.last(SC.activation(
                            sq[:, par * TPC:(par + 1) * TPC],
                            xb[:, par * TPC:(par + 1) * TPC], AF.Square),
                            "se", 1)
                        ctx.w(TE, "se")
                        last = None
                        for h in range(2):
                            TE.matmul(ps1[:, h * 512:(h + 1) * 512], onesb[:],
                                      xb[:, par * TPC + h * 512:
                                         par * TPC + (h + 1) * 512],
                                      start=(fc == 0), stop=(fc == FC - 1))
                            last = TE.matmul(
                                ps2[:, h * 512:(h + 1) * 512], onesb[:],
                                sq[:, par * TPC + h * 512:
                                   par * TPC + (h + 1) * 512],
                                start=(fc == 0), stop=(fc == FC - 1))
                        pe_done[fc] = ctx.inc(last, "pe", 1)
                    ctx.w(VE, "pe")
                    VE.tensor_scalar_mul(m_row, ps1[:], 1.0 / D)
                    VE.tensor_mul(msq, m_row, m_row)
                    ctx.last(VE.scalar_tensor_tensor(
                        varr, ps2[:], 1.0 / D, msq, ALU.mult, ALU.subtract),
                        "ve", 1)
                    ctx.w(SC, "ve")
                    lnd = ctx.last(SC.activation(lnt, varr, AF.Ln,
                                                bias=epsln[:]), "se", 1)
                    ctx.w(SC, "se", lnd)
                    ctx.last(SC.activation(rstd, lnt, AF.Exp, scale=-0.5),
                            "se", 1)
                with nc.psum_tensor(f"mb{tag}", [128, TPC], F32) as ps_mb, \
                     nc.psum_tensor(f"rb{tag}", [128, TPC], F32) as ps_rb:
                    ctx.w(TE, "se")
                    last = None
                    for h in range(2):
                        TE.matmul(ps_mb[:, h * 512:(h + 1) * 512], onesf[:],
                                  m_row[:, h * 512:(h + 1) * 512],
                                  start=True, stop=True)
                        last = TE.matmul(ps_rb[:, h * 512:(h + 1) * 512],
                                         onesf[:],
                                         rstd[:, h * 512:(h + 1) * 512],
                                         start=True, stop=True)
                    ctx.inc(last, "pe", 1)
                    ctx.w(VE, "pe")
                    for fc in range(FC):
                        par = fc % 2
                        xs = x_sb[:, fc * TPC:(fc + 1) * TPC]
                        VE.tensor_sub(tsc[:, par * TPC:(par + 1) * TPC], xs,
                                      ps_mb[:])
                        done = ctx.last(VE.tensor_mul(
                            xln_sb[:, fc * TPC:(fc + 1) * TPC],
                            tsc[:, par * TPC:(par + 1) * TPC], ps_rb[:]),
                            "ve", 1)
                        if to_ag:
                            ctx.w(SY, "ve", done)
                            ctx.inc(SY.dma_start(
                                out=ag_in[fc * 128:(fc + 1) * 128, :],
                                in_=xln_sb[:, fc * TPC:(fc + 1) * TPC]),
                                "ls", 16)
                if to_ag:
                    ctx.w(GP, "ls")
                    ctx.inc(GP.collective_compute(
                        "AllGather", ALU.bypass, replica_groups=RG,
                        ins=[ag_in.ap().opt()], outs=[ag_out.ap().opt()]),
                        "cc", 1)
                if gates:
                    with nc.psum_tensor(f"pg{tag}", [16, TPC], F32) as ps_g, \
                         nc.sbuf_tensor(f"gt{tag}", [16, 2 * TPC], F32) as gt:
                        ctx.w(TE, "ve")
                        ctx.w(TE, "lw")
                        last = None
                        for fc in range(FC):
                            for h in range(2):
                                last = TE.matmul(
                                    ps_g[:, h * 512:(h + 1) * 512],
                                    wg_sb[:, fc * 16:(fc + 1) * 16],
                                    xln_sb[:, fc * TPC + h * 512:
                                           fc * TPC + (h + 1) * 512],
                                    start=(fc == 0), stop=(fc == FC - 1))
                        ctx.inc(last, "pe", 1)
                        ctx.w(SC, "pe")
                        ctx.last(SC.activation(gt[:, 0:TPC], ps_g[:], AF.Exp,
                                              scale=-1.0), "se", 1)
                        ctx.w(VE, "se")
                        VE.tensor_scalar_add(gt[:, TPC:2 * TPC], gt[:, 0:TPC],
                                             1.0)
                        ctx.last(VE.reciprocal(gt[:, 0:TPC],
                                              gt[:, TPC:2 * TPC]), "ve", 1)
                        ctx.w(SY, "ve")
                        ctx.inc(SY.dma_start(out=g_in[:, :], in_=gt[:, 0:TPC]),
                                "st", 16)
                        ctx.w(GP, "st")
                        with allow_ici_all_to_all():
                            ctx.inc(GP.collective_compute(
                                "AllToAll", ALU.bypass, replica_groups=RG,
                                ins=[g_in.ap().opt()],
                                outs=[g_out.ap().opt()]), "cc", 1)

        def phase_qkv(tag, w_dram, attn_b=False):
            q_d, k_d = (qb_s, kb_s) if attn_b else (qa_s, ka_s)
            with ExitStack() as ps_:
                e = ps_.enter_context
                wsb = e(nc.sbuf_tensor(f"w{tag}", [128, FC * 768], BF16))
                xg = e(nc.sbuf_tensor(f"xg{tag}", [128, 2 * 8192], BF16))
                if not attn_b:
                    stag = e(nc.sbuf_tensor(f"st{tag}", [128, 4 * 512], BF16))
                vst = e(nc.sbuf_tensor(f"vs{tag}", [128, 2 * 512], BF16))
                vst2 = e(nc.sbuf_tensor(f"v2{tag}", [128, 2 * 512], BF16))
                ps_p = [e(nc.psum_tensor(f"pp{i}{tag}", [128, 512], F32))
                        for i in range(3)]
                ps_vt = e(nc.psum_tensor(f"pv{tag}", [128, 512], BF16))
                if attn_b:
                    sqb = e(nc.sbuf_tensor(f"sb{tag}", [128, 2 * 512], BF16))
                    qf = e(nc.sbuf_tensor(f"qf{tag}", [128, 2 * 512], F32))
                    qsbt = e(nc.sbuf_tensor(f"qs{tag}", [128, 2 * 512], BF16))
                    tco = e(nc.sbuf_tensor(f"tco{tag}", [128, 2 * 512], F32))
                    usi = e(nc.sbuf_tensor(f"us{tag}", [128, 2 * 512], F32))
                    qstg = e(nc.sbuf_tensor(f"qg{tag}", [128, 2 * 512], BF16))
                    cs_sl = e(nc.sbuf_tensor(f"cs{tag}", [128, 2 * 512], F32))
                    sn_sl = e(nc.sbuf_tensor(f"sn{tag}", [128, 2 * 512], F32))
                    lp_sl = e(nc.sbuf_tensor(f"lp{tag}", [1, 2 * 512], F32))
                    rws = e(nc.sbuf_tensor(f"rw{tag}", [1, 2 * 1024], F32))
                    ps_ss = [e(nc.psum_tensor(f"ss{i}{tag}", [1, 512], F32))
                             for i in range(2)]
                    ps_f = e(nc.psum_tensor(f"pf{tag}", [128, 512], F32))
                    ps_rot = e(nc.psum_tensor(f"pr{tag}", [128, 512], F32))
                for a, b in _split4(FC):
                    ap = bass.AP(w_dram.tensor, a * 128 * 768,
                                 [[768, 128], [128 * 768, b - a], [1, 768]])
                    ctx.inc(GP.dma_start(
                        out=wsb[:, a * 768:b * 768].rearrange(
                            "p (c u) -> p c u", c=b - a), in_=ap), "lw", 16)
                wld = ctx.cnt["lw"]
                ctx.w(SY, "cc")
                xg_ld = [0] * 16
                xg_cons = [0, 0]
                csl_ld = [0, 0]
                csl_cons = [0, 0]
                bank_free = [0, 0, 0]
                stag_free = [0] * 4
                vst_free = [0, 0]
                psvt_free = 0
                vst2_free = [0, 0]
                qstg_free = [0, 0]
                ss_free = [0, 0]
                f_free = 0
                rot_free = 0
                frow_free = [0, 0]
                sqb_free = [0, 0]
                qf_free = [0, 0]
                qsbt_free = [0, 0]

                def load_xg(rh):
                    par = rh % 2
                    r, half = rh // 2, rh % 2
                    ch = ctx.chan(f"cxg{par}")
                    ctx.w(SY, "pe", xg_cons[par])
                    for a, b in _split4(FC):
                        ap = bass.AP(ag_out,
                                     (2048 * r + a * 128) * TPC + 512 * half,
                                     [[TPC, 128], [128 * TPC, b - a], [1, 512]])
                        ctx.inc(SY.dma_start(
                            out=xg[:, par * 8192 + a * 512:par * 8192 + b * 512]
                            .rearrange("p (c u) -> p c u", c=b - a), in_=ap),
                            ch, 16)
                    if attn_b:
                        ch2 = ctx.chan(f"ccs{par}")
                        ctx.w(SY, "ve", csl_cons[par])
                        c0 = TPC * r + 512 * half
                        ctx.inc(SY.dma_start(
                            out=cs_sl[:, par * 512:(par + 1) * 512],
                            in_=bass.AP(cos_in.tensor, c0,
                                        [[T, 128], [1, 512]])), ch2, 16)
                        ctx.inc(SY.dma_start(
                            out=sn_sl[:, par * 512:(par + 1) * 512],
                            in_=bass.AP(sin_in.tensor, c0,
                                        [[T, 128], [1, 512]])), ch2, 16)
                        ctx.inc(SY.dma_start(
                            out=lp_sl[:, par * 512:(par + 1) * 512],
                            in_=bass.AP(logp_in.tensor, c0,
                                        [[T, 1], [1, 512]])), ch2, 16)

                pending = []

                def pp_v(rh, m, bk, mm_done):
                    def go():
                        vp = m - 4
                        r, half = rh // 2, rh % 2
                        ctx.w(SC, "pe", mm_done)
                        ctx.w(SC, "pe", vst_free[vp])
                        ev = ctx.last(SC.copy(vst[:, vp * 512:(vp + 1) * 512],
                                             ps_p[bk][:]), "se", 1)
                        bank_free[bk] = ev
                        ctx.w(TE, "se", ev)
                        ctx.w(TE, "se", psvt_free)
                        last = None
                        for u in range(4):
                            last = TE.transpose(
                                ps_vt[:, u * 128:(u + 1) * 128],
                                vst[:, vp * 512 + u * 128:
                                    vp * 512 + (u + 1) * 128], idb_sb[:])
                        tr = ctx.inc(last, "pe", 1)
                        vst_free[vp] = tr
                        ctx.w(SC, "pe", tr)
                        ctx.wc(SC, ctx.chan(f"cv2{vp}", store=True))
                        nonlocal_psvt = ctx.last(SC.copy(
                            vst2[:, vp * 512:(vp + 1) * 512], ps_vt[:]),
                            "se", 1)
                        _set_psvt(nonlocal_psvt)
                        ctx.w(SY, "se", nonlocal_psvt)
                        t0 = TPC * r + 512 * half
                        if attn_b:
                            ap = bass.AP(vb_s, t0 * 256 + 128 * vp,
                                         [[256, 128], [128 * 256, 4], [1, 128]])
                        else:
                            ap = bass.AP(va_s, vp * T * 128 + t0 * 128,
                                         [[128, 128], [128 * 128, 4], [1, 128]])
                        ctx.inc(SY.dma_start(
                            out=ap,
                            in_=vst2[:, vp * 512:(vp + 1) * 512].rearrange(
                                "p (u d) -> p u d", u=4)), f"cv2{vp}", 16)
                    return go

                def _set_psvt(v):
                    nonlocal psvt_free
                    psvt_free = v

                def pp_qk_a(rh, m, bk, mm_done):
                    def go():
                        r, half = rh // 2, rh % 2
                        sl = m % 4
                        ctx.w(SC, "pe", mm_done)
                        ctx.wc(SC, ctx.chan(f"cst{sl}", store=True))
                        if m < 2:
                            ev = ctx.last(SC.activation(
                                stag[:, sl * 512:(sl + 1) * 512], ps_p[bk][:],
                                AF.Copy, scale=ISQHD), "se", 1)
                        else:
                            ev = ctx.last(SC.copy(
                                stag[:, sl * 512:(sl + 1) * 512],
                                ps_p[bk][:]), "se", 1)
                        bank_free[bk] = ev
                        ctx.w(SY, "se", ev)
                        dst = q_d if m < 2 else k_d
                        ctx.inc(SY.dma_start(
                            out=bass.AP(dst, (m % 2) * 128 * T + TPC * r + 512 * half,
                                        [[T, 128], [1, 512]]),
                            in_=stag[:, sl * 512:(sl + 1) * 512]),
                            f"cst{sl}", 16)
                    return go

                def pp_qk_b(rh, m, bk, mm_done):
                    def go():
                        nonlocal f_free, rot_free
                        r, half = rh // 2, rh % 2
                        par = rh % 2
                        rot = m % 2
                        ctx.w(SC, "pe", mm_done)
                        ctx.w(SC, "pe", sqb_free[rot])
                        sqd = ctx.last(SC.activation(
                            sqb[:, rot * 512:(rot + 1) * 512], ps_p[bk][:],
                            AF.Square), "se", 1)
                        ctx.w(TE, "se", sqd)
                        ctx.w(TE, "se", ss_free[rot])
                        ssd = ctx.inc(TE.matmul(ps_ss[rot][:], onesb[:],
                                                sqb[:, rot * 512:(rot + 1) * 512],
                                                start=True, stop=True), "pe", 1)
                        sqb_free[rot] = ssd
                        r0 = (m % 2) * 1024
                        ctx.w(SC, "pe", ssd)
                        ctx.w(SC, "pe", frow_free[m % 2])
                        lnd = ctx.last(SC.activation(
                            rws[:, r0:r0 + 512], ps_ss[rot][:], AF.Ln,
                            scale=1.0 / HD, bias=epsrms[:]), "se", 1)
                        ctx.w(SC, "se", lnd)
                        rsd = ctx.last(SC.activation(
                            rws[:, r0 + 512:r0 + 1024], rws[:, r0:r0 + 512],
                            AF.Exp, scale=-0.5), "se", 1)
                        ss_free[rot] = rsd
                        if m < 2:
                            ctx.w(VE, "se", rsd)
                            ctx.wc(VE, f"ccs{par}")
                            VE.tensor_scalar(rws[:, r0:r0 + 512],
                                             lp_sl[:, par * 512:(par + 1) * 512],
                                             scal_sb[:, m:m + 1], ISQHD,
                                             ALU.mult, ALU.mult)
                            frd = ctx.last(VE.tensor_mul(
                                rws[:, r0 + 512:r0 + 1024],
                                rws[:, r0:r0 + 512],
                                rws[:, r0 + 512:r0 + 1024]), "ve", 1)
                            ctx.w(TE, "ve", frd)
                        else:
                            ctx.w(TE, "se", rsd)
                        frow = rws[:, r0 + 512:r0 + 1024]
                        ctx.w(TE, "ve", f_free)
                        fd = ctx.inc(TE.matmul(ps_f[:], onesf[:], frow,
                                               start=True, stop=True), "pe", 1)
                        frow_free[m % 2] = fd
                        ctx.w(SC, "ve", qf_free[rot])
                        ev = ctx.last(SC.copy(qf[:, rot * 512:(rot + 1) * 512],
                                             ps_p[bk][:]), "se", 1)
                        bank_free[bk] = ev
                        ctx.w(VE, "se", ev)
                        ctx.w(VE, "pe", fd)
                        ctx.w(VE, "pe", qsbt_free[rot])
                        qsd = ctx.last(VE.tensor_mul(
                            qsbt[:, rot * 512:(rot + 1) * 512],
                            qf[:, rot * 512:(rot + 1) * 512], ps_f[:]),
                            "ve", 1)
                        qf_free[rot] = qsd
                        f_free = qsd
                        ctx.w(TE, "ve", qsd)
                        ctx.w(TE, "ve", rot_free)
                        rtd = ctx.inc(TE.matmul(
                            ps_rot[:], pi_sb[:],
                            qsbt[:, rot * 512:(rot + 1) * 512],
                            start=True, stop=True), "pe", 1)
                        ctx.wc(VE, f"ccs{par}")
                        VE.tensor_mul(tco[:, rot * 512:(rot + 1) * 512],
                                      qsbt[:, rot * 512:(rot + 1) * 512],
                                      cs_sl[:, par * 512:(par + 1) * 512])
                        ctx.w(VE, "pe", rtd)
                        VE.tensor_mul(usi[:, rot * 512:(rot + 1) * 512],
                                      ps_rot[:],
                                      sn_sl[:, par * 512:(par + 1) * 512])
                        ctx.wc(VE, ctx.chan(f"cqg{rot}", store=True))
                        qgd = ctx.last(VE.tensor_add(
                            qstg[:, rot * 512:(rot + 1) * 512],
                            tco[:, rot * 512:(rot + 1) * 512],
                            usi[:, rot * 512:(rot + 1) * 512]), "ve", 1)
                        rot_free = qgd
                        qsbt_free[rot] = rtd
                        csl_cons[par] = qgd
                        ctx.w(SY, "ve", qgd)
                        dst = q_d if m < 2 else k_d
                        ctx.inc(SY.dma_start(
                            out=bass.AP(dst, (m % 2) * 128 * T + TPC * r + 512 * half,
                                        [[T, 128], [1, 512]]),
                            in_=qstg[:, rot * 512:(rot + 1) * 512]),
                            f"cqg{rot}", 16)
                    return go

                load_xg(0)
                for rh in range(16):
                    par = rh % 2
                    if rh + 1 < 16:
                        load_xg(rh + 1)
                    first = True
                    for m in range(6):
                        bk = (rh * 6 + m) % 3
                        if first:
                            ctx.w(TE, "lw", wld)
                            ctx.wc(TE, f"cxg{par}")
                            first = False
                        ctx.w(TE, "se", bank_free[bk])
                        last = None
                        for fc in range(FC):
                            last = TE.matmul(
                                ps_p[bk][:],
                                wsb[:, fc * 768 + m * 128:fc * 768 + (m + 1) * 128],
                                xg[:, par * 8192 + fc * 512:
                                   par * 8192 + (fc + 1) * 512],
                                start=(fc == 0), stop=(fc == FC - 1))
                        mm = ctx.inc(last, "pe", 1)
                        if m == 5:
                            xg_cons[par] = mm
                        if m >= 4:
                            pending.append(pp_v(rh, m, bk, mm))
                        elif attn_b:
                            pending.append(pp_qk_b(rh, m, bk, mm))
                        else:
                            pending.append(pp_qk_a(rh, m, bk, mm))
                        while len(pending) > 2:
                            pending.pop(0)()
                    while len(pending) > 2:
                        pending.pop(0)()
                while pending:
                    pending.pop(0)()

        def phase_attn_a():
            with ExitStack() as ps_:
                e = ps_.enter_context
                k_sb = e(nc.sbuf_tensor("ak", [128, T], BF16))
                v_sb = e(nc.sbuf_tensor("av", [128, T], BF16))
                q_sb = e(nc.sbuf_tensor("aq", [128, 2 * 512], BF16))
                tt_sb = e(nc.sbuf_tensor("att", [128, 2 * 512], F32))
                pt_sb = e(nc.sbuf_tensor("apt", [128, 2 * 512], BF16))
                f_sb = e(nc.sbuf_tensor("af", [128, 2 * 512], F32))
                gsl = e(nc.sbuf_tensor("ag_", [1, 2 * 512], F32))
                rcp = e(nc.sbuf_tensor("arc", [1, 512], F32))
                frow = e(nc.sbuf_tensor("afr", [1, 512], F32))
                ostg = e(nc.sbuf_tensor("ao", [128, 2 * 512], BF16))
                ps_st = [e(nc.psum_tensor(f"as{i}", [128, 512], F32))
                         for i in range(2)]
                ps_rs = e(nc.psum_tensor("ars", [1, 512], F32))
                ps_o = e(nc.psum_tensor("apo", [128, 512], F32))
                ps_fb = e(nc.psum_tensor("afb", [128, 512], F32))
                ctx.wall_stores(SY)
                ctx.w(SY, "cc")
                kv_cons = 0
                ost_free = [0, 0]
                for hs in range(2):
                    if hs == 1:
                        ctx.w(SY, "pe", kv_cons)
                    ckv = ctx.chan("ckv")
                    for a, b in ((0, T // 2), (T // 2, T)):
                        ctx.inc(SY.dma_start(
                            out=k_sb[:, a:b],
                            in_=bass.AP(ka_s, hs * 128 * T + a,
                                        [[T, 128], [1, b - a]])), ckv, 16)
                    for a, b in ((0, 32), (32, 64)):
                        ap = bass.AP(va_s, hs * T * 128 + a * 128 * 128,
                                     [[128, 128], [128 * 128, b - a], [1, 128]])
                        ctx.inc(SY.dma_start(
                            out=v_sb[:, a * 128:b * 128].rearrange(
                                "p (u d) -> p u d", u=b - a), in_=ap), ckv, 16)
                    q_free = [0, 0]
                    g_free = [0, 0]
                    exp_done = [0, 0]
                    cons_done = [0, 0]
                    rcp_done = 0
                    om_done = 0
                    fc_done = 0
                    fb_prev = [0]
                    qld = [0] * NQC

                    def load_q(Q):
                        qp = Q % 2
                        ch = ctx.chan(f"caq{qp}")
                        ctx.w(SY, "pe", q_free[qp])
                        ctx.inc(SY.dma_start(
                            out=q_sb[:, qp * 512:(qp + 1) * 512],
                            in_=bass.AP(qa_s, hs * 128 * T + 512 * Q,
                                        [[T, 128], [1, 512]])), ch, 16)
                        ctx.w(SY, "ve", g_free[qp])
                        ctx.inc(SY.dma_start(
                            out=gsl[:, qp * 512:(qp + 1) * 512],
                            in_=bass.AP(g_out,
                                        (2 * (Q // 2) + hs) * TPC + 512 * (Q % 2),
                                        [[TPC, 1], [1, 512]])), ch, 16)

                    load_q(0)
                    for Q in range(NQC):
                        qp = Q % 2
                        if Q + 1 < NQC:
                            load_q(Q + 1)
                        ctx.wc(TE, "ckv")
                        ctx.wc(TE, f"caq{qp}")
                        ctx.w(TE, "ve", rcp_done)
                        ctx.w(TE, "ve", om_done)
                        nsub = 4 * (Q + 1)
                        mm_of = [0] * nsub
                        maskv_of = [0] * nsub

                        def emit_S(u):
                            j, s4 = u // 4, u % 4
                            slot = u % 2
                            ctx.w(TE, "se", exp_done[slot])
                            mm_of[u] = ctx.inc(TE.matmul(
                                ps_st[slot][:],
                                k_sb[:, 512 * j + 128 * s4:
                                     512 * j + 128 * (s4 + 1)],
                                q_sb[:, qp * 512:(qp + 1) * 512],
                                start=True, stop=True), "pe", 1)
                            if j == Q:
                                ctx.w(VE, "pe", mm_of[u])
                                maskv_of[u] = ctx.last(VE.tensor_add(
                                    ps_st[slot][:], ps_st[slot][:],
                                    wmkq_sb[:, 384 - 128 * s4:896 - 128 * s4]),
                                    "ve", 1)

                        def emit_tail(u):
                            j, s4 = u // 4, u % 4
                            slot = u % 2
                            if j == Q:
                                ctx.w(SC, "ve", maskv_of[u])
                            else:
                                ctx.w(SC, "pe", mm_of[u])
                            ctx.w(SC, "pe", cons_done[slot])
                            thd = ctx.last(SC.activation(
                                tt_sb[:, slot * 512:(slot + 1) * 512],
                                ps_st[slot][:], AF.Tanh,
                                scale=1.0 / SOFTCAP), "se", 1)
                            ctx.w(SC, "se", thd)
                            exp_done[slot] = ctx.last(SC.activation(
                                pt_sb[:, slot * 512:(slot + 1) * 512],
                                tt_sb[:, slot * 512:(slot + 1) * 512],
                                AF.Exp, scale=SOFTCAP, bias=neg20[:]), "se", 1)
                            ctx.w(TE, "se", exp_done[slot])
                            TE.matmul(ps_rs[:], onesb[:],
                                      pt_sb[:, slot * 512:(slot + 1) * 512],
                                      start=(u == 0), stop=(u == nsub - 1))
                            cons_done[slot] = ctx.inc(TE.matmul(
                                ps_o[:], v_sb[:, u * 128:(u + 1) * 128],
                                pt_sb[:, slot * 512:(slot + 1) * 512],
                                start=(u == 0), stop=(u == nsub - 1)), "pe", 1)

                        emit_S(0)
                        for u in range(nsub):
                            if u + 1 < nsub:
                                emit_S(u + 1)
                            emit_tail(u)
                        q_free[qp] = cons_done[(nsub - 1) % 2]
                        if hs == 0 and Q == NQC - 1:
                            kv_cons = q_free[qp]
                        ctx.w(VE, "pe", cons_done[(nsub - 1) % 2])
                        rcp_done = ctx.last(VE.reciprocal(rcp[:], ps_rs[:]),
                                           "ve", 1)
                        ctx.wc(VE, f"caq{qp}")
                        ctx.w(VE, "pe", fb_prev[0])
                        fr = ctx.last(VE.tensor_mul(
                            frow[:], rcp[:], gsl[:, qp * 512:(qp + 1) * 512]),
                            "ve", 1)
                        g_free[qp] = fr
                        ctx.w(TE, "ve", fr)
                        ctx.w(TE, "se", fc_done)
                        fb = ctx.inc(TE.matmul(ps_fb[:], onesf[:], frow[:],
                                               start=True, stop=True), "pe", 1)
                        fb_prev[0] = fb
                        ctx.w(SC, "pe", fb)
                        fc_done = ctx.last(SC.copy(
                            f_sb[:, qp * 512:(qp + 1) * 512], ps_fb[:]),
                            "se", 1)
                        ctx.w(VE, "se", fc_done)
                        ctx.wc(VE, ctx.chan(f"cao{qp}", store=True))
                        om_done = ctx.last(VE.tensor_mul(
                            ostg[:, qp * 512:(qp + 1) * 512], ps_o[:],
                            f_sb[:, qp * 512:(qp + 1) * 512]), "ve", 1)
                        ctx.w(SY, "ve", om_done)
                        ctx.inc(SY.dma_start(
                            out=bass.AP(
                                a2aa_in,
                                (256 * (Q // 2) + 128 * hs) * TPC + 512 * (Q % 2),
                                [[TPC, 128], [1, 512]]),
                            in_=ostg[:, qp * 512:(qp + 1) * 512]),
                            f"cao{qp}", 16)
                        if hs == 0 and Q == 0 and "ai_k" in dbg:
                            ctx.w(GP, "ve", om_done)
                            ctx.inc(GP.dma_start(out=dbg["ai_k"],
                                                 in_=k_sb[:, 0:512]), "lw", 16)
                            ctx.inc(GP.dma_start(out=dbg["ai_q"],
                                                 in_=q_sb[:, 0:512]), "lw", 16)
                            ctx.inc(GP.dma_start(out=dbg["ai_pt"],
                                                 in_=pt_sb[:, 512:1024]), "lw", 16)
                            ctx.inc(GP.dma_start(out=dbg["ai_rs"],
                                                 in_=rcp[:]), "lw", 16)
                            ctx.inc(GP.dma_start(out=dbg["ai_o"],
                                                 in_=ostg[:, 0:512]), "lw", 16)
                            ctx.inc(GP.dma_start(out=dbg["ai_f"],
                                                 in_=f_sb[:, 0:512]), "lw", 16)
                            ctx.w(GP, "lw")
                ctx.wall_stores(GP)
                with allow_ici_all_to_all():
                    ctx.inc(GP.collective_compute(
                        "AllToAll", ALU.bypass, replica_groups=RG,
                        ins=[a2aa_in.ap().opt()], outs=[a2aa_out.ap().opt()]),
                        "cc", 1)

        def phase_outproj():
            ol = xln_sb
            with ExitStack() as ps_:
                e = ps_.enter_context
                wsb = e(nc.sbuf_tensor("wos", [128, 2 * 2048], BF16))
                ps_d = [e(nc.psum_tensor(f"od{i}", [128, TPC], F32))
                        for i in range(2)]
                ctx.w(SY, "cc")
                col = ctx.chan("col")
                for a, b in _split4(FC):
                    ap = bass.AP(a2aa_out, a * 128 * TPC,
                                 [[TPC, 128], [128 * TPC, b - a], [1, TPC]])
                    ctx.inc(SY.dma_start(
                        out=ol[:, a * TPC:b * TPC].rearrange(
                            "p (c t) -> p c t", c=b - a), in_=ap), col, 16)
                w_ld = [0] * 16
                w_free = [0, 0]
                add_done = [0, 0]

                def load_w(dm):
                    par = dm % 2
                    ctx.w(GP, "pe", w_free[par])
                    ap = bass.AP(wo_in.tensor, dm * 128 * 16 * 128,
                                 [[16 * 128, 128], [1, 16 * 128]])
                    ctx.inc(GP.dma_start(
                        out=wsb[:, par * 2048:(par + 1) * 2048], in_=ap),
                        ctx.chan(f"cwo{par}"), 16)

                load_w(0)
                for dm in range(16):
                    par = dm % 2
                    if dm + 1 < 16:
                        load_w(dm + 1)
                    if DEBUG_OP_STAGE < 2:
                        continue
                    ctx.wc(TE, f"cwo{par}")
                    if dm == 0:
                        ctx.wc(TE, "col")
                    ctx.w(TE, "ve", add_done[par])
                    last = None
                    for k in range(FC):
                        for h in range(2):
                            last = TE.matmul(
                                ps_d[par][:, h * 512:(h + 1) * 512],
                                wsb[:, par * 2048 + k * 128:
                                    par * 2048 + (k + 1) * 128],
                                ol[:, k * TPC + h * 512:k * TPC + (h + 1) * 512],
                                start=(k == 0), stop=(k == FC - 1))
                    mm = ctx.inc(last, "pe", 1)
                    w_free[par] = mm
                    ctx.w(VE, "pe", mm)
                    last = None
                    for h in range(2):
                        sl = x_sb[:, dm * TPC + h * 512:dm * TPC + (h + 1) * 512]
                        last = VE.tensor_add(sl, sl,
                                             ps_d[par][:, h * 512:(h + 1) * 512])
                    add_done[par] = ctx.last(last, "ve", 1)

        def phase_ffn(tag):
            with ExitStack() as ps_:
                e = ps_.enter_context
                h_sb = e(nc.sbuf_tensor(f"h{tag}", [128, 32 * TPC], BF16))
                w1g = e(nc.sbuf_tensor(f"wg{tag}", [128, 2 * 2048], BF16))
                w1u = e(nc.sbuf_tensor(f"wu{tag}", [128, 2 * 2048], BF16))
                w2sb = e(nc.sbuf_tensor(f"w2{tag}", [128, 2 * 2048], BF16))
                su = e(nc.sbuf_tensor(f"su{tag}", [128, 2 * TPC], F32))
                w1_ld = {}
                w1_free = [0, 0]
                w2_ld = {}
                w2_free = [0, 0]
                h_war = 0

                def load_w1(j):
                    par = j % 2
                    ch = ctx.chan(f"cw1{par}")
                    ctx.w(GP, "pe", w1_free[par])
                    ap = bass.AP(w1_in.tensor, j * 128 * 2048,
                                 [[2048, 128], [1, 2048]])
                    ctx.inc(GP.dma_start(
                        out=w1g[:, par * 2048:(par + 1) * 2048], in_=ap),
                        ch, 16)
                    ap = bass.AP(w1_in.tensor, (j + 64) * 128 * 2048,
                                 [[2048, 128], [1, 2048]])
                    ctx.inc(GP.dma_start(
                        out=w1u[:, par * 2048:(par + 1) * 2048], in_=ap),
                        ch, 16)

                w2c = [0, 0]
                w2_ld = {}

                def load_w2(g, ih):
                    dm, h2 = g >> 1, g & 1
                    sl = g % 2
                    ctx.w(GP, "pe", w2c[sl])
                    ap = bass.AP(w2_in.tensor,
                                 (dm * 2 + ih) * 128 * 32 * 128 + h2 * 2048,
                                 [[32 * 128, 128], [1, 2048]])
                    w2_ld[(ih, g)] = ctx.inc(GP.dma_start(
                        out=w2sb[:, sl * 2048:(sl + 1) * 2048], in_=ap),
                        ctx.chan(f"cw2{sl}"), 16)

                prev_down = 0
                for ih in range(2):
                    with nc.psum_tensor(f"fg{tag}{ih}", [128, TPC], F32) as ps_g, \
                         nc.psum_tensor(f"fu{tag}{ih}", [128, TPC], F32) as ps_u:
                        si_done = 0
                        hm_done = 0
                        hm_slot = [0, 0]
                        for jj in range(min(32, DEBUG_FFN_JJ)):
                            j = 32 * ih + jj
                            if jj == 0:
                                ctx.w(TE, "ve", prev_down)
                                load_w1(j)
                            if jj + 1 < min(32, DEBUG_FFN_JJ):
                                load_w1(j + 1)
                            if DEBUG_FFN_STAGE < 2:
                                continue
                            ctx.wc(TE, f"cw1{j % 2}")
                            ctx.w(TE, "se", si_done)
                            par = j % 2
                            last = None
                            for fc in range(FC):
                                for h in range(2):
                                    last = TE.matmul(
                                        ps_g[:, h * 512:(h + 1) * 512],
                                        w1g[:, par * 2048 + fc * 128:
                                            par * 2048 + (fc + 1) * 128],
                                        xln_sb[:, fc * TPC + h * 512:
                                               fc * TPC + (h + 1) * 512],
                                        start=(fc == 0), stop=(fc == FC - 1))
                            g_done = ctx.inc(last, "pe", 1)
                            if DEBUG_FFN_STAGE < 3:
                                continue
                            ctx.w(TE, "ve", hm_done)
                            last = None
                            for fc in range(FC):
                                for h in range(2):
                                    last = TE.matmul(
                                        ps_u[:, h * 512:(h + 1) * 512],
                                        w1u[:, par * 2048 + fc * 128:
                                            par * 2048 + (fc + 1) * 128],
                                        xln_sb[:, fc * TPC + h * 512:
                                               fc * TPC + (h + 1) * 512],
                                        start=(fc == 0), stop=(fc == FC - 1))
                            u_done = ctx.inc(last, "pe", 1)
                            w1_free[par] = u_done
                            if DEBUG_FFN_STAGE < 4:
                                continue
                            spar = jj % 2
                            ctx.w(SC, "pe", g_done)
                            ctx.w(SC, "ve", hm_slot[spar])
                            si_done = ctx.last(SC.activation(
                                su[:, spar * TPC:(spar + 1) * TPC], ps_g[:],
                                AF.Copy if DEBUG_FFN_NOSILU else AF.Silu),
                                "se", 1)
                            if DEBUG_FFN_STAGE < 5:
                                continue
                            ctx.w(VE, "se", si_done)
                            ctx.w(VE, "pe", u_done)
                            if jj == 0 and ih == 1:
                                ctx.w(VE, "pe", h_war)
                            hm_done = ctx.last(VE.tensor_mul(
                                h_sb[:, jj * TPC:(jj + 1) * TPC],
                                su[:, spar * TPC:(spar + 1) * TPC], ps_u[:]),
                                "ve", 1)
                            hm_slot[spar] = hm_done
                        h_ready = hm_done
                    if DEBUG_FFN_GU_ONLY:
                        prev_down = h_ready
                        continue
                    with nc.psum_tensor(f"fd{tag}{ih}0", [128, TPC], F32) as pd0, \
                         nc.psum_tensor(f"fd{tag}{ih}1", [128, TPC], F32) as pd1:
                        ps_dn = [pd0, pd1]
                        add_done = [0, 0]
                        load_w2(0, ih)
                        load_w2(1, ih)
                        for dm in range(16):
                            par = dm % 2
                            if dm == 0:
                                ctx.w(TE, "ve", h_ready)
                            ctx.w(TE, "ve", add_done[par])
                            for h2 in range(2):
                                g = 2 * dm + h2
                                sl = g % 2
                                ctx.wc(TE, f"cw2{sl}")
                                last = None
                                for i2h in range(16):
                                    i2 = 16 * h2 + i2h
                                    for h in range(2):
                                        last = TE.matmul(
                                            ps_dn[par][:, h * 512:(h + 1) * 512],
                                            w2sb[:, sl * 2048 + i2h * 128:
                                                 sl * 2048 + (i2h + 1) * 128],
                                            h_sb[:, i2 * TPC + h * 512:
                                                 i2 * TPC + (h + 1) * 512],
                                            start=(i2 == 0), stop=(i2 == 31))
                                w2c[sl] = ctx.inc(last, "pe", 1)
                                if g + 2 < 32:
                                    load_w2(g + 2, ih)
                            mm = w2c[(2 * dm + 1) % 2]
                            h_war = mm
                            ctx.w(VE, "pe", mm)
                            last = None
                            for h in range(2):
                                slc = x_sb[:, dm * TPC + h * 512:
                                           dm * TPC + (h + 1) * 512]
                                last = VE.tensor_add(
                                    slc, slc,
                                    ps_dn[par][:, h * 512:(h + 1) * 512])
                            add_done[par] = ctx.last(last, "ve", 1)
                        prev_down = max(add_done)

        def phase_attn_b():
            with ExitStack() as ps_:
                e = ps_.enter_context
                k_sb = e(nc.sbuf_tensor("bk", [128, 2 * T], BF16))
                v_sb = e(nc.sbuf_tensor("bv", [128, 64 * 256], BF16))
                q_sb = e(nc.sbuf_tensor("bq", [128, 4 * 512], BF16))
                pt_sb = e(nc.sbuf_tensor("bpt", [128, 2 * 512], BF16))
                mx_sb = e(nc.sbuf_tensor("bmx", [128, 4 * 17], F32))
                mfin = e(nc.sbuf_tensor("bmf", [128, 4], F32))
                mt_sb = e(nc.sbuf_tensor("bmt", [4, 128], F32))
                mrow = e(nc.sbuf_tensor("bmr", [1, 2 * 512], F32))
                mrowb = e(nc.sbuf_tensor("bmb", [1, 2 * 512], BF16))
                rcp = e(nc.sbuf_tensor("brc", [1, 2 * 512], F32))
                f_sb = e(nc.sbuf_tensor("bf", [128, 2 * 512], F32))
                y1_sb = e(nc.sbuf_tensor("by1", [128, 2 * 512], F32))
                y2t = e(nc.sbuf_tensor("by2", [128, 512], F32))
                ystg = e(nc.sbuf_tensor("bys", [128, 2 * 512], F32))
                ps_s = [e(nc.psum_tensor(f"bs{i}", [128, 512], F32))
                        for i in range(2)]
                ps_t = [e(nc.psum_tensor(f"bt{i}", [128, 512], F32))
                        for i in range(2)]
                ps_rs = e(nc.psum_tensor("brs", [1, 512], F32))
                ps_o = [e(nc.psum_tensor(f"bo{i}", [128, 512], F32))
                        for i in range(2)]
                ps_ms = e(nc.psum_tensor("bms", [128, 512], F32))
                ctx.wall_stores(SY)
                ckb = ctx.chan("ckb")
                for hs in range(2):
                    for a, b in ((0, T // 2), (T // 2, T)):
                        ctx.inc(SY.dma_start(
                            out=k_sb[:, hs * T + a:hs * T + b],
                            in_=bass.AP(kb_s, hs * 128 * T + a,
                                        [[T, 128], [1, b - a]])), ckb, 16)
                for a, b in _split4(64):
                    ap = bass.AP(vb_s, a * 128 * 256,
                                 [[256, 128], [128 * 256, b - a], [1, 256]])
                    ctx.inc(SY.dma_start(
                        out=v_sb[:, a * 256:b * 256].rearrange(
                            "p (u d) -> p u d", u=b - a), in_=ap), ckb, 16)
                q_free = {}
                qld = {}
                s_free = [0, 0]
                exp_done = [0, 0]
                pv_done = [0, 0]
                mrow_war = [0, 0]
                mrowb_war = [0, 0]
                rcp_done = [0, 0]
                fcp_done = [0, 0]
                mtcp = 0
                fms_war = 0
                mfin_war = 0
                o_free = 0
                y1_done = 0
                yst_free = [0, 0]

                def load_q(Q, hs):
                    qp = 2 * (Q % 2) + hs
                    ctx.w(SY, "pe", q_free.get((Q % 2, hs), 0))
                    ctx.inc(SY.dma_start(
                        out=q_sb[:, qp * 512:(qp + 1) * 512],
                        in_=bass.AP(qb_s, hs * 128 * T + 512 * Q,
                                    [[T, 128], [1, 512]])),
                        ctx.chan(f"cbq{qp}"), 16)

                def ph1(Q, hs):
                    nonlocal mfin_war, mtcp
                    qp = 2 * (Q % 2) + hs
                    ctx.wc(TE, "ckb")
                    ctx.wc(TE, f"cbq{qp}")
                    for u in range(4 * (Q + 1)):
                        j, s4 = u // 4, u % 4
                        sl = u % 2
                        ctx.w(TE, "ve", s_free[sl])
                        mm = ctx.inc(TE.matmul(
                            ps_s[sl][:],
                            q_sb[:, qp * 512 + 128 * s4:qp * 512 + 128 * (s4 + 1)],
                            k_sb[:, hs * T + 512 * j:hs * T + 512 * (j + 1)],
                            start=True, stop=True), "pe", 1)
                        ctx.w(VE, "pe", mm)
                        if j == Q:
                            VE.tensor_add(ps_s[sl][:], ps_s[sl][:],
                                          wmqk_sb[:, 384 - 128 * s4:
                                                  896 - 128 * s4])
                        s_free[sl] = ctx.last(VE.reduce_max(
                            mx_sb[:, s4 * 17 + j:s4 * 17 + j + 1],
                            ps_s[sl][:], AX.X), "ve", 1)
                    ctx.w(VE, "pe", mfin_war)
                    last = None
                    for s4 in range(4):
                        last = VE.reduce_max(mfin[:, s4:s4 + 1],
                                             mx_sb[:, s4 * 17:s4 * 17 + Q + 1],
                                             AX.X, negate=True)
                    mxd = ctx.last(last, "ve", 1)
                    ctx.w(TE, "ve", mxd)
                    ctx.w(TE, "se", max(mtcp, fms_war))
                    mfin_war = ctx.inc(TE.transpose(ps_ms[0:4, 0:128], mfin[:],
                                                    idf_sb[:]), "pe", 1)
                    ctx.w(SC, "pe", mfin_war)
                    mtcp = ctx.last(SC.copy(mt_sb[:], ps_ms[0:4, 0:128]),
                                   "se", 1)
                    ctx.w(SY, "se", mtcp)
                    cmb = ctx.chan(f"cmb{hs}", store=True)
                    cmr = ctx.chan(f"cmr{hs}")
                    ctx.inc(SY.dma_start(
                        out=bass.AP(mb_d, (hs * NQC + Q) * 512,
                                    [[128, 4], [1, 128]]),
                        in_=mt_sb[:]), cmb, 16)
                    ctx.wc(SY, cmb)
                    ctx.w(SY, "se", mrow_war[hs])
                    ctx.inc(SY.dma_start(
                        out=mrow[:, hs * 512:(hs + 1) * 512],
                        in_=bass.AP(mb_d, (hs * NQC + Q) * 512,
                                    [[512, 1], [1, 512]])), cmr, 16)
                    ctx.wc(SC, cmr)
                    ctx.w(SC, "pe", mrowb_war[hs])
                    mrow_war[hs] = ctx.last(SC.copy(
                        mrowb[:, hs * 512:(hs + 1) * 512],
                        mrow[:, hs * 512:(hs + 1) * 512]), "se", 1)

                def ph2(Q, hs):
                    nonlocal o_free, y1_done, fms_war
                    qp = 2 * (Q % 2) + hs
                    ctx.w(TE, "se", mrow_war[hs])
                    ctx.w(TE, "ve", rcp_done[hs])
                    ctx.w(TE, "ve", o_free)
                    nsub = 4 * (Q + 1)
                    mm_of = [0] * nsub
                    maskv_of = [0] * nsub

                    def emit_ST(u):
                        j, s4 = u // 4, u % 4
                        sl = u % 2
                        ctx.w(TE, "se", exp_done[sl])
                        TE.matmul(ps_t[sl][:], onesb2[:],
                                  mrowb[:, hs * 512:(hs + 1) * 512],
                                  start=True, stop=False)
                        kt = k_sb[:, hs * T + 512 * j + 128 * s4:
                                  hs * T + 512 * j + 128 * (s4 + 1)]
                        mm_of[u] = ctx.inc(TE.matmul(
                            ps_t[sl][:], kt, q_sb[:, qp * 512:(qp + 1) * 512],
                            start=False, stop=True), "pe", 1)
                        if j == Q:
                            ctx.w(VE, "pe", mm_of[u])
                            maskv_of[u] = ctx.last(VE.tensor_add(
                                ps_t[sl][:], ps_t[sl][:],
                                wmkq_sb[:, 384 - 128 * s4:896 - 128 * s4]),
                                "ve", 1)

                    def emit_tail(u):
                        j, s4 = u // 4, u % 4
                        sl = u % 2
                        if j == Q:
                            ctx.w(SC, "ve", maskv_of[u])
                        else:
                            ctx.w(SC, "pe", mm_of[u])
                        ctx.w(SC, "pe", pv_done[sl])
                        exp_done[sl] = ctx.last(SC.activation(
                            pt_sb[:, sl * 512:(sl + 1) * 512], ps_t[sl][:],
                            AF.Exp), "se", 1)
                        ctx.w(TE, "se", exp_done[sl])
                        TE.matmul(ps_rs[:], onesb[:],
                                  pt_sb[:, sl * 512:(sl + 1) * 512],
                                  start=(u == 0), stop=(u == nsub - 1))
                        ut = 4 * j + s4
                        last = None
                        for dvc in range(2):
                            vt = v_sb[:, ut * 256 + dvc * 128:
                                      ut * 256 + (dvc + 1) * 128]
                            last = TE.matmul(
                                ps_o[dvc][:], vt,
                                pt_sb[:, sl * 512:(sl + 1) * 512],
                                start=(u == 0), stop=(u == nsub - 1))
                        pv_done[sl] = ctx.inc(last, "pe", 1)

                    emit_ST(0)
                    for u in range(nsub):
                        if u + 1 < nsub:
                            emit_ST(u + 1)
                        emit_tail(u)
                    mrowb_war[hs] = mm_of[nsub - 1]
                    q_free[(Q % 2, hs)] = pv_done[(nsub - 1) % 2]
                    ctx.w(VE, "pe", pv_done[(nsub - 1) % 2])
                    rcp_done[hs] = ctx.last(VE.reciprocal(
                        rcp[:, hs * 512:(hs + 1) * 512], ps_rs[:]), "ve", 1)
                    ctx.w(TE, "ve", rcp_done[hs])
                    ctx.w(TE, "se", max(fcp_done[hs], mtcp))
                    fb = ctx.inc(TE.matmul(ps_ms[:], onesf[:],
                                           rcp[:, hs * 512:(hs + 1) * 512],
                                           start=True, stop=True), "pe", 1)
                    ctx.w(SC, "pe", fb)
                    fms_war = ctx.last(SC.copy(f_sb[:, hs * 512:(hs + 1) * 512],
                                              ps_ms[:]), "se", 1)
                    fcp_done[hs] = fms_war
                    ctx.w(VE, "se", fcp_done[hs])
                    if hs == 0:
                        last = None
                        for dvc in range(2):
                            last = VE.tensor_mul(
                                y1_sb[:, dvc * 512:(dvc + 1) * 512],
                                ps_o[dvc][:], f_sb[:, 0:512])
                        y1_done = ctx.last(last, "ve", 1)
                        o_free = y1_done
                    else:
                        for dvc in range(2):
                            ctx.wc(VE, ctx.chan(f"cby{dvc}", store=True))
                            VE.tensor_mul(y2t[:], ps_o[dvc][:],
                                          f_sb[:, 512:1024])
                            yc = ctx.last(VE.scalar_tensor_tensor(
                                ystg[:, dvc * 512:(dvc + 1) * 512], y2t[:],
                                lamn_sb[:],
                                y1_sb[:, dvc * 512:(dvc + 1) * 512],
                                ALU.mult, ALU.add), "ve", 1)
                            o_free = yc
                            ctx.w(SY, "ve", yc)
                            ctx.inc(SY.dma_start(
                                out=bass.AP(
                                    a2ab_in,
                                    (256 * (Q // 2) + 128 * dvc) * TPC + 512 * (Q % 2),
                                    [[TPC, 128], [1, 512]]),
                                in_=ystg[:, dvc * 512:(dvc + 1) * 512]),
                                f"cby{dvc}", 16)

                load_q(0, 0)
                load_q(0, 1)
                for Q in range(NQC):
                    if Q + 1 < NQC:
                        load_q(Q + 1, 0)
                        load_q(Q + 1, 1)
                    ph1(Q, 0)
                    ph1(Q, 1)
                    ph2(Q, 0)
                    ph2(Q, 1)
                ctx.wall_stores(GP)
                with allow_ici_all_to_all():
                    ctx.inc(GP.collective_compute(
                        "AllToAll", ALU.bypass, replica_groups=RG,
                        ins=[a2ab_in.ap().opt()], outs=[a2ab_out.ap().opt()]),
                        "cc", 1)

        def phase_yadd():
            with nc.sbuf_tensor("yt", [128, 2 * TPC], F32) as ytmp:
                ctx.w(SY, "cc")
                cons = [0, 0]
                for fc in range(FC):
                    par = fc % 2
                    cyt = ctx.chan(f"cyt{par}")
                    ctx.w(SY, "ve", cons[par])
                    ctx.inc(SY.dma_start(
                        out=ytmp[:, par * TPC:(par + 1) * TPC],
                        in_=bass.AP(a2ab_out, fc * 128 * TPC,
                                    [[TPC, 128], [1, TPC]])), cyt, 16)
                    ctx.wc(VE, cyt)
                    sl = x_sb[:, fc * TPC:(fc + 1) * TPC]
                    cons[par] = ctx.last(VE.tensor_add(
                        sl, sl, ytmp[:, par * TPC:(par + 1) * TPC]), "ve", 1)

        def phase_out():
            with nc.sbuf_tensor("os", [128, 2 * D], F32) as ostg, \
                 nc.psum_tensor("ot0", [128, 128], F32) as pt0, \
                 nc.psum_tensor("ot1", [128, 128], F32) as pt1:
                ps_tr = [pt0, pt1]
                cp_done = [0, 0]
                st_done = [0, 0]
                for ts in range(8):
                    tpar = ts % 2
                    ctx.wc(SC, ctx.chan(f"cot{tpar}", store=True))
                    for fc in range(FC):
                        rot = fc % 2
                        ctx.w(TE, "se", cp_done[rot])
                        tr = ctx.inc(TE.transpose(
                            ps_tr[rot][:],
                            x_sb[:, fc * TPC + ts * 128:fc * TPC + (ts + 1) * 128],
                            idf_sb[:]), "pe", 1)
                        ctx.w(SC, "pe", tr)
                        cp_done[rot] = ctx.last(SC.copy(
                            ostg[:, tpar * D + fc * 128:tpar * D + (fc + 1) * 128],
                            ps_tr[rot][:]), "se", 1)
                    ctx.w(SY, "se", cp_done[0])
                    ctx.w(SY, "se", cp_done[1])
                    ctx.inc(SY.dma_start(
                        out=bass.AP(out_ext.tensor, ts * 128 * D,
                                    [[D, 128], [1, D]]),
                        in_=ostg[:, tpar * D:(tpar + 1) * D]),
                        f"cot{tpar}", 16)
                ctx.wall_stores(SY)

        def dump_d(name, dram, nelem):
            if name not in dbg:
                return
            ctx.barrier()
            ctx.wall_stores(GP)
            ctx.w(GP, "cc")
            ctx.inc(GP.dma_start(
                out=bass.AP(dbg[name].tensor, 0, [[1, nelem]]),
                in_=bass.AP(dram, 0, [[1, nelem]])), "lw", 16)
            ctx.w(GP, "lw")

        def dump(name, src):
            if name not in dbg:
                return
            ctx.barrier()
            for fc in range(FC):
                ap = bass.AP(dbg[name].tensor, fc * 128 * TPC,
                             [[TPC, 128], [1, TPC]])
                ctx.inc(GP.dma_start(out=ap,
                                     in_=src[:, fc * TPC:(fc + 1) * TPC]),
                        "lw", 16)
            ctx.w(GP, "lw")

        phases = [
            ("p0", phase0, ("x",)),
            ("ln1", lambda: phase_ln("l1", to_ag=True, gates=True), ("xln",)),
            ("qkva", lambda: phase_qkv("qa", wa_in, attn_b=False), ("qkva",)),
            ("attna", phase_attn_a, ("attna",)),
            ("outproj", phase_outproj, ("x",)),
            ("ln2", lambda: phase_ln("l2"), ("xln",)),
            ("ffn1", lambda: phase_ffn("f1"), ("x",)),
            ("ln3", lambda: phase_ln("l3", to_ag=True), ("xln",)),
            ("qkvb", lambda: phase_qkv("qb", wb_in, attn_b=True), ("qkvb",)),
            ("attnb", phase_attn_b, ("attnb",)),
            ("yadd", phase_yadd, ("x",)),
            ("ln4", lambda: phase_ln("l4"), ("xln",)),
            ("ffn2", lambda: phase_ffn("f2"), ("x",)),
            ("out", phase_out, ()),
        ]

        @block.sync
        def _(sync):
            for i, (name, fn, dumps) in enumerate(phases):
                if debug_upto is not None and i > debug_upto:
                    break
                fn()
                ctx.barrier()
                for d in dumps:
                    if d == "qkva":
                        dump_d("qa", qa_s, 256 * T)
                        dump_d("ka", ka_s, 256 * T)
                        dump_d("va", va_s, 2 * T * 128)
                        dump_d("g", g_out, 16 * TPC)
                    elif d == "qkvb":
                        dump_d("qb", qb_s, 256 * T)
                        dump_d("kb", kb_s, 256 * T)
                        dump_d("vb", vb_s, T * 256)
                    elif d == "attna":
                        dump_d("oa", a2aa_out, D * TPC)
                        dump_d("aain", a2aa_in, D * TPC)
                    elif d == "attnb":
                        dump_d("ob", a2ab_out, D * TPC)
                    else:
                        dump(d, {"x": x_sb, "xln": xln_sb}[d])
            ctx.w(SY, "st")

    return nc


def _host_tables():
    inv = 10000.0 ** (-np.arange(0, HD, 2, dtype=np.float64) / HD)
    pos = np.arange(T, dtype=np.float64)
    fr = pos[:, None] * inv[None, :]
    cos = np.cos(fr).T.astype(np.float32)
    sin = np.sin(fr).T.astype(np.float32)
    cos2 = np.ascontiguousarray(np.concatenate([cos, cos], axis=0))
    sin2 = np.ascontiguousarray(np.concatenate([sin, sin], axis=0))
    logp = np.ascontiguousarray(
        np.log(np.arange(1, T + 1, dtype=np.float64)).astype(np.float32)[None, :])
    identf = np.eye(128, dtype=np.float32)
    pim = np.zeros((128, 128), dtype=np.float32)
    for j in range(64):
        pim[64 + j, j] = 1.0
        pim[j, j + 64] = -1.0
    ii = np.arange(128)[:, None]
    ww = np.arange(896)[None, :]
    wmqk = np.where(ww > ii + 384, NEG_BIG, 0.0).astype(np.float32)
    wmkq = np.where(ww < ii + 384, NEG_BIG, 0.0).astype(np.float32)
    return cos2, sin2, logp, identf, pim, wmqk, wmkq


def kernel(L0_attn_a_x, L0_attn_a_w_qkv, L0_attn_a_w_gates, L0_attn_a_w_out,
           L0_attn_b_Wq, L0_attn_b_Wk, L0_attn_b_Wv,
           L0_attn_b_lambda_q1, L0_attn_b_lambda_k1,
           L0_attn_b_lambda_q2, L0_attn_b_lambda_k2,
           L0_attn_b_softmax_scaler, L0_ffn_w_gate_up, L0_ffn_w_down):
    x = np.asarray(L0_attn_a_x, dtype=np.float32).reshape(T, D)
    xT = np.ascontiguousarray(x.T)
    wqkv = np.asarray(L0_attn_a_w_qkv, dtype=np.float32).reshape(D, 3, NH, HD)
    perm0 = [h for p_ in range(8) for h in (p_, p_ + 8)]
    w_gates = np.ascontiguousarray(
        np.asarray(L0_attn_a_w_gates, dtype=np.float32)[:, perm0])
    w_out = np.asarray(L0_attn_a_w_out, dtype=np.float32)
    perm = [h for p_ in range(8) for h in (p_, p_ + 8)]
    w_out_p = w_out.reshape(NH, HD, D)[perm].reshape(D, D)
    wo_t = np.ascontiguousarray(
        w_out_p.reshape(16, 128, 16, 128).transpose(2, 1, 0, 3))
    wgu = np.asarray(L0_ffn_w_gate_up, dtype=np.float32)
    w1_t = np.ascontiguousarray(
        wgu.reshape(128, 128, 16, 128).transpose(0, 3, 2, 1))
    wdn = np.asarray(L0_ffn_w_down, dtype=np.float32)
    w2_t = np.ascontiguousarray(
        wdn.reshape(16, 128, 2, 32, 128).transpose(0, 2, 4, 3, 1))
    Wq = np.asarray(L0_attn_b_Wq, dtype=np.float32).reshape(D, NH, HD)
    Wk = np.asarray(L0_attn_b_Wk, dtype=np.float32).reshape(D, NH, HD)
    Wv = np.asarray(L0_attn_b_Wv, dtype=np.float32).reshape(D, 8, 256)
    scaler = np.asarray(L0_attn_b_softmax_scaler, dtype=np.float32).reshape(NH)
    lam = np.ascontiguousarray(np.concatenate([
        np.asarray(L0_attn_b_lambda_q1, dtype=np.float32),
        np.asarray(L0_attn_b_lambda_k1, dtype=np.float32),
        np.asarray(L0_attn_b_lambda_q2, dtype=np.float32),
        np.asarray(L0_attn_b_lambda_k2, dtype=np.float32)])[None, :])
    cos2, sin2, logp, identf, pim, wmqk, wmkq = _host_tables()

    in_maps = []
    for c in range(NCORE):
        wa = np.ascontiguousarray(np.concatenate(
            [wqkv[:, 0, c], wqkv[:, 0, c + 8],
             wqkv[:, 1, c], wqkv[:, 1, c + 8],
             wqkv[:, 2, c], wqkv[:, 2, c + 8]], axis=1))
        wb = np.ascontiguousarray(np.concatenate(
            [Wq[:, c], Wq[:, c + 8], Wk[:, c], Wk[:, c + 8], Wv[:, c]],
            axis=1))
        in_maps.append({
            "xT": np.ascontiguousarray(xT[:, c * TPC:(c + 1) * TPC]),
            "wa": wa, "wg": w_gates, "wo": wo_t, "wb": wb,
            "w1": w1_t, "w2": w2_t,
            "cos2": cos2, "sin2": sin2, "logpos": logp,
            "scal": np.ascontiguousarray(scaler[[c, c + 8]][None, :]),
            "lam": lam, "identf": identf, "pimat": pim,
            "wmqk": wmqk, "wmkq": wmkq,
        })

    nc = build(debug_upto=DEBUG_UPTO, debug_dumps=DEBUG_DUMPS)
    res = run_bass_kernel_spmd(nc, in_maps, core_ids=list(range(NCORE)))
    if DEBUG_DUMPS:
        kernel.debug_results = res
    out = np.concatenate([res.results[c]["out"] for c in range(NCORE)], axis=0)
    return out.reshape(1, T, D)


# version 02
# speedup vs baseline: 1.0197x; 1.0197x over previous
import math
import numpy as np
from contextlib import ExitStack

import concourse.bass as bass
import concourse.mybir as mybir
from concourse.bass_utils import run_bass_kernel_spmd
from concourse.aot_env import allow_ici_all_to_all

F32 = mybir.dt.float32
BF16 = mybir.dt.bfloat16
AF = mybir.ActivationFunctionType
ALU = mybir.AluOpType
AX = mybir.AxisListType

NCORE = 8
T = 8192
D = 2048
TPC = T // NCORE
FC = D // 128
NQC = T // 512
NH = 16
HD = 128
SOFTCAP = 50.0
ISQHD = 1.0 / math.sqrt(HD)
LN_EPS = 1e-5
RMS_EPS = float(np.finfo(np.float32).eps)
LAMBDA_INIT = 0.2
NEG_BIG = -1e30

DEBUG_UPTO = None
DEBUG_DUMPS = ()
DEBUG_FFN_GU_ONLY = False
DEBUG_FFN_JJ = 32
DEBUG_FFN_NOSILU = False
DEBUG_FFN_STAGE = 9
DEBUG_OP_STAGE = 9


class SerialEngine:

    _PASS = ("wait_ge", "nop", "maybe_nop_then_inc")

    def __init__(self, ctx, eng, sem):
        self._ctx, self._eng, self._sem = ctx, eng, sem

    def __getattr__(self, name):
        fn = getattr(self._eng, name)
        if name in SerialEngine._PASS or name.startswith("_"):
            return fn

        def wrapped(*a, **k):
            self._ctx.w(self._eng, self._sem)
            instr = fn(*a, **k)
            self._ctx.inc(instr, self._sem, 1)
            return instr
        return wrapped


class Ctx:
    def __init__(self, nc, stack):
        self.nc = nc
        self.stack = stack
        self.sems = {}
        self.cnt = {}
        self.store_chans = set()
        for name in ("ls", "lw", "st", "cc", "pe", "se", "ve", "gp",
                     "b0", "b1", "b2", "b3", "b4"):
            self.sems[name] = stack.enter_context(nc.semaphore(name))
            self.cnt[name] = 0
        self.round = 0

    def chan(self, name, store=False):
        if name not in self.sems:
            self.sems[name] = self.stack.enter_context(self.nc.semaphore(name))
            self.cnt[name] = 0
        if store:
            self.store_chans.add(name)
        return name

    def wc(self, eng, name):
        if self.cnt.get(name, 0) > 0:
            eng.wait_ge(self.sems[name], self.cnt[name])

    def wall_stores(self, eng):
        self.wc(eng, "st")
        for c in sorted(self.store_chans):
            self.wc(eng, c)

    def inc(self, instr, name, k=1):
        instr.then_inc(self.sems[name], k)
        self.cnt[name] += k
        return self.cnt[name]

    def last(self, instr, name, k=1):
        return self.cnt[name]

    def w(self, eng, name, v=None):
        val = self.cnt[name] if v is None else v
        if val > 0:
            eng.wait_ge(self.sems[name], val)

    def barrier(self):
        nc = self.nc
        self.round += 1
        r = self.round
        for name in list(self.sems):
            if name.startswith("b"):
                continue
            if name in ("pe", "se", "ve", "gp"):
                continue
            self.wc(nc.sync, name)
        engs = [(nc.sync, "b0", None), (nc.gpsimd, "b1", "gp"),
                (nc.tensor, "b2", "pe"), (nc.scalar, "b3", "se"),
                (nc.vector, "b4", "ve")]
        for eng, s, own in engs:
            if own is not None:
                self.wc(eng, own)
            self.inc(eng.nop(), s, 1)
        for eng, s, _ in engs:
            for _, s2, _ in engs:
                if s2 != s:
                    eng.wait_ge(self.sems[s2], r)


def _split4(n, k=4):
    k = min(k, n)
    base, rem = n // k, n % k
    out, a = [], 0
    for i in range(k):
        b = a + base + (1 if i < rem else 0)
        out.append((a, b))
        a = b
    return out


def build(debug_upto=None, debug_dumps=()):
    nc = bass.Bass("TRN2", target_bir_lowering=False)

    xT_in = nc.dram_tensor("xT", [D, TPC], F32, kind="ExternalInput").ap()
    wa_in = nc.dram_tensor("wa", [D, 768], F32, kind="ExternalInput").ap()
    wg_in = nc.dram_tensor("wg", [D, 16], F32, kind="ExternalInput").ap()
    wo_in = nc.dram_tensor("wo", [16, 128, 16, 128], F32, kind="ExternalInput").ap()
    wb_in = nc.dram_tensor("wb", [D, 768], F32, kind="ExternalInput").ap()
    w1_in = nc.dram_tensor("w1", [128, 128, 16, 128], F32, kind="ExternalInput").ap()
    w2_in = nc.dram_tensor("w2", [16, 2, 128, 32, 128], F32, kind="ExternalInput").ap()
    cos_in = nc.dram_tensor("cos2", [128, T], F32, kind="ExternalInput").ap()
    sin_in = nc.dram_tensor("sin2", [128, T], F32, kind="ExternalInput").ap()
    logp_in = nc.dram_tensor("logpos", [1, T], F32, kind="ExternalInput").ap()
    scal_in = nc.dram_tensor("scal", [1, 2], F32, kind="ExternalInput").ap()
    lam_in = nc.dram_tensor("lam", [1, 256], F32, kind="ExternalInput").ap()
    idf_in = nc.dram_tensor("identf", [128, 128], F32, kind="ExternalInput").ap()
    pi_in = nc.dram_tensor("pimat", [128, 128], F32, kind="ExternalInput").ap()
    wmqk_in = nc.dram_tensor("wmqk", [128, 896], F32, kind="ExternalInput").ap()
    wmkq_in = nc.dram_tensor("wmkq", [128, 896], F32, kind="ExternalInput").ap()
    out_ext = nc.dram_tensor("out", [TPC, D], F32, kind="ExternalOutput").ap()

    dbg = {}
    for name in debug_dumps:
        shp = {"x": [D, TPC], "xln": [D, TPC], "qa": [256, T], "ka": [256, T],
               "va": [2 * T, 128], "qb": [256, T], "kb": [256, T],
               "vb": [T, 256], "g": [16, TPC], "oa": [D, TPC],
               "ob": [D, TPC], "aain": [D, TPC], "ai_k": [128, 512], "ai_q": [128, 512],
               "ai_pt": [128, 512], "ai_rs": [1, 512], "ai_o": [128, 512],
               "ai_f": [128, 512]}[name]
        dbg[name] = nc.dram_tensor("dbg_" + name, shp, F32,
                                   kind="ExternalOutput").ap()

    ag_in = nc.dram_tensor("ag_in", [D, TPC], BF16)
    ag_out = nc.dram_tensor("ag_out", [NCORE * D, TPC], BF16)
    g_in = nc.dram_tensor("g_in", [16, TPC], F32)
    g_out = nc.dram_tensor("g_out", [16, TPC], F32)
    a2aa_in = [nc.dram_tensor(f"a2aa_in{i}", [NCORE * 128, TPC], BF16)
               for i in range(2)]
    a2aa_out = [nc.dram_tensor(f"a2aa_out{i}", [NCORE * 128, TPC], BF16)
                for i in range(2)]
    a2ab_in = nc.dram_tensor("a2ab_in", [D, TPC], F32)
    a2ab_out = nc.dram_tensor("a2ab_out", [D, TPC], F32)
    qa_s = nc.dram_tensor("qa_s", [256, T], BF16)
    ka_s = nc.dram_tensor("ka_s", [256, T], BF16)
    va_s = nc.dram_tensor("va_s", [2, T, 128], BF16)
    qb_s = nc.dram_tensor("qb_s", [256, T], BF16)
    kb_s = nc.dram_tensor("kb_s", [256, T], BF16)
    vb_s = nc.dram_tensor("vb_s", [T, 256], BF16)
    mb_d = nc.dram_tensor("mb_d", [2, NQC, 512], F32)

    RG = [list(range(NCORE))]

    with ExitStack() as st_:
        E = st_.enter_context
        ctx = Ctx(nc, st_)
        SY, GP, TE = nc.sync, nc.gpsimd, nc.tensor
        SC = SerialEngine(ctx, nc.scalar, "se")
        VE = SerialEngine(ctx, nc.vector, "ve")

        x_sb = E(nc.sbuf_tensor("x_sb", [128, FC * TPC], F32))
        xln_sb = E(nc.sbuf_tensor("xln_sb", [128, FC * TPC], BF16))
        idf_sb = E(nc.sbuf_tensor("idf_sb", [128, 128], F32))
        idb_sb = E(nc.sbuf_tensor("idb_sb", [128, 128], BF16))
        pi_sb = E(nc.sbuf_tensor("pi_sb", [128, 128], BF16))
        wmqk_sb = E(nc.sbuf_tensor("wmqk_sb", [128, 896], F32))
        wmkq_sb = E(nc.sbuf_tensor("wmkq_sb", [128, 896], F32))
        onesb = E(nc.sbuf_tensor("onesb", [128, 1], BF16))
        onesf = E(nc.sbuf_tensor("onesf", [1, 128], F32))
        onesb2 = E(nc.sbuf_tensor("onesb2", [1, 128], BF16))
        neg20 = E(nc.sbuf_tensor("neg20", [128, 1], F32))
        epsln = E(nc.sbuf_tensor("epsln", [1, 1], F32))
        epsrms = E(nc.sbuf_tensor("epsrms", [1, 1], F32))
        scal_sb = E(nc.sbuf_tensor("scal_sb", [1, 2], F32))
        lam_sb = E(nc.sbuf_tensor("lam_sb", [1, 256], F32))
        lamn_sb = E(nc.sbuf_tensor("lamn_sb", [128, 1], F32))
        wg_sb = E(nc.sbuf_tensor("wg_sb", [128, FC * 16], BF16))
        block = E(nc.Block())


        def phase0():
            for a, b in _split4(FC):
                ap = bass.AP(xT_in.tensor, a * 128 * TPC,
                             [[TPC, 128], [128 * TPC, b - a], [1, TPC]])
                ctx.inc(SY.dma_start(
                    out=x_sb[:, a * TPC:b * TPC].rearrange("p (c t) -> p c t",
                                                           c=b - a),
                    in_=ap), "ls", 16)
            ctx.inc(SY.dma_start(out=idf_sb[:], in_=idf_in), "ls", 16)
            ctx.inc(SY.dma_start(out=wmqk_sb[:], in_=wmqk_in), "ls", 16)
            ctx.inc(SY.dma_start(out=wmkq_sb[:], in_=wmkq_in), "ls", 16)
            ctx.inc(SY.dma_start(out=scal_sb[:], in_=scal_in), "ls", 16)
            ctx.inc(SY.dma_start(out=lam_sb[:], in_=lam_in), "ls", 16)
            ctx.inc(GP.dma_start(out=idb_sb[:], in_=idf_in), "lw", 16)
            ctx.inc(GP.dma_start(out=pi_sb[:], in_=pi_in), "lw", 16)
            ap = bass.AP(wg_in.tensor, 0, [[16, 128], [128 * 16, FC], [1, 16]])
            ctx.inc(GP.dma_start(
                out=wg_sb[:].rearrange("p (c m) -> p c m", c=FC), in_=ap),
                "lw", 16)
            GP.memset(onesb[:], 1.0)
            GP.memset(onesf[:], 1.0)
            GP.memset(onesb2[:], 1.0)
            GP.memset(neg20[:], -20.0)
            GP.memset(epsln[:], LN_EPS)
            ctx.inc(GP.memset(epsrms[:], RMS_EPS), "gp", 1)
            ctx.w(VE, "ls")
            ctx.w(VE, "gp")
            with nc.sbuf_tensor("lrow", [1, 64], F32) as lrow, \
                 nc.sbuf_tensor("lsc", [1, 4], F32) as lsc, \
                 nc.psum_tensor("ps_l", [128, 1], F32) as ps_l:
                VE.tensor_mul(lrow[:], lam_sb[:, 0:64], lam_sb[:, 64:128])
                VE.reduce_sum(lsc[:, 0:1], lrow[:], AX.X)
                VE.tensor_mul(lrow[:], lam_sb[:, 128:192], lam_sb[:, 192:256])
                ctx.last(VE.reduce_sum(lsc[:, 1:2], lrow[:], AX.X), "ve", 1)
                ctx.w(SC, "ve")
                SC.activation(lsc[:, 2:3], lsc[:, 0:1], AF.Exp)
                ctx.last(SC.activation(lsc[:, 3:4], lsc[:, 1:2], AF.Exp), "se", 1)
                ctx.w(VE, "se")
                VE.tensor_sub(lsc[:, 0:1], lsc[:, 3:4], lsc[:, 2:3])
                ctx.last(VE.tensor_scalar_add(lsc[:, 1:2], lsc[:, 0:1],
                                             -LAMBDA_INIT), "ve", 1)
                ctx.w(TE, "ve")
                ctx.inc(TE.matmul(ps_l[:], onesf[:], lsc[:, 1:2],
                                  start=True, stop=True), "pe", 1)
                ctx.w(SC, "pe")
                ctx.last(SC.copy(lamn_sb[:], ps_l[:]), "se", 1)

        def phase_ln(tag, to_ag=False, gates=False):
            with ExitStack() as ps_:
                e = ps_.enter_context
                xb = e(nc.sbuf_tensor(f"xb{tag}", [128, 2 * TPC], BF16))
                sq = e(nc.sbuf_tensor(f"sq{tag}", [128, 2 * TPC], BF16))
                rows = e(nc.sbuf_tensor(f"ro{tag}", [1, 5 * TPC], F32))
                tsc = e(nc.sbuf_tensor(f"tc{tag}", [128, 2 * TPC], F32))
                m_row = rows[:, 0:TPC]
                msq = rows[:, TPC:2 * TPC]
                varr = rows[:, 2 * TPC:3 * TPC]
                lnt = rows[:, 3 * TPC:4 * TPC]
                rstd = rows[:, 4 * TPC:5 * TPC]
                with nc.psum_tensor(f"p1{tag}", [1, TPC], F32) as ps1, \
                     nc.psum_tensor(f"p2{tag}", [1, TPC], F32) as ps2:
                    pe_done = [0] * FC
                    for fc in range(FC):
                        par = fc % 2
                        if fc >= 2:
                            ctx.w(SC, "pe", pe_done[fc - 2])
                        xs = x_sb[:, fc * TPC:(fc + 1) * TPC]
                        cpd = ctx.last(SC.copy(
                            xb[:, par * TPC:(par + 1) * TPC], xs), "se", 1)
                        ctx.w(SC, "se", cpd)
                        ctx.last(SC.activation(
                            sq[:, par * TPC:(par + 1) * TPC],
                            xb[:, par * TPC:(par + 1) * TPC], AF.Square),
                            "se", 1)
                        ctx.w(TE, "se")
                        last = None
                        for h in range(2):
                            TE.matmul(ps1[:, h * 512:(h + 1) * 512], onesb[:],
                                      xb[:, par * TPC + h * 512:
                                         par * TPC + (h + 1) * 512],
                                      start=(fc == 0), stop=(fc == FC - 1))
                            last = TE.matmul(
                                ps2[:, h * 512:(h + 1) * 512], onesb[:],
                                sq[:, par * TPC + h * 512:
                                   par * TPC + (h + 1) * 512],
                                start=(fc == 0), stop=(fc == FC - 1))
                        pe_done[fc] = ctx.inc(last, "pe", 1)
                    ctx.w(VE, "pe")
                    VE.tensor_scalar_mul(m_row, ps1[:], 1.0 / D)
                    VE.tensor_mul(msq, m_row, m_row)
                    ctx.last(VE.scalar_tensor_tensor(
                        varr, ps2[:], 1.0 / D, msq, ALU.mult, ALU.subtract),
                        "ve", 1)
                    ctx.w(SC, "ve")
                    lnd = ctx.last(SC.activation(lnt, varr, AF.Ln,
                                                bias=epsln[:]), "se", 1)
                    ctx.w(SC, "se", lnd)
                    ctx.last(SC.activation(rstd, lnt, AF.Exp, scale=-0.5),
                            "se", 1)
                with nc.psum_tensor(f"mb{tag}", [128, TPC], F32) as ps_mb, \
                     nc.psum_tensor(f"rb{tag}", [128, TPC], F32) as ps_rb:
                    ctx.w(TE, "se")
                    last = None
                    for h in range(2):
                        TE.matmul(ps_mb[:, h * 512:(h + 1) * 512], onesf[:],
                                  m_row[:, h * 512:(h + 1) * 512],
                                  start=True, stop=True)
                        last = TE.matmul(ps_rb[:, h * 512:(h + 1) * 512],
                                         onesf[:],
                                         rstd[:, h * 512:(h + 1) * 512],
                                         start=True, stop=True)
                    ctx.inc(last, "pe", 1)
                    ctx.w(VE, "pe")
                    for fc in range(FC):
                        par = fc % 2
                        xs = x_sb[:, fc * TPC:(fc + 1) * TPC]
                        VE.tensor_sub(tsc[:, par * TPC:(par + 1) * TPC], xs,
                                      ps_mb[:])
                        done = ctx.last(VE.tensor_mul(
                            xln_sb[:, fc * TPC:(fc + 1) * TPC],
                            tsc[:, par * TPC:(par + 1) * TPC], ps_rb[:]),
                            "ve", 1)
                        if to_ag:
                            ctx.w(SY, "ve", done)
                            ctx.inc(SY.dma_start(
                                out=ag_in[fc * 128:(fc + 1) * 128, :],
                                in_=xln_sb[:, fc * TPC:(fc + 1) * TPC]),
                                "ls", 16)
                if to_ag:
                    ctx.w(GP, "ls")
                    ctx.inc(GP.collective_compute(
                        "AllGather", ALU.bypass, replica_groups=RG,
                        ins=[ag_in.ap().opt()], outs=[ag_out.ap().opt()]),
                        "cc", 1)
                if gates:
                    with nc.psum_tensor(f"pg{tag}", [16, TPC], F32) as ps_g, \
                         nc.sbuf_tensor(f"gt{tag}", [16, 2 * TPC], F32) as gt:
                        ctx.w(TE, "ve")
                        ctx.w(TE, "lw")
                        last = None
                        for fc in range(FC):
                            for h in range(2):
                                last = TE.matmul(
                                    ps_g[:, h * 512:(h + 1) * 512],
                                    wg_sb[:, fc * 16:(fc + 1) * 16],
                                    xln_sb[:, fc * TPC + h * 512:
                                           fc * TPC + (h + 1) * 512],
                                    start=(fc == 0), stop=(fc == FC - 1))
                        ctx.inc(last, "pe", 1)
                        ctx.w(SC, "pe")
                        ctx.last(SC.activation(gt[:, 0:TPC], ps_g[:], AF.Exp,
                                              scale=-1.0), "se", 1)
                        ctx.w(VE, "se")
                        VE.tensor_scalar_add(gt[:, TPC:2 * TPC], gt[:, 0:TPC],
                                             1.0)
                        ctx.last(VE.reciprocal(gt[:, 0:TPC],
                                              gt[:, TPC:2 * TPC]), "ve", 1)
                        ctx.w(SY, "ve")
                        ctx.inc(SY.dma_start(out=g_in[:, :], in_=gt[:, 0:TPC]),
                                "st", 16)
                        ctx.w(GP, "st")
                        with allow_ici_all_to_all():
                            ctx.inc(GP.collective_compute(
                                "AllToAll", ALU.bypass, replica_groups=RG,
                                ins=[g_in.ap().opt()],
                                outs=[g_out.ap().opt()]), "cc", 1)

        def phase_qkv(tag, w_dram, attn_b=False):
            q_d, k_d = (qb_s, kb_s) if attn_b else (qa_s, ka_s)
            with ExitStack() as ps_:
                e = ps_.enter_context
                wsb = e(nc.sbuf_tensor(f"w{tag}", [128, FC * 768], BF16))
                xg = e(nc.sbuf_tensor(f"xg{tag}", [128, 2 * 8192], BF16))
                if not attn_b:
                    stag = e(nc.sbuf_tensor(f"st{tag}", [128, 4 * 512], BF16))
                vst = e(nc.sbuf_tensor(f"vs{tag}", [128, 2 * 512], BF16))
                vst2 = e(nc.sbuf_tensor(f"v2{tag}", [128, 2 * 512], BF16))
                ps_p = [e(nc.psum_tensor(f"pp{i}{tag}", [128, 512], F32))
                        for i in range(3)]
                ps_vt = e(nc.psum_tensor(f"pv{tag}", [128, 512], BF16))
                if attn_b:
                    sqb = e(nc.sbuf_tensor(f"sb{tag}", [128, 2 * 512], BF16))
                    qf = e(nc.sbuf_tensor(f"qf{tag}", [128, 2 * 512], F32))
                    qsbt = e(nc.sbuf_tensor(f"qs{tag}", [128, 2 * 512], BF16))
                    tco = e(nc.sbuf_tensor(f"tco{tag}", [128, 2 * 512], F32))
                    usi = e(nc.sbuf_tensor(f"us{tag}", [128, 2 * 512], F32))
                    qstg = e(nc.sbuf_tensor(f"qg{tag}", [128, 2 * 512], BF16))
                    cs_sl = e(nc.sbuf_tensor(f"cs{tag}", [128, 2 * 512], F32))
                    sn_sl = e(nc.sbuf_tensor(f"sn{tag}", [128, 2 * 512], F32))
                    lp_sl = e(nc.sbuf_tensor(f"lp{tag}", [1, 2 * 512], F32))
                    rws = e(nc.sbuf_tensor(f"rw{tag}", [1, 2 * 1024], F32))
                    ps_ss = [e(nc.psum_tensor(f"ss{i}{tag}", [1, 512], F32))
                             for i in range(2)]
                    ps_f = e(nc.psum_tensor(f"pf{tag}", [128, 512], F32))
                    ps_rot = e(nc.psum_tensor(f"pr{tag}", [128, 512], F32))
                for a, b in _split4(FC):
                    ap = bass.AP(w_dram.tensor, a * 128 * 768,
                                 [[768, 128], [128 * 768, b - a], [1, 768]])
                    ctx.inc(GP.dma_start(
                        out=wsb[:, a * 768:b * 768].rearrange(
                            "p (c u) -> p c u", c=b - a), in_=ap), "lw", 16)
                wld = ctx.cnt["lw"]
                ctx.w(SY, "cc")
                xg_ld = [0] * 16
                xg_cons = [0, 0]
                csl_ld = [0, 0]
                csl_cons = [0, 0]
                bank_free = [0, 0, 0]
                stag_free = [0] * 4
                vst_free = [0, 0]
                psvt_free = 0
                vst2_free = [0, 0]
                qstg_free = [0, 0]
                ss_free = [0, 0]
                f_free = 0
                rot_free = 0
                frow_free = [0, 0]
                sqb_free = [0, 0]
                qf_free = [0, 0]
                qsbt_free = [0, 0]

                def load_xg(rh):
                    par = rh % 2
                    r, half = rh // 2, rh % 2
                    ch = ctx.chan(f"cxg{par}")
                    ctx.w(SY, "pe", xg_cons[par])
                    for a, b in _split4(FC):
                        ap = bass.AP(ag_out,
                                     (2048 * r + a * 128) * TPC + 512 * half,
                                     [[TPC, 128], [128 * TPC, b - a], [1, 512]])
                        ctx.inc(SY.dma_start(
                            out=xg[:, par * 8192 + a * 512:par * 8192 + b * 512]
                            .rearrange("p (c u) -> p c u", c=b - a), in_=ap),
                            ch, 16)
                    if attn_b:
                        ch2 = ctx.chan(f"ccs{par}")
                        ctx.w(SY, "ve", csl_cons[par])
                        c0 = TPC * r + 512 * half
                        ctx.inc(SY.dma_start(
                            out=cs_sl[:, par * 512:(par + 1) * 512],
                            in_=bass.AP(cos_in.tensor, c0,
                                        [[T, 128], [1, 512]])), ch2, 16)
                        ctx.inc(SY.dma_start(
                            out=sn_sl[:, par * 512:(par + 1) * 512],
                            in_=bass.AP(sin_in.tensor, c0,
                                        [[T, 128], [1, 512]])), ch2, 16)
                        ctx.inc(SY.dma_start(
                            out=lp_sl[:, par * 512:(par + 1) * 512],
                            in_=bass.AP(logp_in.tensor, c0,
                                        [[T, 1], [1, 512]])), ch2, 16)

                pending = []

                def pp_v(rh, m, bk, mm_done):
                    def go():
                        vp = m - 4
                        r, half = rh // 2, rh % 2
                        ctx.w(SC, "pe", mm_done)
                        ctx.w(SC, "pe", vst_free[vp])
                        ev = ctx.last(SC.copy(vst[:, vp * 512:(vp + 1) * 512],
                                             ps_p[bk][:]), "se", 1)
                        bank_free[bk] = ev
                        ctx.w(TE, "se", ev)
                        ctx.w(TE, "se", psvt_free)
                        last = None
                        for u in range(4):
                            last = TE.transpose(
                                ps_vt[:, u * 128:(u + 1) * 128],
                                vst[:, vp * 512 + u * 128:
                                    vp * 512 + (u + 1) * 128], idb_sb[:])
                        tr = ctx.inc(last, "pe", 1)
                        vst_free[vp] = tr
                        ctx.w(SC, "pe", tr)
                        ctx.wc(SC, ctx.chan(f"cv2{vp}", store=True))
                        nonlocal_psvt = ctx.last(SC.copy(
                            vst2[:, vp * 512:(vp + 1) * 512], ps_vt[:]),
                            "se", 1)
                        _set_psvt(nonlocal_psvt)
                        ctx.w(SY, "se", nonlocal_psvt)
                        t0 = TPC * r + 512 * half
                        if attn_b:
                            ap = bass.AP(vb_s, t0 * 256 + 128 * vp,
                                         [[256, 128], [128 * 256, 4], [1, 128]])
                        else:
                            ap = bass.AP(va_s, vp * T * 128 + t0 * 128,
                                         [[128, 128], [128 * 128, 4], [1, 128]])
                        ctx.inc(SY.dma_start(
                            out=ap,
                            in_=vst2[:, vp * 512:(vp + 1) * 512].rearrange(
                                "p (u d) -> p u d", u=4)), f"cv2{vp}", 16)
                    return go

                def _set_psvt(v):
                    nonlocal psvt_free
                    psvt_free = v

                def pp_qk_a(rh, m, bk, mm_done):
                    def go():
                        r, half = rh // 2, rh % 2
                        sl = m % 4
                        ctx.w(SC, "pe", mm_done)
                        ctx.wc(SC, ctx.chan(f"cst{sl}", store=True))
                        if m < 2:
                            ev = ctx.last(SC.activation(
                                stag[:, sl * 512:(sl + 1) * 512], ps_p[bk][:],
                                AF.Copy, scale=ISQHD), "se", 1)
                        else:
                            ev = ctx.last(SC.copy(
                                stag[:, sl * 512:(sl + 1) * 512],
                                ps_p[bk][:]), "se", 1)
                        bank_free[bk] = ev
                        ctx.w(SY, "se", ev)
                        dst = q_d if m < 2 else k_d
                        ctx.inc(SY.dma_start(
                            out=bass.AP(dst, (m % 2) * 128 * T + TPC * r + 512 * half,
                                        [[T, 128], [1, 512]]),
                            in_=stag[:, sl * 512:(sl + 1) * 512]),
                            f"cst{sl}", 16)
                    return go

                def pp_qk_b(rh, m, bk, mm_done):
                    def go():
                        nonlocal f_free, rot_free
                        r, half = rh // 2, rh % 2
                        par = rh % 2
                        rot = m % 2
                        ctx.w(SC, "pe", mm_done)
                        ctx.w(SC, "pe", sqb_free[rot])
                        sqd = ctx.last(SC.activation(
                            sqb[:, rot * 512:(rot + 1) * 512], ps_p[bk][:],
                            AF.Square), "se", 1)
                        ctx.w(TE, "se", sqd)
                        ctx.w(TE, "se", ss_free[rot])
                        ssd = ctx.inc(TE.matmul(ps_ss[rot][:], onesb[:],
                                                sqb[:, rot * 512:(rot + 1) * 512],
                                                start=True, stop=True), "pe", 1)
                        sqb_free[rot] = ssd
                        r0 = (m % 2) * 1024
                        ctx.w(SC, "pe", ssd)
                        ctx.w(SC, "pe", frow_free[m % 2])
                        lnd = ctx.last(SC.activation(
                            rws[:, r0:r0 + 512], ps_ss[rot][:], AF.Ln,
                            scale=1.0 / HD, bias=epsrms[:]), "se", 1)
                        ctx.w(SC, "se", lnd)
                        rsd = ctx.last(SC.activation(
                            rws[:, r0 + 512:r0 + 1024], rws[:, r0:r0 + 512],
                            AF.Exp, scale=-0.5), "se", 1)
                        ss_free[rot] = rsd
                        if m < 2:
                            ctx.w(VE, "se", rsd)
                            ctx.wc(VE, f"ccs{par}")
                            VE.tensor_scalar(rws[:, r0:r0 + 512],
                                             lp_sl[:, par * 512:(par + 1) * 512],
                                             scal_sb[:, m:m + 1], ISQHD,
                                             ALU.mult, ALU.mult)
                            frd = ctx.last(VE.tensor_mul(
                                rws[:, r0 + 512:r0 + 1024],
                                rws[:, r0:r0 + 512],
                                rws[:, r0 + 512:r0 + 1024]), "ve", 1)
                            ctx.w(TE, "ve", frd)
                        else:
                            ctx.w(TE, "se", rsd)
                        frow = rws[:, r0 + 512:r0 + 1024]
                        ctx.w(TE, "ve", f_free)
                        fd = ctx.inc(TE.matmul(ps_f[:], onesf[:], frow,
                                               start=True, stop=True), "pe", 1)
                        frow_free[m % 2] = fd
                        ctx.w(SC, "ve", qf_free[rot])
                        ev = ctx.last(SC.copy(qf[:, rot * 512:(rot + 1) * 512],
                                             ps_p[bk][:]), "se", 1)
                        bank_free[bk] = ev
                        ctx.w(VE, "se", ev)
                        ctx.w(VE, "pe", fd)
                        ctx.w(VE, "pe", qsbt_free[rot])
                        qsd = ctx.last(VE.tensor_mul(
                            qsbt[:, rot * 512:(rot + 1) * 512],
                            qf[:, rot * 512:(rot + 1) * 512], ps_f[:]),
                            "ve", 1)
                        qf_free[rot] = qsd
                        f_free = qsd
                        ctx.w(TE, "ve", qsd)
                        ctx.w(TE, "ve", rot_free)
                        rtd = ctx.inc(TE.matmul(
                            ps_rot[:], pi_sb[:],
                            qsbt[:, rot * 512:(rot + 1) * 512],
                            start=True, stop=True), "pe", 1)
                        ctx.wc(VE, f"ccs{par}")
                        VE.tensor_mul(tco[:, rot * 512:(rot + 1) * 512],
                                      qsbt[:, rot * 512:(rot + 1) * 512],
                                      cs_sl[:, par * 512:(par + 1) * 512])
                        ctx.w(VE, "pe", rtd)
                        VE.tensor_mul(usi[:, rot * 512:(rot + 1) * 512],
                                      ps_rot[:],
                                      sn_sl[:, par * 512:(par + 1) * 512])
                        ctx.wc(VE, ctx.chan(f"cqg{rot}", store=True))
                        qgd = ctx.last(VE.tensor_add(
                            qstg[:, rot * 512:(rot + 1) * 512],
                            tco[:, rot * 512:(rot + 1) * 512],
                            usi[:, rot * 512:(rot + 1) * 512]), "ve", 1)
                        rot_free = qgd
                        qsbt_free[rot] = rtd
                        csl_cons[par] = qgd
                        ctx.w(SY, "ve", qgd)
                        dst = q_d if m < 2 else k_d
                        ctx.inc(SY.dma_start(
                            out=bass.AP(dst, (m % 2) * 128 * T + TPC * r + 512 * half,
                                        [[T, 128], [1, 512]]),
                            in_=qstg[:, rot * 512:(rot + 1) * 512]),
                            f"cqg{rot}", 16)
                    return go

                load_xg(0)
                for rh in range(16):
                    par = rh % 2
                    if rh + 1 < 16:
                        load_xg(rh + 1)
                    first = True
                    for m in range(6):
                        bk = (rh * 6 + m) % 3
                        if first:
                            ctx.w(TE, "lw", wld)
                            ctx.wc(TE, f"cxg{par}")
                            first = False
                        ctx.w(TE, "se", bank_free[bk])
                        last = None
                        for fc in range(FC):
                            last = TE.matmul(
                                ps_p[bk][:],
                                wsb[:, fc * 768 + m * 128:fc * 768 + (m + 1) * 128],
                                xg[:, par * 8192 + fc * 512:
                                   par * 8192 + (fc + 1) * 512],
                                start=(fc == 0), stop=(fc == FC - 1))
                        mm = ctx.inc(last, "pe", 1)
                        if m == 5:
                            xg_cons[par] = mm
                        if m >= 4:
                            pending.append(pp_v(rh, m, bk, mm))
                        elif attn_b:
                            pending.append(pp_qk_b(rh, m, bk, mm))
                        else:
                            pending.append(pp_qk_a(rh, m, bk, mm))
                        while len(pending) > 2:
                            pending.pop(0)()
                    while len(pending) > 2:
                        pending.pop(0)()
                while pending:
                    pending.pop(0)()

        def phase_attn_a():
            with ExitStack() as ps_:
                e = ps_.enter_context
                k_sb = e(nc.sbuf_tensor("ak", [128, T], BF16))
                v_sb = e(nc.sbuf_tensor("av", [128, T], BF16))
                q_sb = e(nc.sbuf_tensor("aq", [128, 2 * 512], BF16))
                tt_sb = e(nc.sbuf_tensor("att", [128, 2 * 512], F32))
                pt_sb = e(nc.sbuf_tensor("apt", [128, 2 * 512], BF16))
                f_sb = e(nc.sbuf_tensor("af", [128, 2 * 512], F32))
                gsl = e(nc.sbuf_tensor("ag_", [1, 2 * 512], F32))
                rcp = e(nc.sbuf_tensor("arc", [1, 512], F32))
                frow = e(nc.sbuf_tensor("afr", [1, 512], F32))
                ostg = e(nc.sbuf_tensor("ao", [128, 2 * 512], BF16))
                ps_st = [e(nc.psum_tensor(f"as{i}", [128, 512], F32))
                         for i in range(2)]
                ps_rs = e(nc.psum_tensor("ars", [1, 512], F32))
                ps_o = e(nc.psum_tensor("apo", [128, 512], F32))
                ps_fb = e(nc.psum_tensor("afb", [128, 512], F32))
                ctx.wall_stores(SY)
                ctx.w(SY, "cc")
                kv_cons = 0
                ost_free = [0, 0]
                for hs in range(2):
                    if hs == 1:
                        ctx.wc(GP, "cao0")
                        ctx.wc(GP, "cao1")
                        with allow_ici_all_to_all():
                            ctx.inc(GP.collective_compute(
                                "AllToAll", ALU.bypass, replica_groups=RG,
                                ins=[a2aa_in[0].ap().opt()],
                                outs=[a2aa_out[0].ap().opt()]), "cc", 1)
                        ctx.w(SY, "pe", kv_cons)
                    ckv = ctx.chan("ckv")
                    for a, b in ((0, T // 2), (T // 2, T)):
                        ctx.inc(SY.dma_start(
                            out=k_sb[:, a:b],
                            in_=bass.AP(ka_s, hs * 128 * T + a,
                                        [[T, 128], [1, b - a]])), ckv, 16)
                    for a, b in ((0, 32), (32, 64)):
                        ap = bass.AP(va_s, hs * T * 128 + a * 128 * 128,
                                     [[128, 128], [128 * 128, b - a], [1, 128]])
                        ctx.inc(SY.dma_start(
                            out=v_sb[:, a * 128:b * 128].rearrange(
                                "p (u d) -> p u d", u=b - a), in_=ap), ckv, 16)
                    q_free = [0, 0]
                    g_free = [0, 0]
                    exp_done = [0, 0]
                    cons_done = [0, 0]
                    rcp_done = 0
                    om_done = 0
                    fc_done = 0
                    fb_prev = [0]
                    qld = [0] * NQC

                    def load_q(Q):
                        qp = Q % 2
                        ch = ctx.chan(f"caq{qp}")
                        ctx.w(SY, "pe", q_free[qp])
                        ctx.inc(SY.dma_start(
                            out=q_sb[:, qp * 512:(qp + 1) * 512],
                            in_=bass.AP(qa_s, hs * 128 * T + 512 * Q,
                                        [[T, 128], [1, 512]])), ch, 16)
                        ctx.w(SY, "ve", g_free[qp])
                        ctx.inc(SY.dma_start(
                            out=gsl[:, qp * 512:(qp + 1) * 512],
                            in_=bass.AP(g_out,
                                        (2 * (Q // 2) + hs) * TPC + 512 * (Q % 2),
                                        [[TPC, 1], [1, 512]])), ch, 16)

                    load_q(0)
                    for Q in range(NQC):
                        qp = Q % 2
                        if Q + 1 < NQC:
                            load_q(Q + 1)
                        ctx.wc(TE, "ckv")
                        ctx.wc(TE, f"caq{qp}")
                        ctx.w(TE, "ve", rcp_done)
                        ctx.w(TE, "ve", om_done)
                        nsub = 4 * (Q + 1)
                        mm_of = [0] * nsub
                        maskv_of = [0] * nsub

                        def emit_S(u):
                            j, s4 = u // 4, u % 4
                            slot = u % 2
                            ctx.w(TE, "se", exp_done[slot])
                            mm_of[u] = ctx.inc(TE.matmul(
                                ps_st[slot][:],
                                k_sb[:, 512 * j + 128 * s4:
                                     512 * j + 128 * (s4 + 1)],
                                q_sb[:, qp * 512:(qp + 1) * 512],
                                start=True, stop=True), "pe", 1)
                            if j == Q:
                                ctx.w(VE, "pe", mm_of[u])
                                maskv_of[u] = ctx.last(VE.tensor_add(
                                    ps_st[slot][:], ps_st[slot][:],
                                    wmkq_sb[:, 384 - 128 * s4:896 - 128 * s4]),
                                    "ve", 1)

                        def emit_tail(u):
                            j, s4 = u // 4, u % 4
                            slot = u % 2
                            if j == Q:
                                ctx.w(SC, "ve", maskv_of[u])
                            else:
                                ctx.w(SC, "pe", mm_of[u])
                            ctx.w(SC, "pe", cons_done[slot])
                            thd = ctx.last(SC.activation(
                                tt_sb[:, slot * 512:(slot + 1) * 512],
                                ps_st[slot][:], AF.Tanh,
                                scale=1.0 / SOFTCAP), "se", 1)
                            ctx.w(SC, "se", thd)
                            exp_done[slot] = ctx.last(SC.activation(
                                pt_sb[:, slot * 512:(slot + 1) * 512],
                                tt_sb[:, slot * 512:(slot + 1) * 512],
                                AF.Exp, scale=SOFTCAP, bias=neg20[:]), "se", 1)
                            ctx.w(TE, "se", exp_done[slot])
                            TE.matmul(ps_rs[:], onesb[:],
                                      pt_sb[:, slot * 512:(slot + 1) * 512],
                                      start=(u == 0), stop=(u == nsub - 1))
                            cons_done[slot] = ctx.inc(TE.matmul(
                                ps_o[:], v_sb[:, u * 128:(u + 1) * 128],
                                pt_sb[:, slot * 512:(slot + 1) * 512],
                                start=(u == 0), stop=(u == nsub - 1)), "pe", 1)

                        emit_S(0)
                        for u in range(nsub):
                            if u + 1 < nsub:
                                emit_S(u + 1)
                            emit_tail(u)
                        q_free[qp] = cons_done[(nsub - 1) % 2]
                        if hs == 0 and Q == NQC - 1:
                            kv_cons = q_free[qp]
                        ctx.w(VE, "pe", cons_done[(nsub - 1) % 2])
                        rcp_done = ctx.last(VE.reciprocal(rcp[:], ps_rs[:]),
                                           "ve", 1)
                        ctx.wc(VE, f"caq{qp}")
                        ctx.w(VE, "pe", fb_prev[0])
                        fr = ctx.last(VE.tensor_mul(
                            frow[:], rcp[:], gsl[:, qp * 512:(qp + 1) * 512]),
                            "ve", 1)
                        g_free[qp] = fr
                        ctx.w(TE, "ve", fr)
                        ctx.w(TE, "se", fc_done)
                        fb = ctx.inc(TE.matmul(ps_fb[:], onesf[:], frow[:],
                                               start=True, stop=True), "pe", 1)
                        fb_prev[0] = fb
                        ctx.w(SC, "pe", fb)
                        fc_done = ctx.last(SC.copy(
                            f_sb[:, qp * 512:(qp + 1) * 512], ps_fb[:]),
                            "se", 1)
                        ctx.w(VE, "se", fc_done)
                        ctx.wc(VE, ctx.chan(f"cao{qp}", store=True))
                        om_done = ctx.last(VE.tensor_mul(
                            ostg[:, qp * 512:(qp + 1) * 512], ps_o[:],
                            f_sb[:, qp * 512:(qp + 1) * 512]), "ve", 1)
                        ctx.w(SY, "ve", om_done)
                        ctx.inc(SY.dma_start(
                            out=bass.AP(
                                a2aa_in[hs],
                                (128 * (Q // 2)) * TPC + 512 * (Q % 2),
                                [[TPC, 128], [1, 512]]),
                            in_=ostg[:, qp * 512:(qp + 1) * 512]),
                            f"cao{qp}", 16)
                        if hs == 0 and Q == 0 and "ai_k" in dbg:
                            ctx.w(GP, "ve", om_done)
                            ctx.inc(GP.dma_start(out=dbg["ai_k"],
                                                 in_=k_sb[:, 0:512]), "lw", 16)
                            ctx.inc(GP.dma_start(out=dbg["ai_q"],
                                                 in_=q_sb[:, 0:512]), "lw", 16)
                            ctx.inc(GP.dma_start(out=dbg["ai_pt"],
                                                 in_=pt_sb[:, 512:1024]), "lw", 16)
                            ctx.inc(GP.dma_start(out=dbg["ai_rs"],
                                                 in_=rcp[:]), "lw", 16)
                            ctx.inc(GP.dma_start(out=dbg["ai_o"],
                                                 in_=ostg[:, 0:512]), "lw", 16)
                            ctx.inc(GP.dma_start(out=dbg["ai_f"],
                                                 in_=f_sb[:, 0:512]), "lw", 16)
                            ctx.w(GP, "lw")


            ctx.wc(GP, "cao0")
            ctx.wc(GP, "cao1")
            with allow_ici_all_to_all():
                ctx.inc(GP.collective_compute(
                    "AllToAll", ALU.bypass, replica_groups=RG,
                    ins=[a2aa_in[1].ap().opt()],
                    outs=[a2aa_out[1].ap().opt()]), "cc", 1)

        def phase_outproj():
            ol = xln_sb
            with ExitStack() as ps_:
                e = ps_.enter_context
                wsb = e(nc.sbuf_tensor("wos", [128, 2 * 2048], BF16))
                ps_d = [e(nc.psum_tensor(f"od{i}", [128, TPC], F32))
                        for i in range(2)]
                ctx.w(SY, "cc")
                col = ctx.chan("col")
                for half in range(2):
                    for a, b in ((0, 4), (4, 8)):
                        ap = bass.AP(a2aa_out[half], a * 128 * TPC,
                                     [[TPC, 128], [128 * TPC, b - a], [1, TPC]])
                        ctx.inc(SY.dma_start(
                            out=ol[:, (half * 8 + a) * TPC:(half * 8 + b) * TPC]
                            .rearrange("p (c t) -> p c t", c=b - a), in_=ap),
                            col, 16)
                w_ld = [0] * 16
                w_free = [0, 0]
                add_done = [0, 0]

                def load_w(dm):
                    par = dm % 2
                    ctx.w(GP, "pe", w_free[par])
                    ap = bass.AP(wo_in.tensor, dm * 128 * 16 * 128,
                                 [[16 * 128, 128], [1, 16 * 128]])
                    ctx.inc(GP.dma_start(
                        out=wsb[:, par * 2048:(par + 1) * 2048], in_=ap),
                        ctx.chan(f"cwo{par}"), 16)

                load_w(0)
                for dm in range(16):
                    par = dm % 2
                    if dm + 1 < 16:
                        load_w(dm + 1)
                    if DEBUG_OP_STAGE < 2:
                        continue
                    ctx.wc(TE, f"cwo{par}")
                    if dm == 0:
                        ctx.wc(TE, "col")
                    ctx.w(TE, "ve", add_done[par])
                    last = None
                    for k in range(FC):
                        for h in range(2):
                            last = TE.matmul(
                                ps_d[par][:, h * 512:(h + 1) * 512],
                                wsb[:, par * 2048 + k * 128:
                                    par * 2048 + (k + 1) * 128],
                                ol[:, k * TPC + h * 512:k * TPC + (h + 1) * 512],
                                start=(k == 0), stop=(k == FC - 1))
                    mm = ctx.inc(last, "pe", 1)
                    w_free[par] = mm
                    ctx.w(VE, "pe", mm)
                    last = None
                    for h in range(2):
                        sl = x_sb[:, dm * TPC + h * 512:dm * TPC + (h + 1) * 512]
                        last = VE.tensor_add(sl, sl,
                                             ps_d[par][:, h * 512:(h + 1) * 512])
                    add_done[par] = ctx.last(last, "ve", 1)

        def phase_ffn(tag):
            with ExitStack() as ps_:
                e = ps_.enter_context
                h_sb = e(nc.sbuf_tensor(f"h{tag}", [128, 32 * TPC], BF16))
                w1g = e(nc.sbuf_tensor(f"wg{tag}", [128, 2 * 2048], BF16))
                w1u = e(nc.sbuf_tensor(f"wu{tag}", [128, 2 * 2048], BF16))
                w2sb = e(nc.sbuf_tensor(f"w2{tag}", [128, 2 * 2048], BF16))
                su = e(nc.sbuf_tensor(f"su{tag}", [128, 2 * TPC], F32))
                w1_ld = {}
                w1_free = [0, 0]
                w2_ld = {}
                w2_free = [0, 0]
                h_war = 0

                def load_w1(j):
                    par = j % 2
                    ch = ctx.chan(f"cw1{par}")
                    ctx.w(GP, "pe", w1_free[par])
                    ap = bass.AP(w1_in.tensor, j * 128 * 2048,
                                 [[2048, 128], [1, 2048]])
                    ctx.inc(GP.dma_start(
                        out=w1g[:, par * 2048:(par + 1) * 2048], in_=ap),
                        ch, 16)
                    ap = bass.AP(w1_in.tensor, (j + 64) * 128 * 2048,
                                 [[2048, 128], [1, 2048]])
                    ctx.inc(GP.dma_start(
                        out=w1u[:, par * 2048:(par + 1) * 2048], in_=ap),
                        ch, 16)

                w2c = [0, 0]
                w2_ld = {}

                def load_w2(g, ih):
                    dm, h2 = g >> 1, g & 1
                    sl = g % 2
                    ctx.w(GP, "pe", w2c[sl])
                    ap = bass.AP(w2_in.tensor,
                                 (dm * 2 + ih) * 128 * 32 * 128 + h2 * 2048,
                                 [[32 * 128, 128], [1, 2048]])
                    w2_ld[(ih, g)] = ctx.inc(GP.dma_start(
                        out=w2sb[:, sl * 2048:(sl + 1) * 2048], in_=ap),
                        ctx.chan(f"cw2{sl}"), 16)

                prev_down = 0
                for ih in range(2):
                    with nc.psum_tensor(f"fg{tag}{ih}", [128, TPC], F32) as ps_g, \
                         nc.psum_tensor(f"fu{tag}{ih}", [128, TPC], F32) as ps_u:
                        si_done = 0
                        hm_done = 0
                        hm_slot = [0, 0]
                        for jj in range(min(32, DEBUG_FFN_JJ)):
                            j = 32 * ih + jj
                            if jj == 0:
                                ctx.w(TE, "ve", prev_down)
                                load_w1(j)
                            if jj + 1 < min(32, DEBUG_FFN_JJ):
                                load_w1(j + 1)
                            if DEBUG_FFN_STAGE < 2:
                                continue
                            ctx.wc(TE, f"cw1{j % 2}")
                            ctx.w(TE, "se", si_done)
                            par = j % 2
                            last = None
                            for fc in range(FC):
                                for h in range(2):
                                    last = TE.matmul(
                                        ps_g[:, h * 512:(h + 1) * 512],
                                        w1g[:, par * 2048 + fc * 128:
                                            par * 2048 + (fc + 1) * 128],
                                        xln_sb[:, fc * TPC + h * 512:
                                               fc * TPC + (h + 1) * 512],
                                        start=(fc == 0), stop=(fc == FC - 1))
                            g_done = ctx.inc(last, "pe", 1)
                            if DEBUG_FFN_STAGE < 3:
                                continue
                            ctx.w(TE, "ve", hm_done)
                            last = None
                            for fc in range(FC):
                                for h in range(2):
                                    last = TE.matmul(
                                        ps_u[:, h * 512:(h + 1) * 512],
                                        w1u[:, par * 2048 + fc * 128:
                                            par * 2048 + (fc + 1) * 128],
                                        xln_sb[:, fc * TPC + h * 512:
                                               fc * TPC + (h + 1) * 512],
                                        start=(fc == 0), stop=(fc == FC - 1))
                            u_done = ctx.inc(last, "pe", 1)
                            w1_free[par] = u_done
                            if DEBUG_FFN_STAGE < 4:
                                continue
                            spar = jj % 2
                            ctx.w(SC, "pe", g_done)
                            ctx.w(SC, "ve", hm_slot[spar])
                            si_done = ctx.last(SC.activation(
                                su[:, spar * TPC:(spar + 1) * TPC], ps_g[:],
                                AF.Copy if DEBUG_FFN_NOSILU else AF.Silu),
                                "se", 1)
                            if DEBUG_FFN_STAGE < 5:
                                continue
                            ctx.w(VE, "se", si_done)
                            ctx.w(VE, "pe", u_done)
                            if jj == 0 and ih == 1:
                                ctx.w(VE, "pe", h_war)
                            hm_done = ctx.last(VE.tensor_mul(
                                h_sb[:, jj * TPC:(jj + 1) * TPC],
                                su[:, spar * TPC:(spar + 1) * TPC], ps_u[:]),
                                "ve", 1)
                            hm_slot[spar] = hm_done
                        h_ready = hm_done
                    if DEBUG_FFN_GU_ONLY:
                        prev_down = h_ready
                        continue
                    with nc.psum_tensor(f"fd{tag}{ih}0", [128, TPC], F32) as pd0, \
                         nc.psum_tensor(f"fd{tag}{ih}1", [128, TPC], F32) as pd1:
                        ps_dn = [pd0, pd1]
                        add_done = [0, 0]
                        load_w2(0, ih)
                        load_w2(1, ih)
                        for dm in range(16):
                            par = dm % 2
                            if dm == 0:
                                ctx.w(TE, "ve", h_ready)
                            ctx.w(TE, "ve", add_done[par])
                            for h2 in range(2):
                                g = 2 * dm + h2
                                sl = g % 2
                                ctx.wc(TE, f"cw2{sl}")
                                last = None
                                for i2h in range(16):
                                    i2 = 16 * h2 + i2h
                                    for h in range(2):
                                        last = TE.matmul(
                                            ps_dn[par][:, h * 512:(h + 1) * 512],
                                            w2sb[:, sl * 2048 + i2h * 128:
                                                 sl * 2048 + (i2h + 1) * 128],
                                            h_sb[:, i2 * TPC + h * 512:
                                                 i2 * TPC + (h + 1) * 512],
                                            start=(i2 == 0), stop=(i2 == 31))
                                w2c[sl] = ctx.inc(last, "pe", 1)
                                if g + 2 < 32:
                                    load_w2(g + 2, ih)
                            mm = w2c[(2 * dm + 1) % 2]
                            h_war = mm
                            ctx.w(VE, "pe", mm)
                            last = None
                            for h in range(2):
                                slc = x_sb[:, dm * TPC + h * 512:
                                           dm * TPC + (h + 1) * 512]
                                last = VE.tensor_add(
                                    slc, slc,
                                    ps_dn[par][:, h * 512:(h + 1) * 512])
                            add_done[par] = ctx.last(last, "ve", 1)
                        prev_down = max(add_done)

        def phase_attn_b():
            with ExitStack() as ps_:
                e = ps_.enter_context
                k_sb = e(nc.sbuf_tensor("bk", [128, 2 * T], BF16))
                v_sb = e(nc.sbuf_tensor("bv", [128, 64 * 256], BF16))
                q_sb = e(nc.sbuf_tensor("bq", [128, 4 * 512], BF16))
                pt_sb = e(nc.sbuf_tensor("bpt", [128, 2 * 512], BF16))
                mx_sb = e(nc.sbuf_tensor("bmx", [128, 4 * 17], F32))
                mfin = e(nc.sbuf_tensor("bmf", [128, 4], F32))
                mt_sb = e(nc.sbuf_tensor("bmt", [4, 128], F32))
                mrow = e(nc.sbuf_tensor("bmr", [1, 2 * 512], F32))
                mbc_sb = e(nc.sbuf_tensor("bmb", [128, 2 * 512], F32))
                rcp = e(nc.sbuf_tensor("brc", [1, 2 * 512], F32))
                f_sb = e(nc.sbuf_tensor("bf", [128, 2 * 512], F32))
                y1_sb = e(nc.sbuf_tensor("by1", [128, 2 * 512], F32))
                y2t = e(nc.sbuf_tensor("by2", [128, 512], F32))
                ystg = e(nc.sbuf_tensor("bys", [128, 2 * 512], F32))
                ps_s = e(nc.psum_tensor("bs0", [128, 1024], F32))
                ps_t = [e(nc.psum_tensor(f"bt{i}", [128, 512], F32))
                        for i in range(2)]
                ps_rs = e(nc.psum_tensor("brs", [1, 512], F32))
                ps_o = [e(nc.psum_tensor(f"bo{i}", [128, 512], F32))
                        for i in range(2)]
                ps_ms = e(nc.psum_tensor("bms", [128, 512], F32))
                ctx.wall_stores(SY)
                ckb = ctx.chan("ckb")
                for hs in range(2):
                    for a, b in ((0, T // 2), (T // 2, T)):
                        ctx.inc(SY.dma_start(
                            out=k_sb[:, hs * T + a:hs * T + b],
                            in_=bass.AP(kb_s, hs * 128 * T + a,
                                        [[T, 128], [1, b - a]])), ckb, 16)
                for a, b in _split4(64):
                    ap = bass.AP(vb_s, a * 128 * 256,
                                 [[256, 128], [128 * 256, b - a], [1, 256]])
                    ctx.inc(SY.dma_start(
                        out=v_sb[:, a * 256:b * 256].rearrange(
                            "p (u d) -> p u d", u=b - a), in_=ap), ckb, 16)
                q_free = {}
                qld = {}
                s_free0 = 0
                exp_done = [0, 0]
                pv_done = [0, 0]
                mrow_war = [0, 0]
                mrowb_war = [0, 0]
                mbc_done = [0, 0]
                mbc_war = [0, 0]
                rcp_done = [0, 0]
                fcp_done = [0, 0]
                mtcp = 0
                fms_war = 0
                mfin_war = 0
                o_free = 0
                y1_done = 0
                yst_free = [0, 0]

                def load_q(Q, hs):
                    qp = 2 * (Q % 2) + hs
                    ctx.w(SY, "pe", q_free.get((Q % 2, hs), 0))
                    ctx.inc(SY.dma_start(
                        out=q_sb[:, qp * 512:(qp + 1) * 512],
                        in_=bass.AP(qb_s, hs * 128 * T + 512 * Q,
                                    [[T, 128], [1, 512]])),
                        ctx.chan(f"cbq{qp}"), 16)

                def ph1(Q, hs):
                    nonlocal mfin_war, mtcp, s_free0, fms_war
                    qp = 2 * (Q % 2) + hs
                    ctx.wc(TE, "ckb")
                    ctx.wc(TE, f"cbq{qp}")
                    npair = (Q + 2) // 2
                    for s4 in range(4):
                        for pj in range(npair):
                            j0 = 2 * pj
                            wid = min(2, Q + 1 - j0)
                            ctx.w(TE, "ve", s_free0)
                            last = None
                            for dj in range(wid):
                                j = j0 + dj
                                last = TE.matmul(
                                    ps_s[:, dj * 512:(dj + 1) * 512],
                                    q_sb[:, qp * 512 + 128 * s4:
                                         qp * 512 + 128 * (s4 + 1)],
                                    k_sb[:, hs * T + 512 * j:hs * T + 512 * (j + 1)],
                                    start=True, stop=True)
                            mm = ctx.inc(last, "pe", 1)
                            ctx.w(VE, "pe", mm)
                            if j0 + wid - 1 == Q:
                                VE.tensor_add(
                                    ps_s[:, (wid - 1) * 512:wid * 512],
                                    ps_s[:, (wid - 1) * 512:wid * 512],
                                    wmqk_sb[:, 384 - 128 * s4:896 - 128 * s4])
                            s_free0 = ctx.last(VE.reduce_max(
                                mx_sb[:, s4 * 9 + pj:s4 * 9 + pj + 1],
                                ps_s[:, 0:wid * 512], AX.X), "ve", 1)
                    ctx.w(VE, "pe", mfin_war)
                    last = None
                    for s4 in range(4):
                        last = VE.reduce_max(mfin[:, s4:s4 + 1],
                                             mx_sb[:, s4 * 9:s4 * 9 + npair],
                                             AX.X, negate=True)
                    mxd = ctx.last(last, "ve", 1)
                    ctx.w(TE, "ve", mxd)
                    ctx.w(TE, "se", max(mtcp, fms_war))
                    mfin_war = ctx.inc(TE.transpose(ps_ms[0:4, 0:128], mfin[:],
                                                    idf_sb[:]), "pe", 1)
                    ctx.w(SC, "pe", mfin_war)
                    mtcp = ctx.last(SC.copy(mt_sb[:], ps_ms[0:4, 0:128]), "se", 1)
                    ctx.w(SY, "se", mtcp)
                    cmb = ctx.chan(f"cmb{hs}", store=True)
                    cmr = ctx.chan(f"cmr{hs}")
                    ctx.inc(SY.dma_start(
                        out=bass.AP(mb_d, (hs * NQC + Q) * 512,
                                    [[128, 4], [1, 128]]),
                        in_=mt_sb[:]), cmb, 16)
                    ctx.wc(SY, cmb)
                    ctx.w(SY, "se", mrow_war[hs])
                    ctx.inc(SY.dma_start(
                        out=mrow[:, hs * 512:(hs + 1) * 512],
                        in_=bass.AP(mb_d, (hs * NQC + Q) * 512,
                                    [[512, 1], [1, 512]])), cmr, 16)
                    ctx.wc(TE, cmr)
                    ctx.w(TE, "se", max(mtcp, fms_war))
                    ctx.w(TE, "ve", mbc_war[hs])
                    fb = ctx.inc(TE.matmul(ps_ms[:], onesf[:],
                                           mrow[:, hs * 512:(hs + 1) * 512],
                                           start=True, stop=True), "pe", 1)
                    ctx.w(SC, "pe", fb)
                    ctx.w(SC, "pe", mrowb_war[hs])
                    fms_war = ctx.last(SC.copy(mbc_sb[:, hs * 512:(hs + 1) * 512],
                                               ps_ms[:]), "se", 1)
                    mrow_war[hs] = fms_war
                    mbc_done[hs] = fms_war

                def ph2(Q, hs):
                    nonlocal o_free, y1_done, fms_war
                    qp = 2 * (Q % 2) + hs
                    ctx.w(TE, "se", mbc_done[hs])
                    ctx.w(TE, "ve", rcp_done[hs])
                    ctx.w(TE, "ve", o_free)
                    nsub = 4 * (Q + 1)
                    mm_of = [0] * nsub
                    maskv_of = [0] * nsub

                    def emit_ST(u):
                        j, s4 = u // 4, u % 4
                        sl = u % 2
                        ctx.w(TE, "se", exp_done[sl])
                        kt = k_sb[:, hs * T + 512 * j + 128 * s4:
                                  hs * T + 512 * j + 128 * (s4 + 1)]
                        mm_of[u] = ctx.inc(TE.matmul(
                            ps_t[sl][:], kt, q_sb[:, qp * 512:(qp + 1) * 512],
                            start=True, stop=True), "pe", 1)
                        ctx.w(VE, "pe", mm_of[u])
                        ctx.w(VE, "se", mbc_done[hs])
                        VE.tensor_add(ps_t[sl][:], ps_t[sl][:],
                                      mbc_sb[:, hs * 512:(hs + 1) * 512])
                        if j == Q:
                            VE.tensor_add(ps_t[sl][:], ps_t[sl][:],
                                          wmkq_sb[:, 384 - 128 * s4:896 - 128 * s4])
                        maskv_of[u] = ctx.cnt["ve"]

                    def emit_tail(u):
                        j, s4 = u // 4, u % 4
                        sl = u % 2
                        ctx.w(SC, "ve", maskv_of[u])
                        ctx.w(SC, "pe", pv_done[sl])
                        exp_done[sl] = ctx.last(SC.activation(
                            pt_sb[:, sl * 512:(sl + 1) * 512], ps_t[sl][:],
                            AF.Exp), "se", 1)
                        ctx.w(TE, "se", exp_done[sl])
                        TE.matmul(ps_rs[:], onesb[:],
                                  pt_sb[:, sl * 512:(sl + 1) * 512],
                                  start=(u == 0), stop=(u == nsub - 1))
                        ut = 4 * j + s4
                        last = None
                        for dvc in range(2):
                            vt = v_sb[:, ut * 256 + dvc * 128:
                                      ut * 256 + (dvc + 1) * 128]
                            last = TE.matmul(
                                ps_o[dvc][:], vt,
                                pt_sb[:, sl * 512:(sl + 1) * 512],
                                start=(u == 0), stop=(u == nsub - 1))
                        pv_done[sl] = ctx.inc(last, "pe", 1)

                    emit_ST(0)
                    for u in range(nsub):
                        if u + 1 < nsub:
                            emit_ST(u + 1)
                        emit_tail(u)
                    mrowb_war[hs] = pv_done[(nsub - 1) % 2]
                    mbc_war[hs] = ctx.cnt["ve"]
                    q_free[(Q % 2, hs)] = pv_done[(nsub - 1) % 2]
                    ctx.w(VE, "pe", pv_done[(nsub - 1) % 2])
                    rcp_done[hs] = ctx.last(VE.reciprocal(
                        rcp[:, hs * 512:(hs + 1) * 512], ps_rs[:]), "ve", 1)
                    ctx.w(TE, "ve", rcp_done[hs])
                    ctx.w(TE, "se", max(fcp_done[hs], mtcp))
                    fb = ctx.inc(TE.matmul(ps_ms[:], onesf[:],
                                           rcp[:, hs * 512:(hs + 1) * 512],
                                           start=True, stop=True), "pe", 1)
                    ctx.w(SC, "pe", fb)
                    fms_war = ctx.last(SC.copy(f_sb[:, hs * 512:(hs + 1) * 512],
                                              ps_ms[:]), "se", 1)
                    fcp_done[hs] = fms_war
                    ctx.w(VE, "se", fcp_done[hs])
                    if hs == 0:
                        last = None
                        for dvc in range(2):
                            last = VE.tensor_mul(
                                y1_sb[:, dvc * 512:(dvc + 1) * 512],
                                ps_o[dvc][:], f_sb[:, 0:512])
                        y1_done = ctx.last(last, "ve", 1)
                        o_free = y1_done
                    else:
                        for dvc in range(2):
                            ctx.wc(VE, ctx.chan(f"cby{dvc}", store=True))
                            VE.tensor_mul(y2t[:], ps_o[dvc][:],
                                          f_sb[:, 512:1024])
                            yc = ctx.last(VE.scalar_tensor_tensor(
                                ystg[:, dvc * 512:(dvc + 1) * 512], y2t[:],
                                lamn_sb[:],
                                y1_sb[:, dvc * 512:(dvc + 1) * 512],
                                ALU.mult, ALU.add), "ve", 1)
                            o_free = yc
                            ctx.w(SY, "ve", yc)
                            ctx.inc(SY.dma_start(
                                out=bass.AP(
                                    a2ab_in,
                                    (256 * (Q // 2) + 128 * dvc) * TPC + 512 * (Q % 2),
                                    [[TPC, 128], [1, 512]]),
                                in_=ystg[:, dvc * 512:(dvc + 1) * 512]),
                                f"cby{dvc}", 16)

                load_q(0, 0)
                load_q(0, 1)
                for Q in range(NQC):
                    if Q + 1 < NQC:
                        load_q(Q + 1, 0)
                        load_q(Q + 1, 1)
                    ph1(Q, 0)
                    ph1(Q, 1)
                    ph2(Q, 0)
                    ph2(Q, 1)
                ctx.wall_stores(GP)
                with allow_ici_all_to_all():
                    ctx.inc(GP.collective_compute(
                        "AllToAll", ALU.bypass, replica_groups=RG,
                        ins=[a2ab_in.ap().opt()], outs=[a2ab_out.ap().opt()]),
                        "cc", 1)

        def phase_yadd():
            with nc.sbuf_tensor("yt", [128, 2 * TPC], F32) as ytmp:
                ctx.w(SY, "cc")
                cons = [0, 0]
                for fc in range(FC):
                    par = fc % 2
                    cyt = ctx.chan(f"cyt{par}")
                    ctx.w(SY, "ve", cons[par])
                    ctx.inc(SY.dma_start(
                        out=ytmp[:, par * TPC:(par + 1) * TPC],
                        in_=bass.AP(a2ab_out, fc * 128 * TPC,
                                    [[TPC, 128], [1, TPC]])), cyt, 16)
                    ctx.wc(VE, cyt)
                    sl = x_sb[:, fc * TPC:(fc + 1) * TPC]
                    cons[par] = ctx.last(VE.tensor_add(
                        sl, sl, ytmp[:, par * TPC:(par + 1) * TPC]), "ve", 1)

        def phase_out():
            with nc.sbuf_tensor("os", [128, 2 * D], F32) as ostg, \
                 nc.psum_tensor("ot0", [128, 128], F32) as pt0, \
                 nc.psum_tensor("ot1", [128, 128], F32) as pt1:
                ps_tr = [pt0, pt1]
                cp_done = [0, 0]
                st_done = [0, 0]
                for ts in range(8):
                    tpar = ts % 2
                    ctx.wc(SC, ctx.chan(f"cot{tpar}", store=True))
                    for fc in range(FC):
                        rot = fc % 2
                        ctx.w(TE, "se", cp_done[rot])
                        tr = ctx.inc(TE.transpose(
                            ps_tr[rot][:],
                            x_sb[:, fc * TPC + ts * 128:fc * TPC + (ts + 1) * 128],
                            idf_sb[:]), "pe", 1)
                        ctx.w(SC, "pe", tr)
                        cp_done[rot] = ctx.last(SC.copy(
                            ostg[:, tpar * D + fc * 128:tpar * D + (fc + 1) * 128],
                            ps_tr[rot][:]), "se", 1)
                    ctx.w(SY, "se", cp_done[0])
                    ctx.w(SY, "se", cp_done[1])
                    ctx.inc(SY.dma_start(
                        out=bass.AP(out_ext.tensor, ts * 128 * D,
                                    [[D, 128], [1, D]]),
                        in_=ostg[:, tpar * D:(tpar + 1) * D]),
                        f"cot{tpar}", 16)
                ctx.wall_stores(SY)

        def dump_d(name, dram, nelem):
            if name not in dbg:
                return
            ctx.barrier()
            ctx.wall_stores(GP)
            ctx.w(GP, "cc")
            ctx.inc(GP.dma_start(
                out=bass.AP(dbg[name].tensor, 0, [[1, nelem]]),
                in_=bass.AP(dram, 0, [[1, nelem]])), "lw", 16)
            ctx.w(GP, "lw")

        def dump(name, src):
            if name not in dbg:
                return
            ctx.barrier()
            for fc in range(FC):
                ap = bass.AP(dbg[name].tensor, fc * 128 * TPC,
                             [[TPC, 128], [1, TPC]])
                ctx.inc(GP.dma_start(out=ap,
                                     in_=src[:, fc * TPC:(fc + 1) * TPC]),
                        "lw", 16)
            ctx.w(GP, "lw")

        phases = [
            ("p0", phase0, ("x",)),
            ("ln1", lambda: phase_ln("l1", to_ag=True, gates=True), ("xln",)),
            ("qkva", lambda: phase_qkv("qa", wa_in, attn_b=False), ("qkva",)),
            ("attna", phase_attn_a, ()),
            ("outproj", phase_outproj, ("x",)),
            ("ln2", lambda: phase_ln("l2"), ("xln",)),
            ("ffn1", lambda: phase_ffn("f1"), ("x",)),
            ("ln3", lambda: phase_ln("l3", to_ag=True), ("xln",)),
            ("qkvb", lambda: phase_qkv("qb", wb_in, attn_b=True), ("qkvb",)),
            ("attnb", phase_attn_b, ("attnb",)),
            ("yadd", phase_yadd, ("x",)),
            ("ln4", lambda: phase_ln("l4"), ("xln",)),
            ("ffn2", lambda: phase_ffn("f2"), ("x",)),
            ("out", phase_out, ()),
        ]

        @block.sync
        def _(sync):
            for i, (name, fn, dumps) in enumerate(phases):
                if debug_upto is not None and i > debug_upto:
                    break
                fn()
                ctx.barrier()
                for d in dumps:
                    if d == "qkva":
                        dump_d("qa", qa_s, 256 * T)
                        dump_d("ka", ka_s, 256 * T)
                        dump_d("va", va_s, 2 * T * 128)
                        dump_d("g", g_out, 16 * TPC)
                    elif d == "qkvb":
                        dump_d("qb", qb_s, 256 * T)
                        dump_d("kb", kb_s, 256 * T)
                        dump_d("vb", vb_s, T * 256)
                    elif d == "attnb":
                        dump_d("ob", a2ab_out, D * TPC)
                    else:
                        dump(d, {"x": x_sb, "xln": xln_sb}[d])
            ctx.w(SY, "st")

    return nc


def _host_tables():
    inv = 10000.0 ** (-np.arange(0, HD, 2, dtype=np.float64) / HD)
    pos = np.arange(T, dtype=np.float64)
    fr = pos[:, None] * inv[None, :]
    cos = np.cos(fr).T.astype(np.float32)
    sin = np.sin(fr).T.astype(np.float32)
    cos2 = np.ascontiguousarray(np.concatenate([cos, cos], axis=0))
    sin2 = np.ascontiguousarray(np.concatenate([sin, sin], axis=0))
    logp = np.ascontiguousarray(
        np.log(np.arange(1, T + 1, dtype=np.float64)).astype(np.float32)[None, :])
    identf = np.eye(128, dtype=np.float32)
    pim = np.zeros((128, 128), dtype=np.float32)
    for j in range(64):
        pim[64 + j, j] = 1.0
        pim[j, j + 64] = -1.0
    ii = np.arange(128)[:, None]
    ww = np.arange(896)[None, :]
    wmqk = np.where(ww > ii + 384, NEG_BIG, 0.0).astype(np.float32)
    wmkq = np.where(ww < ii + 384, NEG_BIG, 0.0).astype(np.float32)
    return cos2, sin2, logp, identf, pim, wmqk, wmkq


def kernel(L0_attn_a_x, L0_attn_a_w_qkv, L0_attn_a_w_gates, L0_attn_a_w_out,
           L0_attn_b_Wq, L0_attn_b_Wk, L0_attn_b_Wv,
           L0_attn_b_lambda_q1, L0_attn_b_lambda_k1,
           L0_attn_b_lambda_q2, L0_attn_b_lambda_k2,
           L0_attn_b_softmax_scaler, L0_ffn_w_gate_up, L0_ffn_w_down):
    x = np.asarray(L0_attn_a_x, dtype=np.float32).reshape(T, D)
    xT = np.ascontiguousarray(x.T)
    wqkv = np.asarray(L0_attn_a_w_qkv, dtype=np.float32).reshape(D, 3, NH, HD)
    perm0 = [h for p_ in range(8) for h in (p_, p_ + 8)]
    w_gates = np.ascontiguousarray(
        np.asarray(L0_attn_a_w_gates, dtype=np.float32)[:, perm0])
    w_out = np.asarray(L0_attn_a_w_out, dtype=np.float32)
    wo_t = np.ascontiguousarray(
        w_out.reshape(16, 128, 16, 128).transpose(2, 1, 0, 3))
    wgu = np.asarray(L0_ffn_w_gate_up, dtype=np.float32)
    w1_t = np.ascontiguousarray(
        wgu.reshape(128, 128, 16, 128).transpose(0, 3, 2, 1))
    wdn = np.asarray(L0_ffn_w_down, dtype=np.float32)
    w2_t = np.ascontiguousarray(
        wdn.reshape(16, 128, 2, 32, 128).transpose(0, 2, 4, 3, 1))
    Wq = np.asarray(L0_attn_b_Wq, dtype=np.float32).reshape(D, NH, HD)
    Wk = np.asarray(L0_attn_b_Wk, dtype=np.float32).reshape(D, NH, HD)
    Wv = np.asarray(L0_attn_b_Wv, dtype=np.float32).reshape(D, 8, 256)
    scaler = np.asarray(L0_attn_b_softmax_scaler, dtype=np.float32).reshape(NH)
    lam = np.ascontiguousarray(np.concatenate([
        np.asarray(L0_attn_b_lambda_q1, dtype=np.float32),
        np.asarray(L0_attn_b_lambda_k1, dtype=np.float32),
        np.asarray(L0_attn_b_lambda_q2, dtype=np.float32),
        np.asarray(L0_attn_b_lambda_k2, dtype=np.float32)])[None, :])
    cos2, sin2, logp, identf, pim, wmqk, wmkq = _host_tables()

    in_maps = []
    for c in range(NCORE):
        wa = np.ascontiguousarray(np.concatenate(
            [wqkv[:, 0, c], wqkv[:, 0, c + 8],
             wqkv[:, 1, c], wqkv[:, 1, c + 8],
             wqkv[:, 2, c], wqkv[:, 2, c + 8]], axis=1))
        wb = np.ascontiguousarray(np.concatenate(
            [Wq[:, c], Wq[:, c + 8], Wk[:, c], Wk[:, c + 8], Wv[:, c]],
            axis=1))
        in_maps.append({
            "xT": np.ascontiguousarray(xT[:, c * TPC:(c + 1) * TPC]),
            "wa": wa, "wg": w_gates, "wo": wo_t, "wb": wb,
            "w1": w1_t, "w2": w2_t,
            "cos2": cos2, "sin2": sin2, "logpos": logp,
            "scal": np.ascontiguousarray(scaler[[c, c + 8]][None, :]),
            "lam": lam, "identf": identf, "pimat": pim,
            "wmqk": wmqk, "wmkq": wmkq,
        })

    nc = build(debug_upto=DEBUG_UPTO, debug_dumps=DEBUG_DUMPS)
    res = run_bass_kernel_spmd(nc, in_maps, core_ids=list(range(NCORE)))
    if DEBUG_DUMPS:
        kernel.debug_results = res
    out = np.concatenate([res.results[c]["out"] for c in range(NCORE)], axis=0)
    return out.reshape(1, T, D)


# version 04 (final)
# speedup vs baseline: 1.0356x; 1.0156x over previous
import math
import numpy as np
from contextlib import ExitStack

import concourse.bass as bass
import concourse.mybir as mybir
from concourse.bass_utils import run_bass_kernel_spmd
from concourse.aot_env import allow_ici_all_to_all

F32 = mybir.dt.float32
BF16 = mybir.dt.bfloat16
AF = mybir.ActivationFunctionType
ALU = mybir.AluOpType
AX = mybir.AxisListType

NCORE = 8
T = 8192
D = 2048
TPC = T // NCORE
FC = D // 128
NQC = T // 512
NH = 16
HD = 128
SOFTCAP = 50.0
ISQHD = 1.0 / math.sqrt(HD)
LN_EPS = 1e-5
RMS_EPS = float(np.finfo(np.float32).eps)
LAMBDA_INIT = 0.2
NEG_BIG = -1e30

DEBUG_UPTO = None
DEBUG_DUMPS = ()
DEBUG_FFN_GU_ONLY = False
DEBUG_FFN_JJ = 32
DEBUG_FFN_NOSILU = False
DEBUG_FFN_STAGE = 9
DEBUG_OP_STAGE = 9


class SerialEngine:

    _PASS = ("wait_ge", "nop", "maybe_nop_then_inc")

    def __init__(self, ctx, eng, sem):
        self._ctx, self._eng, self._sem = ctx, eng, sem

    def __getattr__(self, name):
        fn = getattr(self._eng, name)
        if name in SerialEngine._PASS or name.startswith("_"):
            return fn

        def wrapped(*a, **k):
            self._ctx.w(self._eng, self._sem)
            instr = fn(*a, **k)
            self._ctx.inc(instr, self._sem, 1)
            return instr
        return wrapped


class Ctx:
    def __init__(self, nc, stack):
        self.nc = nc
        self.stack = stack
        self.sems = {}
        self.cnt = {}
        self.store_chans = set()
        for name in ("ls", "lw", "st", "cc", "pe", "se", "ve", "gp",
                     "b0", "b1", "b2", "b3", "b4"):
            self.sems[name] = stack.enter_context(nc.semaphore(name))
            self.cnt[name] = 0
        self.round = 0

    def chan(self, name, store=False):
        if name not in self.sems:
            self.sems[name] = self.stack.enter_context(self.nc.semaphore(name))
            self.cnt[name] = 0
        if store:
            self.store_chans.add(name)
        return name

    def wc(self, eng, name):
        if self.cnt.get(name, 0) > 0:
            eng.wait_ge(self.sems[name], self.cnt[name])

    def wall_stores(self, eng):
        self.wc(eng, "st")
        for c in sorted(self.store_chans):
            self.wc(eng, c)

    def inc(self, instr, name, k=1):
        instr.then_inc(self.sems[name], k)
        self.cnt[name] += k
        return self.cnt[name]

    def last(self, instr, name, k=1):
        return self.cnt[name]

    def w(self, eng, name, v=None):
        val = self.cnt[name] if v is None else v
        if val > 0:
            eng.wait_ge(self.sems[name], val)

    def barrier(self):
        nc = self.nc
        self.round += 1
        r = self.round
        for name in list(self.sems):
            if name.startswith("b"):
                continue
            if name in ("pe", "se", "ve", "gp"):
                continue
            self.wc(nc.sync, name)
        engs = [(nc.sync, "b0", None), (nc.gpsimd, "b1", "gp"),
                (nc.tensor, "b2", "pe"), (nc.scalar, "b3", "se"),
                (nc.vector, "b4", "ve")]
        for eng, s, own in engs:
            if own is not None:
                self.wc(eng, own)
            self.inc(eng.nop(), s, 1)
        for eng, s, _ in engs:
            for _, s2, _ in engs:
                if s2 != s:
                    eng.wait_ge(self.sems[s2], r)


def _split4(n, k=4):
    k = min(k, n)
    base, rem = n // k, n % k
    out, a = [], 0
    for i in range(k):
        b = a + base + (1 if i < rem else 0)
        out.append((a, b))
        a = b
    return out


def build(debug_upto=None, debug_dumps=()):
    nc = bass.Bass("TRN2", target_bir_lowering=False)

    xT_in = nc.dram_tensor("xT", [D, TPC], F32, kind="ExternalInput").ap()
    wa_in = nc.dram_tensor("wa", [D, 768], F32, kind="ExternalInput").ap()
    wg_in = nc.dram_tensor("wg", [D, 16], F32, kind="ExternalInput").ap()
    wo_in = nc.dram_tensor("wo", [16, 128, 16, 128], F32, kind="ExternalInput").ap()
    wb_in = nc.dram_tensor("wb", [D, 768], F32, kind="ExternalInput").ap()
    w1_in = nc.dram_tensor("w1", [128, 128, 16, 128], F32, kind="ExternalInput").ap()
    w2_in = nc.dram_tensor("w2", [16, 2, 128, 32, 128], F32, kind="ExternalInput").ap()
    cos_in = nc.dram_tensor("cos2", [128, T], F32, kind="ExternalInput").ap()
    sin_in = nc.dram_tensor("sin2", [128, T], F32, kind="ExternalInput").ap()
    logp_in = nc.dram_tensor("logpos", [1, T], F32, kind="ExternalInput").ap()
    scal_in = nc.dram_tensor("scal", [1, 2], F32, kind="ExternalInput").ap()
    lam_in = nc.dram_tensor("lam", [1, 256], F32, kind="ExternalInput").ap()
    idf_in = nc.dram_tensor("identf", [128, 128], F32, kind="ExternalInput").ap()
    pi_in = nc.dram_tensor("pimat", [128, 128], F32, kind="ExternalInput").ap()
    wmqk_in = nc.dram_tensor("wmqk", [128, 896], F32, kind="ExternalInput").ap()
    wmkq_in = nc.dram_tensor("wmkq", [128, 896], F32, kind="ExternalInput").ap()
    out_ext = nc.dram_tensor("out", [TPC, D], F32, kind="ExternalOutput").ap()

    dbg = {}
    for name in debug_dumps:
        shp = {"x": [D, TPC], "xln": [D, TPC], "qa": [256, T], "ka": [256, T],
               "va": [2 * T, 128], "qb": [256, T], "kb": [256, T],
               "vb": [T, 256], "g": [16, TPC], "oa": [D, TPC],
               "ob": [D, TPC], "aain": [D, TPC], "ai_k": [128, 512], "ai_q": [128, 512],
               "ai_pt": [128, 512], "ai_rs": [1, 512], "ai_o": [128, 512],
               "ai_f": [128, 512]}[name]
        dbg[name] = nc.dram_tensor("dbg_" + name, shp, F32,
                                   kind="ExternalOutput").ap()

    ag_in = nc.dram_tensor("ag_in", [D, TPC], BF16)
    ag_out = nc.dram_tensor("ag_out", [NCORE * D, TPC], BF16,
                            addr_space="Shared")
    g_in = nc.dram_tensor("g_in", [16, TPC], F32)
    g_out = nc.dram_tensor("g_out", [16, TPC], F32)
    a2aa_in = [nc.dram_tensor(f"a2aa_in{i}", [NCORE * 128, TPC], BF16)
               for i in range(2)]
    a2aa_out = [nc.dram_tensor(f"a2aa_out{i}", [NCORE * 128, TPC], BF16)
                for i in range(2)]
    a2ab_in = nc.dram_tensor("a2ab_in", [D, TPC], F32)
    a2ab_out = nc.dram_tensor("a2ab_out", [D, TPC], F32)
    qa_s = nc.dram_tensor("qa_s", [256, T], BF16)
    ka_s = nc.dram_tensor("ka_s", [256, T], BF16)
    va_s = nc.dram_tensor("va_s", [2, T, 128], BF16)
    qb_s = nc.dram_tensor("qb_s", [256, T], BF16)
    kb_s = nc.dram_tensor("kb_s", [256, T], BF16)
    vb_s = nc.dram_tensor("vb_s", [T, 256], BF16)
    mb_d = nc.dram_tensor("mb_d", [2, NQC, 512], F32)

    RG = [list(range(NCORE))]

    with ExitStack() as st_:
        E = st_.enter_context
        ctx = Ctx(nc, st_)
        SY, GP, TE = nc.sync, nc.gpsimd, nc.tensor
        SC = SerialEngine(ctx, nc.scalar, "se")
        VE = SerialEngine(ctx, nc.vector, "ve")

        x_sb = E(nc.sbuf_tensor("x_sb", [128, FC * TPC], F32))
        xln_sb = E(nc.sbuf_tensor("xln_sb", [128, FC * TPC], BF16))
        idf_sb = E(nc.sbuf_tensor("idf_sb", [128, 128], F32))
        idb_sb = E(nc.sbuf_tensor("idb_sb", [128, 128], BF16))
        pi_sb = E(nc.sbuf_tensor("pi_sb", [128, 128], BF16))
        wmqk_sb = E(nc.sbuf_tensor("wmqk_sb", [128, 896], F32))
        wmkq_sb = E(nc.sbuf_tensor("wmkq_sb", [128, 896], F32))
        onesb = E(nc.sbuf_tensor("onesb", [128, 1], BF16))
        onesf = E(nc.sbuf_tensor("onesf", [1, 128], F32))
        onesb2 = E(nc.sbuf_tensor("onesb2", [1, 128], BF16))
        neg20 = E(nc.sbuf_tensor("neg20", [128, 1], F32))
        epsln = E(nc.sbuf_tensor("epsln", [1, 1], F32))
        epsrms = E(nc.sbuf_tensor("epsrms", [1, 1], F32))
        scal_sb = E(nc.sbuf_tensor("scal_sb", [1, 2], F32))
        lam_sb = E(nc.sbuf_tensor("lam_sb", [1, 256], F32))
        lamn_sb = E(nc.sbuf_tensor("lamn_sb", [128, 1], F32))
        wg_sb = E(nc.sbuf_tensor("wg_sb", [128, FC * 16], BF16))
        block = E(nc.Block())


        def phase0():
            for a, b in _split4(FC):
                ap = bass.AP(xT_in.tensor, a * 128 * TPC,
                             [[TPC, 128], [128 * TPC, b - a], [1, TPC]])
                ctx.inc(SY.dma_start(
                    out=x_sb[:, a * TPC:b * TPC].rearrange("p (c t) -> p c t",
                                                           c=b - a),
                    in_=ap), "ls", 16)
            ctx.inc(SY.dma_start(out=idf_sb[:], in_=idf_in), "ls", 16)
            ctx.inc(SY.dma_start(out=wmqk_sb[:], in_=wmqk_in), "ls", 16)
            ctx.inc(SY.dma_start(out=wmkq_sb[:], in_=wmkq_in), "ls", 16)
            ctx.inc(SY.dma_start(out=scal_sb[:], in_=scal_in), "ls", 16)
            ctx.inc(SY.dma_start(out=lam_sb[:], in_=lam_in), "ls", 16)
            ctx.inc(GP.dma_start(out=idb_sb[:], in_=idf_in), "lw", 16)
            ctx.inc(GP.dma_start(out=pi_sb[:], in_=pi_in), "lw", 16)
            ap = bass.AP(wg_in.tensor, 0, [[16, 128], [128 * 16, FC], [1, 16]])
            ctx.inc(GP.dma_start(
                out=wg_sb[:].rearrange("p (c m) -> p c m", c=FC), in_=ap),
                "lw", 16)
            GP.memset(onesb[:], 1.0)
            GP.memset(onesf[:], 1.0)
            GP.memset(onesb2[:], 1.0)
            GP.memset(neg20[:], -20.0)
            GP.memset(epsln[:], LN_EPS)
            ctx.inc(GP.memset(epsrms[:], RMS_EPS), "gp", 1)
            ctx.w(VE, "ls")
            ctx.w(VE, "gp")
            with nc.sbuf_tensor("lrow", [1, 64], F32) as lrow, \
                 nc.sbuf_tensor("lsc", [1, 4], F32) as lsc, \
                 nc.psum_tensor("ps_l", [128, 1], F32) as ps_l:
                VE.tensor_mul(lrow[:], lam_sb[:, 0:64], lam_sb[:, 64:128])
                VE.reduce_sum(lsc[:, 0:1], lrow[:], AX.X)
                VE.tensor_mul(lrow[:], lam_sb[:, 128:192], lam_sb[:, 192:256])
                ctx.last(VE.reduce_sum(lsc[:, 1:2], lrow[:], AX.X), "ve", 1)
                ctx.w(SC, "ve")
                SC.activation(lsc[:, 2:3], lsc[:, 0:1], AF.Exp)
                ctx.last(SC.activation(lsc[:, 3:4], lsc[:, 1:2], AF.Exp), "se", 1)
                ctx.w(VE, "se")
                VE.tensor_sub(lsc[:, 0:1], lsc[:, 3:4], lsc[:, 2:3])
                ctx.last(VE.tensor_scalar_add(lsc[:, 1:2], lsc[:, 0:1],
                                             -LAMBDA_INIT), "ve", 1)
                ctx.w(TE, "ve")
                ctx.inc(TE.matmul(ps_l[:], onesf[:], lsc[:, 1:2],
                                  start=True, stop=True), "pe", 1)
                ctx.w(SC, "pe")
                ctx.last(SC.copy(lamn_sb[:], ps_l[:]), "se", 1)

        def phase_ln(tag, to_ag=False, gates=False):
            with ExitStack() as ps_:
                e = ps_.enter_context
                xb = e(nc.sbuf_tensor(f"xb{tag}", [128, 2 * TPC], BF16))
                sq = e(nc.sbuf_tensor(f"sq{tag}", [128, 2 * TPC], BF16))
                rows = e(nc.sbuf_tensor(f"ro{tag}", [1, 5 * TPC], F32))
                tsc = e(nc.sbuf_tensor(f"tc{tag}", [128, 2 * TPC], F32))
                m_row = rows[:, 0:TPC]
                msq = rows[:, TPC:2 * TPC]
                varr = rows[:, 2 * TPC:3 * TPC]
                lnt = rows[:, 3 * TPC:4 * TPC]
                rstd = rows[:, 4 * TPC:5 * TPC]
                with nc.psum_tensor(f"p1{tag}", [1, TPC], F32) as ps1, \
                     nc.psum_tensor(f"p2{tag}", [1, TPC], F32) as ps2:
                    pe_done = [0] * FC
                    for fc in range(FC):
                        par = fc % 2
                        if fc >= 2:
                            ctx.w(SC, "pe", pe_done[fc - 2])
                        xs = x_sb[:, fc * TPC:(fc + 1) * TPC]
                        cpd = ctx.last(SC.copy(
                            xb[:, par * TPC:(par + 1) * TPC], xs), "se", 1)
                        ctx.w(SC, "se", cpd)
                        ctx.last(SC.activation(
                            sq[:, par * TPC:(par + 1) * TPC],
                            xb[:, par * TPC:(par + 1) * TPC], AF.Square),
                            "se", 1)
                        ctx.w(TE, "se")
                        last = None
                        for h in range(2):
                            TE.matmul(ps1[:, h * 512:(h + 1) * 512], onesb[:],
                                      xb[:, par * TPC + h * 512:
                                         par * TPC + (h + 1) * 512],
                                      start=(fc == 0), stop=(fc == FC - 1))
                            last = TE.matmul(
                                ps2[:, h * 512:(h + 1) * 512], onesb[:],
                                sq[:, par * TPC + h * 512:
                                   par * TPC + (h + 1) * 512],
                                start=(fc == 0), stop=(fc == FC - 1))
                        pe_done[fc] = ctx.inc(last, "pe", 1)
                    ctx.w(VE, "pe")
                    VE.tensor_scalar_mul(m_row, ps1[:], 1.0 / D)
                    VE.tensor_mul(msq, m_row, m_row)
                    ctx.last(VE.scalar_tensor_tensor(
                        varr, ps2[:], 1.0 / D, msq, ALU.mult, ALU.subtract),
                        "ve", 1)
                    ctx.w(SC, "ve")
                    lnd = ctx.last(SC.activation(lnt, varr, AF.Ln,
                                                bias=epsln[:]), "se", 1)
                    ctx.w(SC, "se", lnd)
                    ctx.last(SC.activation(rstd, lnt, AF.Exp, scale=-0.5),
                            "se", 1)
                with nc.psum_tensor(f"mb{tag}", [128, TPC], F32) as ps_mb, \
                     nc.psum_tensor(f"rb{tag}", [128, TPC], F32) as ps_rb:
                    ctx.w(TE, "se")
                    last = None
                    for h in range(2):
                        TE.matmul(ps_mb[:, h * 512:(h + 1) * 512], onesf[:],
                                  m_row[:, h * 512:(h + 1) * 512],
                                  start=True, stop=True)
                        last = TE.matmul(ps_rb[:, h * 512:(h + 1) * 512],
                                         onesf[:],
                                         rstd[:, h * 512:(h + 1) * 512],
                                         start=True, stop=True)
                    ctx.inc(last, "pe", 1)
                    ctx.w(VE, "pe")
                    for fc in range(FC):
                        par = fc % 2
                        xs = x_sb[:, fc * TPC:(fc + 1) * TPC]
                        VE.tensor_sub(tsc[:, par * TPC:(par + 1) * TPC], xs,
                                      ps_mb[:])
                        done = ctx.last(VE.tensor_mul(
                            xln_sb[:, fc * TPC:(fc + 1) * TPC],
                            tsc[:, par * TPC:(par + 1) * TPC], ps_rb[:]),
                            "ve", 1)
                        if to_ag:
                            ctx.w(SY, "ve", done)
                            ctx.inc(SY.dma_start(
                                out=ag_in[fc * 128:(fc + 1) * 128, :],
                                in_=xln_sb[:, fc * TPC:(fc + 1) * TPC]),
                                "ls", 16)
                if to_ag:
                    ctx.w(GP, "ls")
                    ctx.inc(GP.collective_compute(
                        "AllGather", ALU.bypass, replica_groups=RG,
                        ins=[ag_in.ap().opt()], outs=[ag_out.ap().opt()]),
                        "cc", 1)
                if gates:
                    with nc.psum_tensor(f"pg{tag}", [16, TPC], F32) as ps_g, \
                         nc.sbuf_tensor(f"gt{tag}", [16, 2 * TPC], F32) as gt:
                        ctx.w(TE, "ve")
                        ctx.w(TE, "lw")
                        last = None
                        for fc in range(FC):
                            for h in range(2):
                                last = TE.matmul(
                                    ps_g[:, h * 512:(h + 1) * 512],
                                    wg_sb[:, fc * 16:(fc + 1) * 16],
                                    xln_sb[:, fc * TPC + h * 512:
                                           fc * TPC + (h + 1) * 512],
                                    start=(fc == 0), stop=(fc == FC - 1))
                        ctx.inc(last, "pe", 1)
                        ctx.w(SC, "pe")
                        ctx.last(SC.activation(gt[:, 0:TPC], ps_g[:], AF.Exp,
                                              scale=-1.0), "se", 1)
                        ctx.w(VE, "se")
                        VE.tensor_scalar_add(gt[:, TPC:2 * TPC], gt[:, 0:TPC],
                                             1.0)
                        ctx.last(VE.reciprocal(gt[:, 0:TPC],
                                              gt[:, TPC:2 * TPC]), "ve", 1)
                        ctx.w(SY, "ve")
                        ctx.inc(SY.dma_start(out=g_in[:, :], in_=gt[:, 0:TPC]),
                                "st", 16)
                        ctx.w(GP, "st")
                        with allow_ici_all_to_all():
                            ctx.inc(GP.collective_compute(
                                "AllToAll", ALU.bypass, replica_groups=RG,
                                ins=[g_in.ap().opt()],
                                outs=[g_out.ap().opt()]), "cc", 1)

        def phase_qkv(tag, w_dram, attn_b=False):
            q_d, k_d = (qb_s, kb_s) if attn_b else (qa_s, ka_s)
            with ExitStack() as ps_:
                e = ps_.enter_context
                wsb = e(nc.sbuf_tensor(f"w{tag}", [128, FC * 768], BF16))
                xg = e(nc.sbuf_tensor(f"xg{tag}", [128, 2 * 8192], BF16))
                if not attn_b:
                    stag = e(nc.sbuf_tensor(f"st{tag}", [128, 4 * 512], BF16))
                vst = e(nc.sbuf_tensor(f"vs{tag}", [128, 2 * 512], BF16))
                vst2 = e(nc.sbuf_tensor(f"v2{tag}", [128, 2 * 512], BF16))
                ps_p = [e(nc.psum_tensor(f"pp{i}{tag}", [128, 512], F32))
                        for i in range(3)]
                ps_vt = e(nc.psum_tensor(f"pv{tag}", [128, 512], BF16))
                if attn_b:
                    sqb = e(nc.sbuf_tensor(f"sb{tag}", [128, 2 * 512], BF16))
                    qf = e(nc.sbuf_tensor(f"qf{tag}", [128, 2 * 512], F32))
                    qsbt = e(nc.sbuf_tensor(f"qs{tag}", [128, 2 * 512], BF16))
                    tco = e(nc.sbuf_tensor(f"tco{tag}", [128, 2 * 512], F32))
                    usi = e(nc.sbuf_tensor(f"us{tag}", [128, 2 * 512], F32))
                    qstg = e(nc.sbuf_tensor(f"qg{tag}", [128, 2 * 512], BF16))
                    cs_sl = e(nc.sbuf_tensor(f"cs{tag}", [128, 2 * 512], F32))
                    sn_sl = e(nc.sbuf_tensor(f"sn{tag}", [128, 2 * 512], F32))
                    lp_sl = e(nc.sbuf_tensor(f"lp{tag}", [1, 2 * 512], F32))
                    rws = e(nc.sbuf_tensor(f"rw{tag}", [1, 2 * 1024], F32))
                    ps_ss = [e(nc.psum_tensor(f"ss{i}{tag}", [1, 512], F32))
                             for i in range(2)]
                    ps_f = e(nc.psum_tensor(f"pf{tag}", [128, 512], F32))
                    ps_rot = e(nc.psum_tensor(f"pr{tag}", [128, 512], F32))
                for a, b in _split4(FC):
                    ap = bass.AP(w_dram.tensor, a * 128 * 768,
                                 [[768, 128], [128 * 768, b - a], [1, 768]])
                    ctx.inc(GP.dma_start(
                        out=wsb[:, a * 768:b * 768].rearrange(
                            "p (c u) -> p c u", c=b - a), in_=ap), "lw", 16)
                wld = ctx.cnt["lw"]
                ctx.w(SY, "cc")
                xg_ld = [0] * 16
                xg_cons = [0, 0]
                csl_ld = [0, 0]
                csl_cons = [0, 0]
                bank_free = [0, 0, 0]
                stag_free = [0] * 4
                vst_free = [0, 0]
                psvt_free = 0
                vst2_free = [0, 0]
                qstg_free = [0, 0]
                ss_free = [0, 0]
                f_free = 0
                rot_free = 0
                frow_free = [0, 0]
                sqb_free = [0, 0]
                qf_free = [0, 0]
                qsbt_free = [0, 0]

                def load_xg(rh):
                    par = rh % 2
                    r, half = rh // 2, rh % 2
                    ch = ctx.chan(f"cxg{par}")
                    ctx.w(SY, "pe", xg_cons[par])
                    for a, b in _split4(FC):
                        ap = bass.AP(ag_out,
                                     (2048 * r + a * 128) * TPC + 512 * half,
                                     [[TPC, 128], [128 * TPC, b - a], [1, 512]])
                        ctx.inc(SY.dma_start(
                            out=xg[:, par * 8192 + a * 512:par * 8192 + b * 512]
                            .rearrange("p (c u) -> p c u", c=b - a), in_=ap),
                            ch, 16)
                    if attn_b:
                        ch2 = ctx.chan(f"ccs{par}")
                        ctx.w(SY, "ve", csl_cons[par])
                        c0 = TPC * r + 512 * half
                        ctx.inc(SY.dma_start(
                            out=cs_sl[:, par * 512:(par + 1) * 512],
                            in_=bass.AP(cos_in.tensor, c0,
                                        [[T, 128], [1, 512]])), ch2, 16)
                        ctx.inc(SY.dma_start(
                            out=sn_sl[:, par * 512:(par + 1) * 512],
                            in_=bass.AP(sin_in.tensor, c0,
                                        [[T, 128], [1, 512]])), ch2, 16)
                        ctx.inc(SY.dma_start(
                            out=lp_sl[:, par * 512:(par + 1) * 512],
                            in_=bass.AP(logp_in.tensor, c0,
                                        [[T, 1], [1, 512]])), ch2, 16)

                pending = []

                def pp_v(rh, m, bk, mm_done):
                    def go():
                        vp = m - 4
                        r, half = rh // 2, rh % 2
                        ctx.w(SC, "pe", mm_done)
                        ctx.w(SC, "pe", vst_free[vp])
                        ev = ctx.last(SC.copy(vst[:, vp * 512:(vp + 1) * 512],
                                             ps_p[bk][:]), "se", 1)
                        bank_free[bk] = ev
                        ctx.w(TE, "se", ev)
                        ctx.w(TE, "se", psvt_free)
                        last = None
                        for u in range(4):
                            last = TE.transpose(
                                ps_vt[:, u * 128:(u + 1) * 128],
                                vst[:, vp * 512 + u * 128:
                                    vp * 512 + (u + 1) * 128], idb_sb[:])
                        tr = ctx.inc(last, "pe", 1)
                        vst_free[vp] = tr
                        ctx.w(SC, "pe", tr)
                        ctx.wc(SC, ctx.chan(f"cv2{vp}", store=True))
                        nonlocal_psvt = ctx.last(SC.copy(
                            vst2[:, vp * 512:(vp + 1) * 512], ps_vt[:]),
                            "se", 1)
                        _set_psvt(nonlocal_psvt)
                        ctx.w(SY, "se", nonlocal_psvt)
                        t0 = TPC * r + 512 * half
                        if attn_b:
                            ap = bass.AP(vb_s, t0 * 256 + 128 * vp,
                                         [[256, 128], [128 * 256, 4], [1, 128]])
                        else:
                            ap = bass.AP(va_s, vp * T * 128 + t0 * 128,
                                         [[128, 128], [128 * 128, 4], [1, 128]])
                        ctx.inc(SY.dma_start(
                            out=ap,
                            in_=vst2[:, vp * 512:(vp + 1) * 512].rearrange(
                                "p (u d) -> p u d", u=4)), f"cv2{vp}", 16)
                    return go

                def _set_psvt(v):
                    nonlocal psvt_free
                    psvt_free = v

                def pp_qk_a(rh, m, bk, mm_done):
                    def go():
                        r, half = rh // 2, rh % 2
                        sl = m % 4
                        ctx.w(SC, "pe", mm_done)
                        ctx.wc(SC, ctx.chan(f"cst{sl}", store=True))
                        if m < 2:
                            ev = ctx.last(SC.activation(
                                stag[:, sl * 512:(sl + 1) * 512], ps_p[bk][:],
                                AF.Copy, scale=ISQHD), "se", 1)
                        else:
                            ev = ctx.last(SC.copy(
                                stag[:, sl * 512:(sl + 1) * 512],
                                ps_p[bk][:]), "se", 1)
                        bank_free[bk] = ev
                        ctx.w(SY, "se", ev)
                        dst = q_d if m < 2 else k_d
                        ctx.inc(SY.dma_start(
                            out=bass.AP(dst, (m % 2) * 128 * T + TPC * r + 512 * half,
                                        [[T, 128], [1, 512]]),
                            in_=stag[:, sl * 512:(sl + 1) * 512]),
                            f"cst{sl}", 16)
                    return go

                def pp_qk_b(rh, m, bk, mm_done):
                    def go():
                        nonlocal f_free, rot_free
                        r, half = rh // 2, rh % 2
                        par = rh % 2
                        rot = m % 2
                        ctx.w(SC, "pe", mm_done)
                        ctx.w(SC, "pe", sqb_free[rot])
                        sqd = ctx.last(SC.activation(
                            sqb[:, rot * 512:(rot + 1) * 512], ps_p[bk][:],
                            AF.Square), "se", 1)
                        ctx.w(TE, "se", sqd)
                        ctx.w(TE, "se", ss_free[rot])
                        ssd = ctx.inc(TE.matmul(ps_ss[rot][:], onesb[:],
                                                sqb[:, rot * 512:(rot + 1) * 512],
                                                start=True, stop=True), "pe", 1)
                        sqb_free[rot] = ssd
                        r0 = (m % 2) * 1024
                        ctx.w(SC, "pe", ssd)
                        ctx.w(SC, "pe", frow_free[m % 2])
                        lnd = ctx.last(SC.activation(
                            rws[:, r0:r0 + 512], ps_ss[rot][:], AF.Ln,
                            scale=1.0 / HD, bias=epsrms[:]), "se", 1)
                        ctx.w(SC, "se", lnd)
                        rsd = ctx.last(SC.activation(
                            rws[:, r0 + 512:r0 + 1024], rws[:, r0:r0 + 512],
                            AF.Exp, scale=-0.5), "se", 1)
                        ss_free[rot] = rsd
                        if m < 2:
                            ctx.w(VE, "se", rsd)
                            ctx.wc(VE, f"ccs{par}")
                            VE.tensor_scalar(rws[:, r0:r0 + 512],
                                             lp_sl[:, par * 512:(par + 1) * 512],
                                             scal_sb[:, m:m + 1], ISQHD,
                                             ALU.mult, ALU.mult)
                            frd = ctx.last(VE.tensor_mul(
                                rws[:, r0 + 512:r0 + 1024],
                                rws[:, r0:r0 + 512],
                                rws[:, r0 + 512:r0 + 1024]), "ve", 1)
                            ctx.w(TE, "ve", frd)
                        else:
                            ctx.w(TE, "se", rsd)
                        frow = rws[:, r0 + 512:r0 + 1024]
                        ctx.w(TE, "ve", f_free)
                        fd = ctx.inc(TE.matmul(ps_f[:], onesf[:], frow,
                                               start=True, stop=True), "pe", 1)
                        frow_free[m % 2] = fd
                        ctx.w(SC, "ve", qf_free[rot])
                        ev = ctx.last(SC.copy(qf[:, rot * 512:(rot + 1) * 512],
                                             ps_p[bk][:]), "se", 1)
                        bank_free[bk] = ev
                        ctx.w(VE, "se", ev)
                        ctx.w(VE, "pe", fd)
                        ctx.w(VE, "pe", qsbt_free[rot])
                        qsd = ctx.last(VE.tensor_mul(
                            qsbt[:, rot * 512:(rot + 1) * 512],
                            qf[:, rot * 512:(rot + 1) * 512], ps_f[:]),
                            "ve", 1)
                        qf_free[rot] = qsd
                        f_free = qsd
                        ctx.w(TE, "ve", qsd)
                        ctx.w(TE, "ve", rot_free)
                        rtd = ctx.inc(TE.matmul(
                            ps_rot[:], pi_sb[:],
                            qsbt[:, rot * 512:(rot + 1) * 512],
                            start=True, stop=True), "pe", 1)
                        ctx.wc(VE, f"ccs{par}")
                        VE.tensor_mul(tco[:, rot * 512:(rot + 1) * 512],
                                      qsbt[:, rot * 512:(rot + 1) * 512],
                                      cs_sl[:, par * 512:(par + 1) * 512])
                        ctx.w(VE, "pe", rtd)
                        VE.tensor_mul(usi[:, rot * 512:(rot + 1) * 512],
                                      ps_rot[:],
                                      sn_sl[:, par * 512:(par + 1) * 512])
                        ctx.wc(VE, ctx.chan(f"cqg{rot}", store=True))
                        qgd = ctx.last(VE.tensor_add(
                            qstg[:, rot * 512:(rot + 1) * 512],
                            tco[:, rot * 512:(rot + 1) * 512],
                            usi[:, rot * 512:(rot + 1) * 512]), "ve", 1)
                        rot_free = qgd
                        qsbt_free[rot] = rtd
                        csl_cons[par] = qgd
                        ctx.w(SY, "ve", qgd)
                        dst = q_d if m < 2 else k_d
                        ctx.inc(SY.dma_start(
                            out=bass.AP(dst, (m % 2) * 128 * T + TPC * r + 512 * half,
                                        [[T, 128], [1, 512]]),
                            in_=qstg[:, rot * 512:(rot + 1) * 512]),
                            f"cqg{rot}", 16)
                    return go

                load_xg(0)
                for rh in range(16):
                    par = rh % 2
                    if rh + 1 < 16:
                        load_xg(rh + 1)
                    first = True
                    for m in range(6):
                        bk = (rh * 6 + m) % 3
                        if first:
                            ctx.w(TE, "lw", wld)
                            ctx.wc(TE, f"cxg{par}")
                            first = False
                        ctx.w(TE, "se", bank_free[bk])
                        last = None
                        for fc in range(FC):
                            last = TE.matmul(
                                ps_p[bk][:],
                                wsb[:, fc * 768 + m * 128:fc * 768 + (m + 1) * 128],
                                xg[:, par * 8192 + fc * 512:
                                   par * 8192 + (fc + 1) * 512],
                                start=(fc == 0), stop=(fc == FC - 1))
                        mm = ctx.inc(last, "pe", 1)
                        if m == 5:
                            xg_cons[par] = mm
                        if m >= 4:
                            pending.append(pp_v(rh, m, bk, mm))
                        elif attn_b:
                            pending.append(pp_qk_b(rh, m, bk, mm))
                        else:
                            pending.append(pp_qk_a(rh, m, bk, mm))
                        while len(pending) > 2:
                            pending.pop(0)()
                    while len(pending) > 2:
                        pending.pop(0)()
                while pending:
                    pending.pop(0)()

        def phase_attn_a():
            with ExitStack() as ps_:
                e = ps_.enter_context
                k_sb = e(nc.sbuf_tensor("ak", [128, T], BF16))
                v_sb = e(nc.sbuf_tensor("av", [128, T], BF16))
                q_sb = e(nc.sbuf_tensor("aq", [128, 2 * 512], BF16))
                tt_sb = e(nc.sbuf_tensor("att", [128, 2 * 512], F32))
                pt_sb = e(nc.sbuf_tensor("apt", [128, 2 * 512], BF16))
                f_sb = e(nc.sbuf_tensor("af", [128, 2 * 512], F32))
                gsl = e(nc.sbuf_tensor("ag_", [1, 2 * 512], F32))
                rcp = e(nc.sbuf_tensor("arc", [1, 512], F32))
                frow = e(nc.sbuf_tensor("afr", [1, 512], F32))
                ostg = e(nc.sbuf_tensor("ao", [128, 2 * 512], BF16))
                ps_st = [e(nc.psum_tensor(f"as{i}", [128, 512], F32))
                         for i in range(2)]
                ps_rs = e(nc.psum_tensor("ars", [1, 512], F32))
                ps_o = e(nc.psum_tensor("apo", [128, 512], F32))
                ps_fb = e(nc.psum_tensor("afb", [128, 512], F32))
                ctx.wall_stores(SY)
                ctx.w(SY, "cc")
                kv_cons = 0
                ost_free = [0, 0]
                for hs in range(2):
                    if hs == 1:
                        ctx.wc(GP, "cao0")
                        ctx.wc(GP, "cao1")
                        with allow_ici_all_to_all():
                            ctx.inc(GP.collective_compute(
                                "AllToAll", ALU.bypass, replica_groups=RG,
                                ins=[a2aa_in[0].ap().opt()],
                                outs=[a2aa_out[0].ap().opt()]), "cc", 1)
                        ctx.w(SY, "pe", kv_cons)
                    ckv = ctx.chan("ckv")
                    for a, b in ((0, T // 2), (T // 2, T)):
                        ctx.inc(SY.dma_start(
                            out=k_sb[:, a:b],
                            in_=bass.AP(ka_s, hs * 128 * T + a,
                                        [[T, 128], [1, b - a]])), ckv, 16)
                    for a, b in ((0, 32), (32, 64)):
                        ap = bass.AP(va_s, hs * T * 128 + a * 128 * 128,
                                     [[128, 128], [128 * 128, b - a], [1, 128]])
                        ctx.inc(SY.dma_start(
                            out=v_sb[:, a * 128:b * 128].rearrange(
                                "p (u d) -> p u d", u=b - a), in_=ap), ckv, 16)
                    q_free = [0, 0]
                    g_free = [0, 0]
                    exp_done = [0, 0]
                    cons_done = [0, 0]
                    rcp_done = 0
                    om_done = 0
                    fc_done = 0
                    fb_prev = [0]
                    qld = [0] * NQC

                    def load_q(Q):
                        qp = Q % 2
                        ch = ctx.chan(f"caq{qp}")
                        ctx.w(SY, "pe", q_free[qp])
                        ctx.inc(SY.dma_start(
                            out=q_sb[:, qp * 512:(qp + 1) * 512],
                            in_=bass.AP(qa_s, hs * 128 * T + 512 * Q,
                                        [[T, 128], [1, 512]])), ch, 16)
                        ctx.w(SY, "ve", g_free[qp])
                        ctx.inc(SY.dma_start(
                            out=gsl[:, qp * 512:(qp + 1) * 512],
                            in_=bass.AP(g_out,
                                        (2 * (Q // 2) + hs) * TPC + 512 * (Q % 2),
                                        [[TPC, 1], [1, 512]])), ch, 16)

                    load_q(0)
                    for Q in range(NQC):
                        qp = Q % 2
                        if Q + 1 < NQC:
                            load_q(Q + 1)
                        ctx.wc(TE, "ckv")
                        ctx.wc(TE, f"caq{qp}")
                        ctx.w(TE, "ve", rcp_done)
                        ctx.w(TE, "ve", om_done)
                        nsub = 4 * (Q + 1)
                        mm_of = [0] * nsub
                        maskv_of = [0] * nsub

                        def emit_S(u):
                            j, s4 = u // 4, u % 4
                            slot = u % 2
                            ctx.w(TE, "se", exp_done[slot])
                            mm_of[u] = ctx.inc(TE.matmul(
                                ps_st[slot][:],
                                k_sb[:, 512 * j + 128 * s4:
                                     512 * j + 128 * (s4 + 1)],
                                q_sb[:, qp * 512:(qp + 1) * 512],
                                start=True, stop=True), "pe", 1)
                            if j == Q:
                                ctx.w(VE, "pe", mm_of[u])
                                maskv_of[u] = ctx.last(VE.tensor_add(
                                    ps_st[slot][:], ps_st[slot][:],
                                    wmkq_sb[:, 384 - 128 * s4:896 - 128 * s4]),
                                    "ve", 1)

                        def emit_tail(u):
                            j, s4 = u // 4, u % 4
                            slot = u % 2
                            if j == Q:
                                ctx.w(SC, "ve", maskv_of[u])
                            else:
                                ctx.w(SC, "pe", mm_of[u])
                            ctx.w(SC, "pe", cons_done[slot])
                            thd = ctx.last(SC.activation(
                                tt_sb[:, slot * 512:(slot + 1) * 512],
                                ps_st[slot][:], AF.Tanh,
                                scale=1.0 / SOFTCAP), "se", 1)
                            ctx.w(SC, "se", thd)
                            exp_done[slot] = ctx.last(SC.activation(
                                pt_sb[:, slot * 512:(slot + 1) * 512],
                                tt_sb[:, slot * 512:(slot + 1) * 512],
                                AF.Exp, scale=SOFTCAP, bias=neg20[:]), "se", 1)
                            ctx.w(TE, "se", exp_done[slot])
                            TE.matmul(ps_rs[:], onesb[:],
                                      pt_sb[:, slot * 512:(slot + 1) * 512],
                                      start=(u == 0), stop=(u == nsub - 1))
                            cons_done[slot] = ctx.inc(TE.matmul(
                                ps_o[:], v_sb[:, u * 128:(u + 1) * 128],
                                pt_sb[:, slot * 512:(slot + 1) * 512],
                                start=(u == 0), stop=(u == nsub - 1)), "pe", 1)

                        emit_S(0)
                        for u in range(nsub):
                            if u + 1 < nsub:
                                emit_S(u + 1)
                            emit_tail(u)
                        q_free[qp] = cons_done[(nsub - 1) % 2]
                        if hs == 0 and Q == NQC - 1:
                            kv_cons = q_free[qp]
                        ctx.w(VE, "pe", cons_done[(nsub - 1) % 2])
                        rcp_done = ctx.last(VE.reciprocal(rcp[:], ps_rs[:]),
                                           "ve", 1)
                        ctx.wc(VE, f"caq{qp}")
                        ctx.w(VE, "pe", fb_prev[0])
                        fr = ctx.last(VE.tensor_mul(
                            frow[:], rcp[:], gsl[:, qp * 512:(qp + 1) * 512]),
                            "ve", 1)
                        g_free[qp] = fr
                        ctx.w(TE, "ve", fr)
                        ctx.w(TE, "se", fc_done)
                        fb = ctx.inc(TE.matmul(ps_fb[:], onesf[:], frow[:],
                                               start=True, stop=True), "pe", 1)
                        fb_prev[0] = fb
                        ctx.w(SC, "pe", fb)
                        fc_done = ctx.last(SC.copy(
                            f_sb[:, qp * 512:(qp + 1) * 512], ps_fb[:]),
                            "se", 1)
                        ctx.w(VE, "se", fc_done)
                        ctx.wc(VE, ctx.chan(f"cao{qp}", store=True))
                        om_done = ctx.last(VE.tensor_mul(
                            ostg[:, qp * 512:(qp + 1) * 512], ps_o[:],
                            f_sb[:, qp * 512:(qp + 1) * 512]), "ve", 1)
                        ctx.w(SY, "ve", om_done)
                        ctx.inc(SY.dma_start(
                            out=bass.AP(
                                a2aa_in[hs],
                                (128 * (Q // 2)) * TPC + 512 * (Q % 2),
                                [[TPC, 128], [1, 512]]),
                            in_=ostg[:, qp * 512:(qp + 1) * 512]),
                            f"cao{qp}", 16)
                        if hs == 0 and Q == 0 and "ai_k" in dbg:
                            ctx.w(GP, "ve", om_done)
                            ctx.inc(GP.dma_start(out=dbg["ai_k"],
                                                 in_=k_sb[:, 0:512]), "lw", 16)
                            ctx.inc(GP.dma_start(out=dbg["ai_q"],
                                                 in_=q_sb[:, 0:512]), "lw", 16)
                            ctx.inc(GP.dma_start(out=dbg["ai_pt"],
                                                 in_=pt_sb[:, 512:1024]), "lw", 16)
                            ctx.inc(GP.dma_start(out=dbg["ai_rs"],
                                                 in_=rcp[:]), "lw", 16)
                            ctx.inc(GP.dma_start(out=dbg["ai_o"],
                                                 in_=ostg[:, 0:512]), "lw", 16)
                            ctx.inc(GP.dma_start(out=dbg["ai_f"],
                                                 in_=f_sb[:, 0:512]), "lw", 16)
                            ctx.w(GP, "lw")


            ctx.wc(GP, "cao0")
            ctx.wc(GP, "cao1")
            with allow_ici_all_to_all():
                ctx.inc(GP.collective_compute(
                    "AllToAll", ALU.bypass, replica_groups=RG,
                    ins=[a2aa_in[1].ap().opt()],
                    outs=[a2aa_out[1].ap().opt()]), "cc", 1)

        def phase_outproj():
            ol = xln_sb
            with ExitStack() as ps_:
                e = ps_.enter_context
                wsb = e(nc.sbuf_tensor("wos", [128, 2 * 2048], BF16))
                ps_d = [e(nc.psum_tensor(f"od{i}", [128, TPC], F32))
                        for i in range(2)]
                ctx.w(SY, "cc")
                col = ctx.chan("col")
                for half in range(2):
                    for a, b in ((0, 4), (4, 8)):
                        ap = bass.AP(a2aa_out[half], a * 128 * TPC,
                                     [[TPC, 128], [128 * TPC, b - a], [1, TPC]])
                        ctx.inc(SY.dma_start(
                            out=ol[:, (half * 8 + a) * TPC:(half * 8 + b) * TPC]
                            .rearrange("p (c t) -> p c t", c=b - a), in_=ap),
                            col, 16)
                w_ld = [0] * 16
                w_free = [0, 0]
                add_done = [0, 0]

                def load_w(dm):
                    par = dm % 2
                    ctx.w(GP, "pe", w_free[par])
                    ap = bass.AP(wo_in.tensor, dm * 128 * 16 * 128,
                                 [[16 * 128, 128], [1, 16 * 128]])
                    ctx.inc(GP.dma_start(
                        out=wsb[:, par * 2048:(par + 1) * 2048], in_=ap),
                        ctx.chan(f"cwo{par}"), 16)

                load_w(0)
                for dm in range(16):
                    par = dm % 2
                    if dm + 1 < 16:
                        load_w(dm + 1)
                    if DEBUG_OP_STAGE < 2:
                        continue
                    ctx.wc(TE, f"cwo{par}")
                    if dm == 0:
                        ctx.wc(TE, "col")
                    ctx.w(TE, "ve", add_done[par])
                    last = None
                    for k in range(FC):
                        for h in range(2):
                            last = TE.matmul(
                                ps_d[par][:, h * 512:(h + 1) * 512],
                                wsb[:, par * 2048 + k * 128:
                                    par * 2048 + (k + 1) * 128],
                                ol[:, k * TPC + h * 512:k * TPC + (h + 1) * 512],
                                start=(k == 0), stop=(k == FC - 1))
                    mm = ctx.inc(last, "pe", 1)
                    w_free[par] = mm
                    ctx.w(VE, "pe", mm)
                    last = None
                    for h in range(2):
                        sl = x_sb[:, dm * TPC + h * 512:dm * TPC + (h + 1) * 512]
                        last = VE.tensor_add(sl, sl,
                                             ps_d[par][:, h * 512:(h + 1) * 512])
                    add_done[par] = ctx.last(last, "ve", 1)

        def phase_ffn(tag):
            with ExitStack() as ps_:
                e = ps_.enter_context
                h_sb = e(nc.sbuf_tensor(f"h{tag}", [128, 32 * TPC], BF16))
                w1g = e(nc.sbuf_tensor(f"wg{tag}", [128, 2 * 2048], BF16))
                w1u = e(nc.sbuf_tensor(f"wu{tag}", [128, 2 * 2048], BF16))
                w2sb = e(nc.sbuf_tensor(f"w2{tag}", [128, 2 * 2048], BF16))
                su = e(nc.sbuf_tensor(f"su{tag}", [128, 2 * TPC], F32))
                w1_ld = {}
                w1_free = [0, 0]
                w2_ld = {}
                w2_free = [0, 0]
                h_war = 0

                def load_w1(j):
                    par = j % 2
                    ch = ctx.chan(f"cw1{par}")
                    ctx.w(GP, "pe", w1_free[par])
                    ap = bass.AP(w1_in.tensor, j * 128 * 2048,
                                 [[2048, 128], [1, 2048]])
                    ctx.inc(GP.dma_start(
                        out=w1g[:, par * 2048:(par + 1) * 2048], in_=ap),
                        ch, 16)
                    ap = bass.AP(w1_in.tensor, (j + 64) * 128 * 2048,
                                 [[2048, 128], [1, 2048]])
                    ctx.inc(GP.dma_start(
                        out=w1u[:, par * 2048:(par + 1) * 2048], in_=ap),
                        ch, 16)

                w2c = [0, 0]
                w2_ld = {}

                def load_w2(g, ih):
                    dm, h2 = g >> 1, g & 1
                    sl = g % 2
                    ctx.w(GP, "pe", w2c[sl])
                    ap = bass.AP(w2_in.tensor,
                                 (dm * 2 + ih) * 128 * 32 * 128 + h2 * 2048,
                                 [[32 * 128, 128], [1, 2048]])
                    w2_ld[(ih, g)] = ctx.inc(GP.dma_start(
                        out=w2sb[:, sl * 2048:(sl + 1) * 2048], in_=ap),
                        ctx.chan(f"cw2{sl}"), 16)

                prev_down = 0
                for ih in range(2):
                    with nc.psum_tensor(f"fg{tag}{ih}", [128, TPC], F32) as ps_g, \
                         nc.psum_tensor(f"fu{tag}{ih}", [128, TPC], F32) as ps_u:
                        si_done = 0
                        hm_done = 0
                        hm_slot = [0, 0]
                        for jj in range(min(32, DEBUG_FFN_JJ)):
                            j = 32 * ih + jj
                            if jj == 0:
                                ctx.w(TE, "ve", prev_down)
                                load_w1(j)
                            if jj + 1 < min(32, DEBUG_FFN_JJ):
                                load_w1(j + 1)
                            if DEBUG_FFN_STAGE < 2:
                                continue
                            ctx.wc(TE, f"cw1{j % 2}")
                            ctx.w(TE, "se", si_done)
                            par = j % 2
                            last = None
                            for fc in range(FC):
                                for h in range(2):
                                    last = TE.matmul(
                                        ps_g[:, h * 512:(h + 1) * 512],
                                        w1g[:, par * 2048 + fc * 128:
                                            par * 2048 + (fc + 1) * 128],
                                        xln_sb[:, fc * TPC + h * 512:
                                               fc * TPC + (h + 1) * 512],
                                        start=(fc == 0), stop=(fc == FC - 1))
                            g_done = ctx.inc(last, "pe", 1)
                            if DEBUG_FFN_STAGE < 3:
                                continue
                            ctx.w(TE, "ve", hm_done)
                            last = None
                            for fc in range(FC):
                                for h in range(2):
                                    last = TE.matmul(
                                        ps_u[:, h * 512:(h + 1) * 512],
                                        w1u[:, par * 2048 + fc * 128:
                                            par * 2048 + (fc + 1) * 128],
                                        xln_sb[:, fc * TPC + h * 512:
                                               fc * TPC + (h + 1) * 512],
                                        start=(fc == 0), stop=(fc == FC - 1))
                            u_done = ctx.inc(last, "pe", 1)
                            w1_free[par] = u_done
                            if DEBUG_FFN_STAGE < 4:
                                continue
                            spar = jj % 2
                            ctx.w(SC, "pe", g_done)
                            ctx.w(SC, "ve", hm_slot[spar])
                            si_done = ctx.last(SC.activation(
                                su[:, spar * TPC:(spar + 1) * TPC], ps_g[:],
                                AF.Copy if DEBUG_FFN_NOSILU else AF.Silu),
                                "se", 1)
                            if DEBUG_FFN_STAGE < 5:
                                continue
                            ctx.w(VE, "se", si_done)
                            ctx.w(VE, "pe", u_done)
                            if jj == 0 and ih == 1:
                                ctx.w(VE, "pe", h_war)
                            hm_done = ctx.last(VE.tensor_mul(
                                h_sb[:, jj * TPC:(jj + 1) * TPC],
                                su[:, spar * TPC:(spar + 1) * TPC], ps_u[:]),
                                "ve", 1)
                            hm_slot[spar] = hm_done
                        h_ready = hm_done
                    if DEBUG_FFN_GU_ONLY:
                        prev_down = h_ready
                        continue
                    with nc.psum_tensor(f"fd{tag}{ih}0", [128, TPC], F32) as pd0, \
                         nc.psum_tensor(f"fd{tag}{ih}1", [128, TPC], F32) as pd1:
                        ps_dn = [pd0, pd1]
                        add_done = [0, 0]
                        load_w2(0, ih)
                        load_w2(1, ih)
                        for dm in range(16):
                            par = dm % 2
                            if dm == 0:
                                ctx.w(TE, "ve", h_ready)
                            ctx.w(TE, "ve", add_done[par])
                            for h2 in range(2):
                                g = 2 * dm + h2
                                sl = g % 2
                                ctx.wc(TE, f"cw2{sl}")
                                last = None
                                for i2h in range(16):
                                    i2 = 16 * h2 + i2h
                                    for h in range(2):
                                        last = TE.matmul(
                                            ps_dn[par][:, h * 512:(h + 1) * 512],
                                            w2sb[:, sl * 2048 + i2h * 128:
                                                 sl * 2048 + (i2h + 1) * 128],
                                            h_sb[:, i2 * TPC + h * 512:
                                                 i2 * TPC + (h + 1) * 512],
                                            start=(i2 == 0), stop=(i2 == 31))
                                w2c[sl] = ctx.inc(last, "pe", 1)
                                if g + 2 < 32:
                                    load_w2(g + 2, ih)
                            mm = w2c[(2 * dm + 1) % 2]
                            h_war = mm
                            ctx.w(VE, "pe", mm)
                            last = None
                            for h in range(2):
                                slc = x_sb[:, dm * TPC + h * 512:
                                           dm * TPC + (h + 1) * 512]
                                last = VE.tensor_add(
                                    slc, slc,
                                    ps_dn[par][:, h * 512:(h + 1) * 512])
                            add_done[par] = ctx.last(last, "ve", 1)
                        prev_down = max(add_done)

        def phase_attn_b():
            with ExitStack() as ps_:
                e = ps_.enter_context
                k_sb = e(nc.sbuf_tensor("bk", [128, 2 * T], BF16))
                v_sb = e(nc.sbuf_tensor("bv", [128, 64 * 256], BF16))
                q_sb = e(nc.sbuf_tensor("bq", [128, 4 * 512], BF16))
                pt_sb = e(nc.sbuf_tensor("bpt", [128, 2 * 512], BF16))
                mx_sb = e(nc.sbuf_tensor("bmx", [128, 4 * 17], F32))
                mfin = e(nc.sbuf_tensor("bmf", [128, 4], F32))
                mt_sb = e(nc.sbuf_tensor("bmt", [4, 128], F32))
                mrow = e(nc.sbuf_tensor("bmr", [1, 2 * 512], F32))
                mbc_sb = e(nc.sbuf_tensor("bmb", [128, 2 * 512], F32))
                rcp = e(nc.sbuf_tensor("brc", [1, 2 * 512], F32))
                f_sb = e(nc.sbuf_tensor("bf", [128, 2 * 512], F32))
                y1_sb = e(nc.sbuf_tensor("by1", [128, 2 * 512], F32))
                y2t = e(nc.sbuf_tensor("by2", [128, 512], F32))
                ystg = e(nc.sbuf_tensor("bys", [128, 2 * 512], F32))
                ps_s = e(nc.psum_tensor("bs0", [128, 1024], F32))
                ps_t = [e(nc.psum_tensor(f"bt{i}", [128, 512], F32))
                        for i in range(2)]
                ps_rs = e(nc.psum_tensor("brs", [1, 512], F32))
                ps_o = [e(nc.psum_tensor(f"bo{i}", [128, 512], F32))
                        for i in range(2)]
                ps_ms = e(nc.psum_tensor("bms", [128, 512], F32))
                ctx.wall_stores(SY)
                ckb = ctx.chan("ckb")
                for hs in range(2):
                    for a, b in ((0, T // 2), (T // 2, T)):
                        ctx.inc(SY.dma_start(
                            out=k_sb[:, hs * T + a:hs * T + b],
                            in_=bass.AP(kb_s, hs * 128 * T + a,
                                        [[T, 128], [1, b - a]])), ckb, 16)
                for a, b in _split4(64):
                    ap = bass.AP(vb_s, a * 128 * 256,
                                 [[256, 128], [128 * 256, b - a], [1, 256]])
                    ctx.inc(SY.dma_start(
                        out=v_sb[:, a * 256:b * 256].rearrange(
                            "p (u d) -> p u d", u=b - a), in_=ap), ckb, 16)
                q_free = {}
                qld = {}
                s_free0 = 0
                exp_done = [0, 0]
                pv_done = [0, 0]
                mrow_war = [0, 0]
                mrowb_war = [0, 0]
                mbc_done = [0, 0]
                mbc_war = [0, 0]
                rcp_done = [0, 0]
                fcp_done = [0, 0]
                mtcp = 0
                fms_war = 0
                mfin_war = 0
                o_free = 0
                y1_done = 0
                yst_free = [0, 0]

                def load_q(Q, hs):
                    qp = 2 * (Q % 2) + hs
                    ctx.w(SY, "pe", q_free.get((Q % 2, hs), 0))
                    ctx.inc(SY.dma_start(
                        out=q_sb[:, qp * 512:(qp + 1) * 512],
                        in_=bass.AP(qb_s, hs * 128 * T + 512 * Q,
                                    [[T, 128], [1, 512]])),
                        ctx.chan(f"cbq{qp}"), 16)

                def ph1(Q, hs):
                    nonlocal mfin_war, mtcp, s_free0, fms_war
                    qp = 2 * (Q % 2) + hs
                    ctx.wc(TE, "ckb")
                    ctx.wc(TE, f"cbq{qp}")
                    npair = (Q + 2) // 2
                    for s4 in range(4):
                        for pj in range(npair):
                            j0 = 2 * pj
                            wid = min(2, Q + 1 - j0)
                            ctx.w(TE, "ve", s_free0)
                            last = None
                            for dj in range(wid):
                                j = j0 + dj
                                last = TE.matmul(
                                    ps_s[:, dj * 512:(dj + 1) * 512],
                                    q_sb[:, qp * 512 + 128 * s4:
                                         qp * 512 + 128 * (s4 + 1)],
                                    k_sb[:, hs * T + 512 * j:hs * T + 512 * (j + 1)],
                                    start=True, stop=True)
                            mm = ctx.inc(last, "pe", 1)
                            ctx.w(VE, "pe", mm)
                            if j0 + wid - 1 == Q:
                                VE.tensor_add(
                                    ps_s[:, (wid - 1) * 512:wid * 512],
                                    ps_s[:, (wid - 1) * 512:wid * 512],
                                    wmqk_sb[:, 384 - 128 * s4:896 - 128 * s4])
                            s_free0 = ctx.last(VE.reduce_max(
                                mx_sb[:, s4 * 9 + pj:s4 * 9 + pj + 1],
                                ps_s[:, 0:wid * 512], AX.X), "ve", 1)
                    ctx.w(VE, "pe", mfin_war)
                    last = None
                    for s4 in range(4):
                        last = VE.reduce_max(mfin[:, s4:s4 + 1],
                                             mx_sb[:, s4 * 9:s4 * 9 + npair],
                                             AX.X, negate=True)
                    mxd = ctx.last(last, "ve", 1)
                    ctx.w(TE, "ve", mxd)
                    ctx.w(TE, "se", max(mtcp, fms_war))
                    mfin_war = ctx.inc(TE.transpose(ps_ms[0:4, 0:128], mfin[:],
                                                    idf_sb[:]), "pe", 1)
                    ctx.w(SC, "pe", mfin_war)
                    mtcp = ctx.last(SC.copy(mt_sb[:], ps_ms[0:4, 0:128]), "se", 1)
                    ctx.w(SY, "se", mtcp)
                    cmb = ctx.chan(f"cmb{hs}", store=True)
                    cmr = ctx.chan(f"cmr{hs}")
                    ctx.inc(SY.dma_start(
                        out=bass.AP(mb_d, (hs * NQC + Q) * 512,
                                    [[128, 4], [1, 128]]),
                        in_=mt_sb[:]), cmb, 16)
                    ctx.wc(SY, cmb)
                    ctx.w(SY, "se", mrow_war[hs])
                    ctx.inc(SY.dma_start(
                        out=mrow[:, hs * 512:(hs + 1) * 512],
                        in_=bass.AP(mb_d, (hs * NQC + Q) * 512,
                                    [[512, 1], [1, 512]])), cmr, 16)
                    ctx.wc(TE, cmr)
                    ctx.w(TE, "se", max(mtcp, fms_war))
                    ctx.w(TE, "ve", mbc_war[hs])
                    fb = ctx.inc(TE.matmul(ps_ms[:], onesf[:],
                                           mrow[:, hs * 512:(hs + 1) * 512],
                                           start=True, stop=True), "pe", 1)
                    ctx.w(SC, "pe", fb)
                    ctx.w(SC, "pe", mrowb_war[hs])
                    fms_war = ctx.last(SC.copy(mbc_sb[:, hs * 512:(hs + 1) * 512],
                                               ps_ms[:]), "se", 1)
                    mrow_war[hs] = fms_war
                    mbc_done[hs] = fms_war

                def ph2(Q, hs):
                    nonlocal o_free, y1_done, fms_war
                    qp = 2 * (Q % 2) + hs
                    ctx.w(TE, "se", mbc_done[hs])
                    ctx.w(TE, "ve", rcp_done[hs])
                    ctx.w(TE, "ve", o_free)
                    nsub = 4 * (Q + 1)
                    mm_of = [0] * nsub
                    maskv_of = [0] * nsub

                    def emit_ST(u):
                        j, s4 = u // 4, u % 4
                        sl = u % 2
                        ctx.w(TE, "se", exp_done[sl])
                        kt = k_sb[:, hs * T + 512 * j + 128 * s4:
                                  hs * T + 512 * j + 128 * (s4 + 1)]
                        mm_of[u] = ctx.inc(TE.matmul(
                            ps_t[sl][:], kt, q_sb[:, qp * 512:(qp + 1) * 512],
                            start=True, stop=True), "pe", 1)
                        ctx.w(VE, "pe", mm_of[u])
                        ctx.w(VE, "se", mbc_done[hs])
                        VE.tensor_add(ps_t[sl][:], ps_t[sl][:],
                                      mbc_sb[:, hs * 512:(hs + 1) * 512])
                        if j == Q:
                            VE.tensor_add(ps_t[sl][:], ps_t[sl][:],
                                          wmkq_sb[:, 384 - 128 * s4:896 - 128 * s4])
                        maskv_of[u] = ctx.cnt["ve"]

                    def emit_tail(u):
                        j, s4 = u // 4, u % 4
                        sl = u % 2
                        ctx.w(SC, "ve", maskv_of[u])
                        ctx.w(SC, "pe", pv_done[sl])
                        exp_done[sl] = ctx.last(SC.activation(
                            pt_sb[:, sl * 512:(sl + 1) * 512], ps_t[sl][:],
                            AF.Exp), "se", 1)
                        ctx.w(TE, "se", exp_done[sl])
                        TE.matmul(ps_rs[:], onesb[:],
                                  pt_sb[:, sl * 512:(sl + 1) * 512],
                                  start=(u == 0), stop=(u == nsub - 1))
                        ut = 4 * j + s4
                        last = None
                        for dvc in range(2):
                            vt = v_sb[:, ut * 256 + dvc * 128:
                                      ut * 256 + (dvc + 1) * 128]
                            last = TE.matmul(
                                ps_o[dvc][:], vt,
                                pt_sb[:, sl * 512:(sl + 1) * 512],
                                start=(u == 0), stop=(u == nsub - 1))
                        pv_done[sl] = ctx.inc(last, "pe", 1)

                    emit_ST(0)
                    for u in range(nsub):
                        if u + 1 < nsub:
                            emit_ST(u + 1)
                        emit_tail(u)
                    mrowb_war[hs] = pv_done[(nsub - 1) % 2]
                    mbc_war[hs] = ctx.cnt["ve"]
                    q_free[(Q % 2, hs)] = pv_done[(nsub - 1) % 2]
                    ctx.w(VE, "pe", pv_done[(nsub - 1) % 2])
                    rcp_done[hs] = ctx.last(VE.reciprocal(
                        rcp[:, hs * 512:(hs + 1) * 512], ps_rs[:]), "ve", 1)
                    ctx.w(TE, "ve", rcp_done[hs])
                    ctx.w(TE, "se", max(fcp_done[hs], mtcp))
                    fb = ctx.inc(TE.matmul(ps_ms[:], onesf[:],
                                           rcp[:, hs * 512:(hs + 1) * 512],
                                           start=True, stop=True), "pe", 1)
                    ctx.w(SC, "pe", fb)
                    fms_war = ctx.last(SC.copy(f_sb[:, hs * 512:(hs + 1) * 512],
                                              ps_ms[:]), "se", 1)
                    fcp_done[hs] = fms_war
                    ctx.w(VE, "se", fcp_done[hs])
                    if hs == 0:
                        last = None
                        for dvc in range(2):
                            last = VE.tensor_mul(
                                y1_sb[:, dvc * 512:(dvc + 1) * 512],
                                ps_o[dvc][:], f_sb[:, 0:512])
                        y1_done = ctx.last(last, "ve", 1)
                        o_free = y1_done
                    else:
                        for dvc in range(2):
                            ctx.wc(VE, ctx.chan(f"cby{dvc}", store=True))
                            VE.tensor_mul(y2t[:], ps_o[dvc][:],
                                          f_sb[:, 512:1024])
                            yc = ctx.last(VE.scalar_tensor_tensor(
                                ystg[:, dvc * 512:(dvc + 1) * 512], y2t[:],
                                lamn_sb[:],
                                y1_sb[:, dvc * 512:(dvc + 1) * 512],
                                ALU.mult, ALU.add), "ve", 1)
                            o_free = yc
                            ctx.w(SY, "ve", yc)
                            ctx.inc(SY.dma_start(
                                out=bass.AP(
                                    a2ab_in,
                                    (256 * (Q // 2) + 128 * dvc) * TPC + 512 * (Q % 2),
                                    [[TPC, 128], [1, 512]]),
                                in_=ystg[:, dvc * 512:(dvc + 1) * 512]),
                                f"cby{dvc}", 16)

                load_q(0, 0)
                load_q(0, 1)
                for Q in range(NQC):
                    if Q + 1 < NQC:
                        load_q(Q + 1, 0)
                        load_q(Q + 1, 1)
                    ph1(Q, 0)
                    ph1(Q, 1)
                    ph2(Q, 0)
                    ph2(Q, 1)
                ctx.wall_stores(GP)
                with allow_ici_all_to_all():
                    ctx.inc(GP.collective_compute(
                        "AllToAll", ALU.bypass, replica_groups=RG,
                        ins=[a2ab_in.ap().opt()], outs=[a2ab_out.ap().opt()]),
                        "cc", 1)

        def phase_yadd():
            with nc.sbuf_tensor("yt", [128, 2 * TPC], F32) as ytmp:
                ctx.w(SY, "cc")
                cons = [0, 0]
                for fc in range(FC):
                    par = fc % 2
                    cyt = ctx.chan(f"cyt{par}")
                    ctx.w(SY, "ve", cons[par])
                    ctx.inc(SY.dma_start(
                        out=ytmp[:, par * TPC:(par + 1) * TPC],
                        in_=bass.AP(a2ab_out, fc * 128 * TPC,
                                    [[TPC, 128], [1, TPC]])), cyt, 16)
                    ctx.wc(VE, cyt)
                    sl = x_sb[:, fc * TPC:(fc + 1) * TPC]
                    cons[par] = ctx.last(VE.tensor_add(
                        sl, sl, ytmp[:, par * TPC:(par + 1) * TPC]), "ve", 1)

        def phase_out():
            with nc.sbuf_tensor("os", [128, 2 * D], F32) as ostg, \
                 nc.psum_tensor("ot0", [128, 128], F32) as pt0, \
                 nc.psum_tensor("ot1", [128, 128], F32) as pt1:
                ps_tr = [pt0, pt1]
                cp_done = [0, 0]
                st_done = [0, 0]
                for ts in range(8):
                    tpar = ts % 2
                    ctx.wc(SC, ctx.chan(f"cot{tpar}", store=True))
                    for fc in range(FC):
                        rot = fc % 2
                        ctx.w(TE, "se", cp_done[rot])
                        tr = ctx.inc(TE.transpose(
                            ps_tr[rot][:],
                            x_sb[:, fc * TPC + ts * 128:fc * TPC + (ts + 1) * 128],
                            idf_sb[:]), "pe", 1)
                        ctx.w(SC, "pe", tr)
                        cp_done[rot] = ctx.last(SC.copy(
                            ostg[:, tpar * D + fc * 128:tpar * D + (fc + 1) * 128],
                            ps_tr[rot][:]), "se", 1)
                    ctx.w(SY, "se", cp_done[0])
                    ctx.w(SY, "se", cp_done[1])
                    ctx.inc(SY.dma_start(
                        out=bass.AP(out_ext.tensor, ts * 128 * D,
                                    [[D, 128], [1, D]]),
                        in_=ostg[:, tpar * D:(tpar + 1) * D]),
                        f"cot{tpar}", 16)
                ctx.wall_stores(SY)

        def dump_d(name, dram, nelem):
            if name not in dbg:
                return
            ctx.barrier()
            ctx.wall_stores(GP)
            ctx.w(GP, "cc")
            ctx.inc(GP.dma_start(
                out=bass.AP(dbg[name].tensor, 0, [[1, nelem]]),
                in_=bass.AP(dram, 0, [[1, nelem]])), "lw", 16)
            ctx.w(GP, "lw")

        def dump(name, src):
            if name not in dbg:
                return
            ctx.barrier()
            for fc in range(FC):
                ap = bass.AP(dbg[name].tensor, fc * 128 * TPC,
                             [[TPC, 128], [1, TPC]])
                ctx.inc(GP.dma_start(out=ap,
                                     in_=src[:, fc * TPC:(fc + 1) * TPC]),
                        "lw", 16)
            ctx.w(GP, "lw")

        phases = [
            ("p0", phase0, ("x",)),
            ("ln1", lambda: phase_ln("l1", to_ag=True, gates=True), ("xln",)),
            ("qkva", lambda: phase_qkv("qa", wa_in, attn_b=False), ("qkva",)),
            ("attna", phase_attn_a, ()),
            ("outproj", phase_outproj, ("x",)),
            ("ln2", lambda: phase_ln("l2"), ("xln",)),
            ("ffn1", lambda: phase_ffn("f1"), ("x",)),
            ("ln3", lambda: phase_ln("l3", to_ag=True), ("xln",)),
            ("qkvb", lambda: phase_qkv("qb", wb_in, attn_b=True), ("qkvb",)),
            ("attnb", phase_attn_b, ("attnb",)),
            ("yadd", phase_yadd, ("x",)),
            ("ln4", lambda: phase_ln("l4"), ("xln",)),
            ("ffn2", lambda: phase_ffn("f2"), ("x",)),
            ("out", phase_out, ()),
        ]

        @block.sync
        def _(sync):
            for i, (name, fn, dumps) in enumerate(phases):
                if debug_upto is not None and i > debug_upto:
                    break
                fn()
                ctx.barrier()
                for d in dumps:
                    if d == "qkva":
                        dump_d("qa", qa_s, 256 * T)
                        dump_d("ka", ka_s, 256 * T)
                        dump_d("va", va_s, 2 * T * 128)
                        dump_d("g", g_out, 16 * TPC)
                    elif d == "qkvb":
                        dump_d("qb", qb_s, 256 * T)
                        dump_d("kb", kb_s, 256 * T)
                        dump_d("vb", vb_s, T * 256)
                    elif d == "attnb":
                        dump_d("ob", a2ab_out, D * TPC)
                    else:
                        dump(d, {"x": x_sb, "xln": xln_sb}[d])
            ctx.w(SY, "st")

    return nc


def _host_tables():
    inv = 10000.0 ** (-np.arange(0, HD, 2, dtype=np.float64) / HD)
    pos = np.arange(T, dtype=np.float64)
    fr = pos[:, None] * inv[None, :]
    cos = np.cos(fr).T.astype(np.float32)
    sin = np.sin(fr).T.astype(np.float32)
    cos2 = np.ascontiguousarray(np.concatenate([cos, cos], axis=0))
    sin2 = np.ascontiguousarray(np.concatenate([sin, sin], axis=0))
    logp = np.ascontiguousarray(
        np.log(np.arange(1, T + 1, dtype=np.float64)).astype(np.float32)[None, :])
    identf = np.eye(128, dtype=np.float32)
    pim = np.zeros((128, 128), dtype=np.float32)
    for j in range(64):
        pim[64 + j, j] = 1.0
        pim[j, j + 64] = -1.0
    ii = np.arange(128)[:, None]
    ww = np.arange(896)[None, :]
    wmqk = np.where(ww > ii + 384, NEG_BIG, 0.0).astype(np.float32)
    wmkq = np.where(ww < ii + 384, NEG_BIG, 0.0).astype(np.float32)
    return cos2, sin2, logp, identf, pim, wmqk, wmkq


def kernel(L0_attn_a_x, L0_attn_a_w_qkv, L0_attn_a_w_gates, L0_attn_a_w_out,
           L0_attn_b_Wq, L0_attn_b_Wk, L0_attn_b_Wv,
           L0_attn_b_lambda_q1, L0_attn_b_lambda_k1,
           L0_attn_b_lambda_q2, L0_attn_b_lambda_k2,
           L0_attn_b_softmax_scaler, L0_ffn_w_gate_up, L0_ffn_w_down):
    x = np.asarray(L0_attn_a_x, dtype=np.float32).reshape(T, D)
    xT = np.ascontiguousarray(x.T)
    wqkv = np.asarray(L0_attn_a_w_qkv, dtype=np.float32).reshape(D, 3, NH, HD)
    perm0 = [h for p_ in range(8) for h in (p_, p_ + 8)]
    w_gates = np.ascontiguousarray(
        np.asarray(L0_attn_a_w_gates, dtype=np.float32)[:, perm0])
    w_out = np.asarray(L0_attn_a_w_out, dtype=np.float32)
    wo_t = np.ascontiguousarray(
        w_out.reshape(16, 128, 16, 128).transpose(2, 1, 0, 3))
    wgu = np.asarray(L0_ffn_w_gate_up, dtype=np.float32)
    w1_t = np.ascontiguousarray(
        wgu.reshape(128, 128, 16, 128).transpose(0, 3, 2, 1))
    wdn = np.asarray(L0_ffn_w_down, dtype=np.float32)
    w2_t = np.ascontiguousarray(
        wdn.reshape(16, 128, 2, 32, 128).transpose(0, 2, 4, 3, 1))
    Wq = np.asarray(L0_attn_b_Wq, dtype=np.float32).reshape(D, NH, HD)
    Wk = np.asarray(L0_attn_b_Wk, dtype=np.float32).reshape(D, NH, HD)
    Wv = np.asarray(L0_attn_b_Wv, dtype=np.float32).reshape(D, 8, 256)
    scaler = np.asarray(L0_attn_b_softmax_scaler, dtype=np.float32).reshape(NH)
    lam = np.ascontiguousarray(np.concatenate([
        np.asarray(L0_attn_b_lambda_q1, dtype=np.float32),
        np.asarray(L0_attn_b_lambda_k1, dtype=np.float32),
        np.asarray(L0_attn_b_lambda_q2, dtype=np.float32),
        np.asarray(L0_attn_b_lambda_k2, dtype=np.float32)])[None, :])
    cos2, sin2, logp, identf, pim, wmqk, wmkq = _host_tables()

    in_maps = []
    for c in range(NCORE):
        wa = np.ascontiguousarray(np.concatenate(
            [wqkv[:, 0, c], wqkv[:, 0, c + 8],
             wqkv[:, 1, c], wqkv[:, 1, c + 8],
             wqkv[:, 2, c], wqkv[:, 2, c + 8]], axis=1))
        wb = np.ascontiguousarray(np.concatenate(
            [Wq[:, c], Wq[:, c + 8], Wk[:, c], Wk[:, c + 8], Wv[:, c]],
            axis=1))
        in_maps.append({
            "xT": np.ascontiguousarray(xT[:, c * TPC:(c + 1) * TPC]),
            "wa": wa, "wg": w_gates, "wo": wo_t, "wb": wb,
            "w1": w1_t, "w2": w2_t,
            "cos2": cos2, "sin2": sin2, "logpos": logp,
            "scal": np.ascontiguousarray(scaler[[c, c + 8]][None, :]),
            "lam": lam, "identf": identf, "pimat": pim,
            "wmqk": wmqk, "wmkq": wmkq,
        })

    nc = build(debug_upto=DEBUG_UPTO, debug_dumps=DEBUG_DUMPS)
    res = run_bass_kernel_spmd(nc, in_maps, core_ids=list(range(NCORE)))
    if DEBUG_DUMPS:
        kernel.debug_results = res
    out = np.concatenate([res.results[c]["out"] for c in range(NCORE)], axis=0)
    return out.reshape(1, T, D)
